# version 07
# baseline (speedup 1.0000x reference)
import numpy as np
from contextlib import ExitStack

import concourse.bass as bass
import concourse.mybir as mybir
from concourse.bass_utils import run_bass_kernel_spmd
from concourse.alu_op_type import AluOpType

F32 = mybir.dt.float32
BF16 = mybir.dt.bfloat16
I32 = mybir.dt.int32
AF = mybir.ActivationFunctionType
AX = mybir.AxisListType

B, S, D = 32, 2048, 1024
N_CORES = 8
BPC = B // N_CORES
NT = S // 128
DK = D // 128
RSQRT2 = float(2 ** (-0.5))
LN_EPS = 1e-5
QK_SCALE = 0.125
DEQ = 1.0 / 127.0
MODE = "fused"

ALL_SLOTS = ("l0a", "l0f", "l1a", "l1f")


def _lhs_stream(w_t, n_out_tiles):
    dk = w_t.shape[0] // 128
    a = w_t.reshape(dk, 128, n_out_tiles, 128)
    return np.ascontiguousarray(a.transpose(2, 1, 0, 3))


def _up_stream(w_t, n_hc, w_hc):
    dk = w_t.shape[0] // 128
    a = w_t.reshape(dk, 128, n_hc, w_hc)
    return np.ascontiguousarray(a.transpose(2, 1, 0, 3))


def _down_stream(w_t, n_dc, w_dc, g):
    kt = w_t.shape[0] // 128
    a = w_t.reshape(kt // g, g, 128, n_dc, w_dc)
    return np.ascontiguousarray(a.transpose(3, 0, 2, 1, 4))


def host_prep(inp):
    p = {}
    W_qk = inp["L0_attn_W_qk"]
    wq = np.ascontiguousarray(W_qk.reshape(D, 8, 64, 2)[..., 0].reshape(D, 512))
    wk = np.ascontiguousarray(W_qk.reshape(D, 8, 64, 2)[..., 1].reshape(D, 512))
    b_qk = inp["L0_attn_b_qk"]
    p["l0_bq"] = np.ascontiguousarray(b_qk.reshape(8, 64, 2)[..., 0].reshape(4, 128).T)
    p["l0_bk"] = np.ascontiguousarray(b_qk.reshape(8, 64, 2)[..., 1].reshape(4, 128).T)
    p["l0_wq"] = _lhs_stream(wq, 4)
    p["l0_wk"] = _lhs_stream(wk, 4)
    wv_ext = np.zeros((D, 8, 65), np.float32)
    wv_ext[:, :, :64] = inp["L0_attn_W_v"].reshape(D, 8, 64)
    p["l0_wv"] = np.ascontiguousarray(wv_ext.reshape(D, 520))
    bv_ext = np.zeros((8, 65), np.float32)
    bv_ext[:, :64] = inp["L0_attn_b_v"].reshape(8, 64)
    bv_ext[:, 64] = 1.0
    p["l0_bv"] = np.ascontiguousarray(bv_ext.reshape(1, 520))
    p["l0_wout"] = np.ascontiguousarray(inp["L0_attn_W_out"])
    p["l0_bout"] = np.ascontiguousarray(inp["L0_attn_b_out"].reshape(1, D))
    p["l0_gamma"] = np.ascontiguousarray(inp["L0_attn_ln_gamma"].reshape(1, D))
    p["l0_beta"] = np.ascontiguousarray(inp["L0_attn_ln_beta"].reshape(1, D))
    wg_t = np.ascontiguousarray(inp["L0_ffn_w_gate"].T)
    wu_t = np.ascontiguousarray(inp["L0_ffn_w_up"].T)
    wd_t = np.ascontiguousarray(inp["L0_ffn_w_down"].T)
    p["l0_wg"] = _up_stream(wg_t, 6, 512)
    p["l0_wu"] = _up_stream(wu_t, 6, 512)
    p["l0_wd"] = _down_stream(wd_t, 2, 512, 4)
    wq1 = np.ascontiguousarray(inp["L1_attn_W_Q"].transpose(2, 0, 1).reshape(D, 1024))
    wk1 = np.ascontiguousarray(inp["L1_attn_W_K"].transpose(2, 0, 1).reshape(D, 1024))
    p["l1_wq"] = _lhs_stream(wq1, 8)
    p["l1_wk"] = _lhs_stream(wk1, 8)
    wv1_ext = np.zeros((D, 16, 65), np.float32)
    wv1_ext[:, :, :64] = inp["L1_attn_W_V"].transpose(2, 0, 1).reshape(D, 16, 64)
    p["l1_wv"] = np.ascontiguousarray(wv1_ext.reshape(D, 1040))
    bv1 = np.zeros((16, 65), np.float32)
    bv1[:, 64] = 1.0
    p["l1_bv"] = np.ascontiguousarray(bv1.reshape(1, 1040))
    p["l1_wo"] = np.ascontiguousarray(inp["L1_attn_W_O"].T)
    W1 = np.concatenate([inp["L1_ffn_W1"], inp["L1_ffn_Ws1"][None]], 0)
    W2 = np.concatenate([inp["L1_ffn_W2"], inp["L1_ffn_Ws2"][None]], 0)
    b1 = np.concatenate([inp["L1_ffn_b1"], inp["L1_ffn_bs1"][None]], 0)
    b2 = np.concatenate([inp["L1_ffn_b2"], inp["L1_ffn_bs2"][None]], 0)
    w1t = W1.transpose(0, 2, 1)
    p["l1_w1"] = np.ascontiguousarray(
        w1t.reshape(9, DK, 128, 8, 512).transpose(0, 3, 2, 1, 4))
    w2t = W2.transpose(0, 2, 1)
    p["l1_w2"] = np.ascontiguousarray(
        w2t.reshape(9, 8, 4, 128, 2, 512).transpose(0, 4, 1, 3, 2, 5))
    p["l1_b1"] = np.ascontiguousarray(b1.reshape(9, 32, 128).transpose(0, 2, 1))
    p["l1_b2"] = np.ascontiguousarray(b2.reshape(1, 9 * D))
    p["l1_wgr"] = np.ascontiguousarray(inp["L1_ffn_Wg"].T)
    p["l1_bgb"] = np.ascontiguousarray(
        np.stack([inp["L1_ffn_bg"], inp["L1_ffn_bias"]], 0))
    return p


PREP_SPECS = {
    "l0_bq": ((128, 4), F32), "l0_bk": ((128, 4), F32),
    "l0_wq": ((4, 128, 8, 128), F32), "l0_wk": ((4, 128, 8, 128), F32),
    "l0_wv": ((D, 520), F32), "l0_bv": ((1, 520), F32),
    "l0_wout": ((512, D), F32), "l0_bout": ((1, D), F32),
    "l0_gamma": ((1, D), F32), "l0_beta": ((1, D), F32),
    "l0_wg": ((6, 128, 8, 512), I32), "l0_wu": ((6, 128, 8, 512), I32),
    "l0_wd": ((2, 6, 128, 4, 512), I32),
    "l1_wq": ((8, 128, 8, 128), F32), "l1_wk": ((8, 128, 8, 128), F32),
    "l1_wv": ((D, 1040), F32), "l1_bv": ((1, 1040), F32),
    "l1_wo": ((D, D), F32),
    "l1_w1": ((9, 8, 128, 8, 512), F32), "l1_w2": ((9, 2, 8, 128, 4, 512), F32),
    "l1_b1": ((9, 128, 32), F32), "l1_b2": ((1, 9 * D), F32),
    "l1_wgr": ((D, 8), F32), "l1_bgb": ((2, 8), F32),
}

SLOT_WEIGHTS = {
    "l0a": ["l0_bq", "l0_bk", "l0_wq", "l0_wk", "l0_wv", "l0_bv", "l0_wout",
            "l0_bout", "l0_gamma", "l0_beta"],
    "l0f": ["l0_wg", "l0_wu", "l0_wd"],
    "l1a": ["l1_wq", "l1_wk", "l1_wv", "l1_bv", "l1_wo"],
    "l1f": ["l1_w1", "l1_w2", "l1_b1", "l1_b2", "l1_wgr", "l1_bgb"],
}


class P:
    def __init__(self, nc, block, st):
        self.nc, self.block, self.st = nc, block, st
        self.c = {}
        self.d = {}
        self.cnt = {}
        self.hs = {}
        for n in ("wS", "ioS", "peS", "dvS", "acS", "gpS"):
            self.hs[n] = st.enter_context(nc.semaphore(n))
            self.cnt[n] = 0

    def inc(self, inst, name, n=1):
        inst.then_inc(self.hs[name], n)
        self.cnt[name] += n
        return self.cnt[name]

    def wait(self, eng, name, val=None):
        v = self.cnt[name] if val is None else val
        if v > 0:
            eng.wait_ge(self.hs[name], v)


def lnf_strip(p, dk, lo, hi):
    r = p.c["lnf"][:].rearrange("p (dk q) -> p dk q", dk=DK)
    return r[:, dk, lo:hi]


def alloc_buffers(p, st):
    nc = p.nc
    def T(n, sh, dt):
        p.c[n] = st.enter_context(nc.sbuf_tensor(n, sh, dt))
    def PS(n):
        p.c[n] = st.enter_context(nc.psum_tensor(n, [128, 2048], F32))
    T("xb", [128, NT * 1024], F32)
    T("lnf", [128, DK * 2048], BF16)
    T("blk", [128, 32896], BF16)
    T("wbA0", [128, 4160], BF16)
    T("wbA1", [128, 4160], BF16)
    T("wbB0", [128, 2048], BF16)
    T("wbB1", [128, 2048], BF16)
    T("ptb", [128, 2048], BF16)
    T("junk", [128, 1024], BF16)
    T("sum_c", [128, NT], F32); T("sq_c", [128, NT], F32)
    T("mu", [128, NT], F32); T("msq", [128, NT], F32); T("mu2", [128, NT], F32)
    T("var", [128, NT], F32); T("lnv", [128, NT], F32); T("rs", [128, NT], F32)
    T("nmr", [128, NT], F32); T("A_c", [128, NT], F32); T("B_c", [128, NT], F32)
    T("ycol", [128, 8], F32)
    T("eps_col", [128, 1], F32)
    T("M_all", [128, 128], F32)
    T("mtmp", [128, 2], F32)
    T("mts", [128, 128], BF16)
    T("negmax", [1, 2048], BF16)
    T("rb", [64, 512], F32)
    T("w_sb", [128, NT * 8], F32)
    T("rtmp", [128, 8], F32)
    T("rtm2", [128, 8], F32)
    T("rtm3", [128, 8], F32)
    T("rcol", [128, 4], F32)
    T("ident", [128, 128], F32)
    T("maskT", [128, 128], BF16)
    T("ones_f", [1, 128], F32)
    T("ones_bf", [1, 128], BF16)
    T("negones_bf", [1, 128], BF16)
    T("rowb", [1, D], BF16)
    T("rrow", [1, 512], BF16)
    T("bgbb", [1, 8], BF16)
    T("bq_fm", [128, 8], F32)
    T("bk_fm", [128, 8], F32)
    T("b1_sb", [128, 9 * 32], F32)
    T("bgb", [1, 8], F32)
    T("wgr_sb", [128, 64], BF16)
    PS("pAB"); PS("pCD")
    p.c["pA"] = p.c["pAB"][:][:, 0:1024]
    p.c["pB"] = p.c["pAB"][:][:, 1024:2048]
    p.c["pC"] = p.c["pCD"][:][:, 0:1024]
    p.c["pD"] = p.c["pCD"][:][:, 1024:2048]
    p.c["pt0"] = p.c["ptb"][:, 0:1024]
    p.c["pt1"] = p.c["ptb"][:, 1024:2048]
    p.c["lntm"] = p.c["ptb"][:].bitcast(F32)
    blk = p.c["blk"]
    p.c["qf"] = blk[:, 0:8192]
    p.c["kf"] = blk[:, 8192:16384]
    p.c["of"] = blk[:, 16384:20480]
    p.c["vtm"] = blk[:, 20480:28800]
    gb = blk[:, 28800:32896]
    p.c["hfm"] = blk[:, 0:32768]
    p.c["gamma_bc"] = gb.bitcast(F32)[:, 0:1024]
    p.c["beta_bc"] = gb.bitcast(F32)[:, 1024:2048]


PSUMS = ("pA", "pB", "pC", "pD")


PRELOAD_FUNCS = {6: AF.Exp, 18: AF.Silu, 2: AF.Sigmoid, 10: AF.Gelu}


def preload_set(p, idx):
    @p.block.scalar
    def _(s):
        s.preload_act_table(PRELOAD_FUNCS[idx])



def emit_consts(p, slots):
    nc, block, c = p.nc, p.block, p.c
    with ExitStack() as st2:
        ic = st2.enter_context(nc.sbuf_tensor("iotac", [128, 128], I32))
        ip_ = st2.enter_context(nc.sbuf_tensor("iotap", [128, 1], I32))
        icf = st2.enter_context(nc.sbuf_tensor("iotacf", [128, 128], F32))
        ipf = st2.enter_context(nc.sbuf_tensor("iotapf", [128, 1], F32))

        @block.gpsimd
        def _(g):
            g.iota(ic[:], pattern=[[1, 128]], base=0, channel_multiplier=0)
            g.iota(ip_[:], pattern=[[0, 1]], base=0, channel_multiplier=1)
            inst = g.tensor_copy(icf[:], ic[:])
            p.inc(inst, "gpS")
            inst = g.tensor_copy(ipf[:], ip_[:])
            p.inc(inst, "gpS")
            p.wait(g, "gpS")
            g.tensor_scalar(c["ident"][:], icf[:], ipf[:, 0:1], None, op0=AluOpType.is_equal)
            inst = g.tensor_scalar(c["maskT"][:], icf[:], ipf[:, 0:1], None, op0=AluOpType.is_ge)
            p.inc(inst, "gpS")
    gp_c = p.cnt["gpS"]

    @block.vector
    def _(v):
        v.memset(c["ones_f"][:], 1.0)
        v.memset(c["ones_bf"][:], 1.0)
        v.memset(c["negones_bf"][:], -1.0)
        inst = v.memset(c["eps_col"][:], LN_EPS)
        p.inc(inst, "dvS")

    @block.sync
    def _(sy):
        if "l1f" in slots:
            inst = sy.dma_start(c["b1_sb"][:].rearrange("p (e c) -> p e c", c=32),
                                p.d["l1_b1"].rearrange("e p c -> p e c"))
            p.inc(inst, "ioS", 16)
            inst = sy.dma_start(c["rtmp"][0:1, :], p.d["l1_bgb"][0:1, :])
            p.inc(inst, "ioS", 16)
            inst = sy.dma_start(c["rtm2"][0:1, :], p.d["l1_bgb"][1:2, :])
            p.inc(inst, "ioS", 16)
        if "l0a" in slots:
            inst = sy.dma_start(c["bq_fm"][:, 0:4], p.d["l0_bq"])
            p.inc(inst, "ioS", 16)
            inst = sy.dma_start(c["bk_fm"][:, 0:4], p.d["l0_bk"])
            p.inc(inst, "ioS", 16)
    io0 = p.cnt["ioS"]

    @block.vector
    def _(v):
        p.wait(v, "ioS", io0)
        p.wait(v, "gpS", gp_c)
        if "l0a" in slots:
            inst = v.tensor_scalar(c["bq_fm"][:, 0:4], c["bq_fm"][:, 0:4], QK_SCALE, None,
                                   op0=AluOpType.mult)
            p.inc(inst, "dvS")
        if "l1f" in slots:
            inst = v.tensor_tensor(c["bgb"][:], c["rtmp"][0:1, :], c["rtm2"][0:1, :],
                                   op=AluOpType.add)
            p.inc(inst, "dvS")
            p.wait(v, "dvS")
            inst = v.tensor_copy(c["bgbb"][:], c["bgb"][:])
            p.inc(inst, "dvS")


def emit_xload(p, b):
    @p.block.sync
    def _(sy):
        p.wait(sy, "peS"); p.wait(sy, "dvS"); p.wait(sy, "acS"); p.wait(sy, "ioS")
        src = p.d["x"][b * S:(b + 1) * S, :].rearrange("(t p) c -> p t c", p=128)
        dst = p.c["xb"][:].rearrange("p (t c) -> p t c", c=1024)
        inst = sy.dma_start(dst, src)
        p.inc(inst, "ioS", 16)


def emit_xstore(p, b):
    @p.block.sync
    def _(sy):
        p.wait(sy, "dvS"); p.wait(sy, "acS"); p.wait(sy, "peS")
        dst = p.d["out"][b * S:(b + 1) * S, :].rearrange("(t p) c -> p t c", p=128)
        srcb = p.c["xb"][:].rearrange("p (t c) -> p t c", c=1024)
        inst = sy.dma_start(dst, srcb)
        p.inc(inst, "ioS", 16)


def emit_ln(p, save_ab=False):
    block, c = p.block, p.c
    io_x = p.cnt["ioS"]

    @block.scalar
    def _(s):
        p.wait(s, "ioS", io_x)
        p.wait(s, "dvS"); p.wait(s, "peS")
        for tt in range(NT):
            inst = s.activation(c["junk"][:], c["xb"][:, tt * 1024:(tt + 1) * 1024],
                                AF.Square, accum_out=c["sq_c"][:, tt:tt + 1])
            p.inc(inst, "acS")
    ac_sq = p.cnt["acS"]

    @block.vector
    def _(v):
        p.wait(v, "ioS", io_x)
        p.wait(v, "peS")
        for tt in range(NT):
            inst = v.tensor_scalar(c["junk"][:], c["xb"][:, tt * 1024:(tt + 1) * 1024],
                                   1.0, None, op0=AluOpType.mult, op1=AluOpType.add,
                                   accum_out=c["sum_c"][:, tt:tt + 1])
            p.inc(inst, "dvS")
        p.wait(v, "dvS")
        inst = v.tensor_scalar(c["mu"][:], c["sum_c"][:], 1.0 / D, None, op0=AluOpType.mult)
        p.inc(inst, "dvS")
        p.wait(v, "acS", ac_sq)
        inst = v.tensor_scalar(c["msq"][:], c["sq_c"][:], 1.0 / D, None, op0=AluOpType.mult)
        p.inc(inst, "dvS")
        p.wait(v, "dvS")
        inst = v.tensor_tensor(c["mu2"][:], c["mu"][:], c["mu"][:], op=AluOpType.mult)
        p.inc(inst, "dvS")
        p.wait(v, "dvS")
        inst = v.tensor_tensor(c["var"][:], c["msq"][:], c["mu2"][:], op=AluOpType.subtract)
        p.inc(inst, "dvS")
    dv_var = p.cnt["dvS"]

    @block.scalar
    def _(s):
        p.wait(s, "dvS", dv_var)
        inst = s.activation(c["lnv"][:], c["var"][:], AF.Ln, bias=c["eps_col"][:, 0:1])
        p.inc(inst, "acS")
        p.wait(s, "acS")
        inst = s.activation(c["rs"][:], c["lnv"][:], AF.Exp, scale=-0.5)
        p.inc(inst, "acS")
    ac_rs = p.cnt["acS"]

    @block.vector
    def _(v):
        p.wait(v, "acS", ac_rs)
        inst = v.scalar_tensor_tensor(c["nmr"][:], c["mu"][:], -1.0, c["rs"][:],
                                      op0=AluOpType.mult, op1=AluOpType.mult)
        p.inc(inst, "dvS")
        if save_ab:
            inst = v.tensor_scalar(c["A_c"][:], c["rs"][:], RSQRT2, 1.0,
                                   op0=AluOpType.mult, op1=AluOpType.add)
            p.inc(inst, "dvS")
            inst = v.scalar_tensor_tensor(c["B_c"][:], c["mu"][:], -RSQRT2, c["rs"][:],
                                          op0=AluOpType.mult, op1=AluOpType.mult)
            p.inc(inst, "dvS")
    dv_nmr = p.cnt["dvS"]

    ac_apply, io_tr = {}, {}
    for tt in range(NT):
        lb = tt % 2

        @block.scalar
        def _(s, tt=tt, lb=lb):
            if tt == 0:
                p.wait(s, "dvS", dv_nmr)
            if tt >= 2:
                p.wait(s, "ioS", io_tr[tt - 2])
            inst = s.activation(c["ptb"][:, lb * 1024:(lb + 1) * 1024],
                                c["xb"][:, tt * 1024:(tt + 1) * 1024],
                                AF.Identity, scale=c["rs"][:, tt:tt + 1],
                                bias=c["nmr"][:, tt:tt + 1])
            ac_apply[tt] = p.inc(inst, "acS")

        @block.sync
        def _(sy, tt=tt, lb=lb):
            p.wait(sy, "acS", ac_apply[tt])
            inst = None
            for dk in range(DK):
                inst = sy.dma_start(
                    c["lnf"][:, dk * 2048 + tt * 128: dk * 2048 + (tt + 1) * 128],
                    c["ptb"][:, lb * 1024 + dk * 128: lb * 1024 + (dk + 1) * 128],
                    transpose=True)
                p.inc(inst, "ioS", 16)
            io_tr[tt] = p.cnt["ioS"]
    p.lnf_ready = p.cnt["ioS"]


def emit_proj_fm(p, dram_name, ot_list, out_sb, bias_fm=None, bias_cols=None, scale=1.0,
                 interleave=None):
    block, c = p.block, p.c
    wmark, pe_done, ac_done = {}, {}, {}
    for i, ot in enumerate(ot_list):
        @block.gpsimd
        def _(g, i=i, ot=ot):
            if i < 2:
                p.wait(g, "peS")
            else:
                p.wait(g, "peS", pe_done[i - 2])
            src = p.d[dram_name][ot].rearrange("p dk c -> p (dk c)")
            inst = g.dma_start(c["wbB0" if i % 2 == 0 else "wbB1"][:, 0:DK * 128], src)
            wmark[i] = p.inc(inst, "wS", 16)

        @block.tensor
        def _(t, i=i):
            p.wait(t, "wS", wmark[i])
            if i == 0:
                p.wait(t, "acS"); p.wait(t, "dvS")
                p.wait(t, "ioS", p.lnf_ready)
            if i >= 1:
                p.wait(t, "acS", ac_done[i - 1])
            wb = c["wbB0" if i % 2 == 0 else "wbB1"]
            inst = None
            for dk in range(DK):
                for tc in range(4):
                    inst = t.matmul(c["pAB"][:][:, tc * 512:(tc + 1) * 512],
                                    wb[:, dk * 128:(dk + 1) * 128],
                                    lnf_strip(p, dk, tc * 512, (tc + 1) * 512),
                                    start=(dk == 0), stop=(dk == DK - 1))
            pe_done[i] = p.inc(inst, "peS")

        @block.scalar
        def _(s, i=i):
            p.wait(s, "peS", pe_done[i])
            kw = {}
            fn = AF.Copy
            if bias_fm is not None:
                kw["bias"] = bias_fm[:, bias_cols[i]:bias_cols[i] + 1]
                fn = AF.Identity
            inst = s.activation(out_sb[:, i * 2048:(i + 1) * 2048], c["pAB"][:], fn,
                                scale=scale, **kw)
            ac_done[i] = p.inc(inst, "acS")
        if interleave is not None:
            interleave()
    return ac_done


def emit_vtm(p, wv_name, bv_name, voff, interleave=None):
    block, c = p.block, p.c
    vc = 520

    @block.gpsimd
    def _(g):
        p.wait(g, "peS")
        src = p.d[wv_name].rearrange("(dk p) c -> p dk c", p=128)[:, :, voff:voff + vc]
        dst = c["wbA0"][:, 0:DK * vc].rearrange("p (dk c) -> p dk c", dk=DK)
        inst = g.dma_start(dst, src)
        p.inc(inst, "wS", 16)
    wm = p.cnt["wS"]

    @block.gpsimd
    def _(g):
        inst = g.dma_start(c["rowb"][:, 0:vc], p.d[bv_name][:, voff:voff + vc])
        p.inc(inst, "wS", 16)
    iom = p.cnt["wS"]

    pe_done, ac_done = {}, {}
    for tt in range(NT):
        @block.tensor
        def _(t, tt=tt):
            if tt == 0:
                p.wait(t, "wS", max(wm, iom))
                p.wait(t, "acS")
                p.wait(t, "ioS", p.lnf_ready)
            if tt >= 2:
                p.wait(t, "acS", ac_done[tt - 2])
            ps = c["pA"] if tt % 2 == 0 else c["pB"]
            t.matmul(ps[:, 0:512], c["ones_bf"][:, 0:128], c["rowb"][0:1, 0:512],
                     start=True, stop=True)
            t.matmul(ps[:, 512:vc], c["ones_bf"][:, 0:128], c["rowb"][0:1, 512:vc],
                     start=True, stop=True)
            inst = None
            for dk in range(DK):
                t.matmul(ps[:, 0:512], lnf_strip(p, dk, tt * 128, (tt + 1) * 128),
                         c["wbA0"][:, dk * vc:dk * vc + 512],
                         start=False, stop=(dk == DK - 1))
                inst = t.matmul(ps[:, 512:vc], lnf_strip(p, dk, tt * 128, (tt + 1) * 128),
                                c["wbA0"][:, dk * vc + 512:(dk + 1) * vc],
                                start=False, stop=(dk == DK - 1))
            pe_done[tt] = p.inc(inst, "peS")

        @block.scalar
        def _(s, tt=tt):
            p.wait(s, "peS", pe_done[tt])
            ps = c["pA"] if tt % 2 == 0 else c["pB"]
            inst = s.activation(c["vtm"][:, tt * 520:tt * 520 + vc], ps[:, 0:vc], AF.Copy)
            ac_done[tt] = p.inc(inst, "acS")
        if interleave is not None:
            interleave()


class PassA:

    def __init__(self, p, causal):
        self.p = p
        self.causal = causal
        self.units = [(h, qt) for h in range(8) for qt in range(NT)]
        self.idx = 0
        self.kmark = {}
        self.dv_done = {}
        self.pe_done = {}

    def set_ready(self, hp, ac_mark):
        self.kmark[hp] = ac_mark

    def avail(self):
        n = 0
        for (h, qt) in self.units[self.idx:]:
            if h // 2 in self.kmark:
                n += 1
            else:
                break
        return n

    def pump(self, n):
        p, block, c = self.p, self.p.block, self.p.c
        n = min(n, self.avail())
        for _ in range(n):
            h, qt = self.units[self.idx]
            k = self.idx
            self.idx += 1
            hp, hb = h // 2, (h % 2) * 64
            lim = 2048 if not self.causal else (qt + 1) * 128
            causal = self.causal

            @block.tensor
            def _(t, h=h, qt=qt, hp=hp, hb=hb, lim=lim, k=k):
                if qt == 0:
                    p.wait(t, "acS", self.kmark[hp])
                if k >= 1:
                    p.wait(t, "dvS", self.dv_done[k - 1])
                lhs = c["qf"][hb:hb + 64, hp * 2048 + qt * 128: hp * 2048 + (qt + 1) * 128]
                inst = None
                for sc in range(0, lim, 512):
                    w = min(512, lim - sc)
                    inst = t.matmul(c["pCD"][:][:, sc:sc + w], lhs,
                                    c["kf"][hb:hb + 64, hp * 2048 + sc: hp * 2048 + sc + w],
                                    start=True, stop=True)
                self.pe_done[k] = p.inc(inst, "peS")

            @block.vector
            def _(v, lim=lim, k=k, h=h, qt=qt):
                p.wait(v, "peS", self.pe_done[k])
                inst = v.tensor_reduce(c["M_all"][:, h * NT + qt:h * NT + qt + 1],
                                       c["pCD"][:][:, 0:lim], axis=AX.X, op=AluOpType.max)
                self.dv_done[k] = p.inc(inst, "dvS")

    def finish(self):
        p, block, c = self.p, self.p.block, self.p.c
        self.pump(len(self.units))
        assert self.idx == len(self.units)
        dvm = p.cnt["dvS"]

        @block.tensor
        def _(t):
            p.wait(t, "dvS", dvm)
            inst = t.transpose(c["pC"][:, 0:128], c["M_all"][:], c["ident"][:])
            p.inc(inst, "peS")
        pet = p.cnt["peS"]

        @block.scalar
        def _(s):
            p.wait(s, "peS", pet)
            inst = s.activation(c["mts"][:], c["pC"][:, 0:128], AF.Copy)
            p.inc(inst, "acS")
        act = p.cnt["acS"]

        @block.sync
        def _(sy):
            p.wait(sy, "acS", act)
            inst = sy.dma_start(p.d["mbounce"][:], c["mts"][:])
            p.inc(inst, "ioS", 16)


def emit_attn_phaseC(p, qh, causal):
    block, c = p.block, p.c
    iom = p.cnt["ioS"]

    def make_norm(pe_last, hp, hb):
        def norm():
            for half in range(2):
                @block.vector
                def _(v, half=half):
                    if half == 0:
                        p.wait(v, "peS", pe_last)
                    p.wait(v, "acS")
                    p.wait(v, "dvS")
                    with p.nc.allow_low_precision(reason="bf16 rinv row for rank-1 bcast"):
                        inst = v.reciprocal(c["rrow"][:],
                                            c["pD"][64:65, half * 512:(half + 1) * 512])
                    p.inc(inst, "dvS")
                dv_r = p.cnt["dvS"]

                @block.tensor
                def _(t, dv_r=dv_r, half=half):
                    p.wait(t, "dvS", dv_r)
                    p.wait(t, "acS")
                    inst = t.matmul(c["pB"][0:64, half * 512:(half + 1) * 512],
                                    c["ones_bf"][:, 0:64], c["rrow"][:],
                                    start=True, stop=True)
                    p.inc(inst, "peS")
                pe_rb = p.cnt["peS"]

                @block.scalar
                def _(s, pe_rb=pe_rb, half=half):
                    p.wait(s, "peS", pe_rb)
                    inst = s.activation(c["rb"][:], c["pB"][0:64, half * 512:(half + 1) * 512],
                                        AF.Copy)
                    p.inc(inst, "acS")
                ac_rb = p.cnt["acS"]

                @block.vector
                def _(v, ac_rb=ac_rb, half=half):
                    p.wait(v, "acS", ac_rb)
                    inst = v.tensor_tensor(
                        c["of"][hb:hb + 64, hp * 1024 + half * 512: hp * 1024 + (half + 1) * 512],
                        c["pD"][0:64, half * 512:(half + 1) * 512], c["rb"][:],
                        op=AluOpType.mult)
                    p.inc(inst, "dvS")
        return norm

    pending_norm = None
    for h in range(8):
        hp, hb = h // 2, (h % 2) * 64
        pt_hi = NT if not causal else (qh + 1) * 8
        ac_head0 = p.cnt["acS"]
        ac_exp = {}
        pe_pv = {}
        pe_s2 = {}

        @block.sync
        def _(sy, h=h):
            p.wait(sy, "ioS", iom)
            if h > 0 or qh > 0:
                p.wait(sy, "peS")
            src = p.d["mbounce"][h * 16:(h + 1) * 16, :].rearrange("a b -> (a b)").unsqueeze(0)
            inst = sy.dma_start(c["negmax"][:], src)
            p.inc(inst, "ioS", 16)
        io_h = p.cnt["ioS"]

        for ptp in range(0, pt_hi, 2):
            pts = [q for q in (ptp, ptp + 1) if q < pt_hi]
            for pt in pts:
                sps = c["pC"] if pt % 2 == 0 else c["pA"]
                qc_min = 0 if not causal else max(0, (pt * 128 - qh * 1024 - 512) // 512 + 1)

                @block.tensor
                def _(t, h=h, pt=pt, hp=hp, hb=hb, qc_min=qc_min, sps=sps):
                    if pt == 0:
                        p.wait(t, "ioS", io_h)
                        p.wait(t, "acS", ac_head0)
                    if pt >= 2:
                        p.wait(t, "acS", ac_exp[pt - 2])
                    inst = None
                    for qc in range(qc_min, 2):
                        ps = sps[:, qc * 512:(qc + 1) * 512]
                        t.matmul(ps, c["negones_bf"][:, 0:128],
                                 c["negmax"][0:1, qh * 1024 + qc * 512: qh * 1024 + (qc + 1) * 512],
                                 start=True, stop=True)
                        inst = t.matmul(ps,
                                        c["kf"][hb:hb + 64,
                                                hp * 2048 + pt * 128:hp * 2048 + (pt + 1) * 128],
                                        c["qf"][hb:hb + 64,
                                                hp * 2048 + qh * 1024 + qc * 512:
                                                hp * 2048 + qh * 1024 + (qc + 1) * 512],
                                        start=False, stop=True)
                    pe_s2[pt] = p.inc(inst, "peS")

            if ptp == 0 and pending_norm is not None:
                pending_norm()
                pending_norm = None

            @block.scalar
            def _(s, pts=tuple(pts), qh=qh, causal=causal):
                for pt in pts:
                    ptile = c["pt0"] if pt % 2 == 0 else c["pt1"]
                    sps = c["pC"] if pt % 2 == 0 else c["pA"]
                    qc_min = 0 if not causal else max(0, (pt * 128 - qh * 1024 - 512) // 512 + 1)
                    p.wait(s, "peS", pe_s2[pt])
                    if pt >= 2:
                        p.wait(s, "peS", pe_pv[pt - 2])
                    inst = s.activation(ptile[:, qc_min * 512:1024],
                                        sps[:, qc_min * 512:1024], AF.Exp)
                    ac_exp[pt] = p.inc(inst, "acS")

            dv_mask = {}
            for pt in pts:
                qd = pt * 128 - qh * 1024 if causal else -1000000
                if causal and 0 <= qd < 1024:
                    ptile = c["pt0"] if pt % 2 == 0 else c["pt1"]

                    @block.vector
                    def _(v, pt=pt, qd=qd, ptile=ptile):
                        p.wait(v, "acS", ac_exp[pt])
                        lo = (qd // 512) * 512
                        if qd > lo:
                            inst = v.memset(ptile[:, lo:qd], 0.0)
                            p.inc(inst, "dvS")
                            p.wait(v, "dvS")
                        inst = v.tensor_tensor(ptile[:, qd:qd + 128], ptile[:, qd:qd + 128],
                                               c["maskT"][:], op=AluOpType.mult)
                        p.inc(inst, "dvS")
                    dv_mask[pt] = p.cnt["dvS"]

            for pt in pts:
                ptile = c["pt0"] if pt % 2 == 0 else c["pt1"]
                qc_min = 0 if not causal else max(0, (pt * 128 - qh * 1024 - 512) // 512 + 1)

                @block.tensor
                def _(t, h=h, pt=pt, qc_min=qc_min, ptile=ptile, pt_hi=pt_hi):
                    p.wait(t, "acS", ac_exp[pt])
                    if pt == 0:
                        p.wait(t, "dvS")
                    if pt in dv_mask:
                        p.wait(t, "dvS", dv_mask[pt])
                    inst = None
                    for qc in range(qc_min, 2):
                        last_pt = pt_hi - 1 if not causal else min(pt_hi - 1, qh * 8 + qc * 4 + 3)
                        inst = t.matmul(c["pD"][0:65, qc * 512:(qc + 1) * 512],
                                        c["vtm"][:, pt * 520 + h * 65: pt * 520 + (h + 1) * 65],
                                        ptile[:, qc * 512:(qc + 1) * 512],
                                        start=(pt == 0), stop=(pt == last_pt))
                    pe_pv[pt] = p.inc(inst, "peS")

        pending_norm = make_norm(p.cnt["peS"], hp, hb)
    pending_norm()


def emit_attn_proj(p, qh, slot):
    block, c = p.block, p.c
    dv_of = p.cnt["dvS"]
    pe_done = {}
    ac_free = {}
    dv_free = {}
    if slot == "l0a":
        @block.gpsimd
        def _(g):
            p.wait(g, "peS")
            inst = g.dma_start(c["rowb"][:], p.d["l0_bout"])
            p.inc(inst, "wS", 16)
        io_bout = p.cnt["wS"]
    for ti in range(8):
        tt = qh * 8 + ti
        ps = c["pA"] if ti % 2 == 0 else c["pB"]

        @block.tensor
        def _(t, ti=ti, ps=ps):
            if ti == 0:
                p.wait(t, "dvS", dv_of)
                p.wait(t, "wS")
                p.wait(t, "acS")
                if slot == "l0a":
                    p.wait(t, "wS", io_bout)
            if ti >= 2:
                if slot == "l0a":
                    p.wait(t, "dvS", dv_free[ti - 2])
                    p.wait(t, "acS", ac_free[ti - 2])
                else:
                    p.wait(t, "dvS")
            inst = None
            if slot == "l0a":
                t.matmul(ps[:, 0:512], c["ones_bf"][:, 0:128], c["rowb"][0:1, 0:512],
                         start=True, stop=True)
                t.matmul(ps[:, 512:1024], c["ones_bf"][:, 0:128],
                         c["rowb"][0:1, 512:1024], start=True, stop=True)
            for vk in range(4):
                for dc in range(2):
                    inst = t.matmul(ps[:, dc * 512:(dc + 1) * 512],
                                    c["of"][:, vk * 1024 + ti * 128: vk * 1024 + (ti + 1) * 128],
                                    c["wbA1"][:, vk * 1024 + dc * 512: vk * 1024 + (dc + 1) * 512],
                                    start=(False if slot == "l0a" else vk == 0),
                                    stop=(vk == 3))
            pe_done[ti] = p.inc(inst, "peS")

        if slot == "l1a":
            @block.vector
            def _(v, ti=ti, tt=tt, ps=ps):
                p.wait(v, "peS", pe_done[ti])
                inst = v.tensor_tensor(c["xb"][:, tt * 1024:(tt + 1) * 1024],
                                       c["xb"][:, tt * 1024:(tt + 1) * 1024],
                                       ps[:], op=AluOpType.add)
                p.inc(inst, "dvS")
        else:
            @block.scalar
            def _(s, ti=ti, ps=ps):
                p.wait(s, "peS", pe_done[ti])
                p.wait(s, "dvS")
                inst = s.activation(c["junk"][:], ps[:], AF.Square,
                                    accum_out=c["ycol"][:, 1:2])
                p.inc(inst, "acS")
            ac_sq = p.cnt["acS"]

            @block.vector
            def _(v, ti=ti, ps=ps, ac_sq=ac_sq):
                p.wait(v, "peS", pe_done[ti])
                p.wait(v, "dvS")
                p.wait(v, "acS", ac_sq - 1)
                inst = v.tensor_scalar(c["junk"][:], ps[:], 1.0, None,
                                       op0=AluOpType.mult, op1=AluOpType.add,
                                       accum_out=c["ycol"][:, 0:1])
                dv_free[ti] = p.inc(inst, "dvS")
                p.wait(v, "dvS")
                inst = v.tensor_scalar(c["ycol"][:, 2:3], c["ycol"][:, 0:1], 1.0 / D, None,
                                       op0=AluOpType.mult)
                p.inc(inst, "dvS")
                p.wait(v, "acS", ac_sq)
                p.wait(v, "dvS")
                inst = v.tensor_scalar(c["ycol"][:, 3:4], c["ycol"][:, 1:2], 1.0 / D, None,
                                       op0=AluOpType.mult)
                p.inc(inst, "dvS")
                p.wait(v, "dvS")
                inst = v.tensor_tensor(c["ycol"][:, 4:5], c["ycol"][:, 2:3], c["ycol"][:, 2:3],
                                       op=AluOpType.mult)
                p.inc(inst, "dvS")
                p.wait(v, "dvS")
                inst = v.tensor_tensor(c["ycol"][:, 5:6], c["ycol"][:, 3:4], c["ycol"][:, 4:5],
                                       op=AluOpType.subtract)
                p.inc(inst, "dvS")
            dv_vy = p.cnt["dvS"]

            @block.scalar
            def _(s, dv_vy=dv_vy):
                p.wait(s, "dvS", dv_vy)
                inst = s.activation(c["ycol"][:, 6:7], c["ycol"][:, 5:6], AF.Ln,
                                    bias=c["eps_col"][:, 0:1])
                p.inc(inst, "acS")
                p.wait(s, "acS")
                inst = s.activation(c["ycol"][:, 7:8], c["ycol"][:, 6:7], AF.Exp, scale=-0.5)
                p.inc(inst, "acS")
            ac_ry = p.cnt["acS"]

            @block.vector
            def _(v, ac_ry=ac_ry):
                p.wait(v, "acS", ac_ry)
                inst = v.scalar_tensor_tensor(c["ycol"][:, 4:5], c["ycol"][:, 2:3], -1.0,
                                              c["ycol"][:, 7:8], op0=AluOpType.mult,
                                              op1=AluOpType.mult)
                p.inc(inst, "dvS")
            dv_nmy = p.cnt["dvS"]

            @block.scalar
            def _(s, ps=ps, dv_nmy=dv_nmy, tt=tt, ti=ti):
                p.wait(s, "dvS", dv_nmy)
                inst = s.activation(c["lntm"][:], ps[:], AF.Identity,
                                    scale=c["ycol"][:, 7:8], bias=c["ycol"][:, 4:5])
                ac_free[ti] = p.inc(inst, "acS")
                p.wait(s, "acS")
                inst = s.activation(c["xb"][:, tt * 1024:(tt + 1) * 1024],
                                    c["xb"][:, tt * 1024:(tt + 1) * 1024], AF.Identity,
                                    scale=c["A_c"][:, tt:tt + 1], bias=c["B_c"][:, tt:tt + 1])
                p.inc(inst, "acS")
            ac_t = p.cnt["acS"]

            @block.vector
            def _(v, ac_t=ac_t, tt=tt):
                p.wait(v, "acS", ac_t)
                inst = v.tensor_tensor(c["lntm"][:], c["lntm"][:], c["gamma_bc"],
                                       op=AluOpType.mult)
                p.inc(inst, "dvS")
                p.wait(v, "dvS")
                inst = v.tensor_tensor(c["lntm"][:], c["lntm"][:], c["beta_bc"],
                                       op=AluOpType.add)
                p.inc(inst, "dvS")
                p.wait(v, "dvS")
                inst = v.tensor_tensor(c["xb"][:, tt * 1024:(tt + 1) * 1024],
                                       c["xb"][:, tt * 1024:(tt + 1) * 1024],
                                       c["lntm"][:], op=AluOpType.add)
                p.inc(inst, "dvS")


def emit_gb_build(p):
    block, c = p.block, p.c
    for which, dname, ps in (("g", "l0_gamma", "pA"), ("b", "l0_beta", "pB")):
        @block.gpsimd
        def _(g, dname=dname):
            p.wait(g, "peS")
            inst = g.dma_start(c["rowb"][:], p.d[dname])
            p.inc(inst, "wS", 16)
        iom = p.cnt["wS"]

        @block.tensor
        def _(t, ps=ps, iom=iom):
            p.wait(t, "wS", iom)
            p.wait(t, "dvS"); p.wait(t, "acS")
            inst = None
            for half in range(2):
                inst = t.matmul(c[ps][:, half * 512:(half + 1) * 512], c["ones_bf"][:, 0:128],
                                c["rowb"][0:1, half * 512:(half + 1) * 512],
                                start=True, stop=True)
            p.inc(inst, "peS")
    pe1 = p.cnt["peS"]

    @block.scalar
    def _(s):
        p.wait(s, "peS", pe1)
        s.activation(c["gamma_bc"], c["pA"][:], AF.Copy, scale=RSQRT2)
        inst = s.activation(c["beta_bc"], c["pB"][:], AF.Copy, scale=RSQRT2)
        p.inc(inst, "acS")


def emit_l0a(p):
    c = p.c
    emit_ln(p, save_ab=True)
    pa = PassA(p, causal=False)
    emit_proj_fm(p, "l0_wq", [0, 1, 2, 3], c["qf"], bias_fm=c["bq_fm"],
                 bias_cols=[0, 1, 2, 3], scale=QK_SCALE)

    def hook():
        pa.pump(6)
    km = emit_proj_fm(p, "l0_wk", [0, 1, 2, 3], c["kf"], bias_fm=c["bk_fm"],
                      bias_cols=[0, 1, 2, 3], scale=1.0,
                      interleave=None)
    for hp in range(4):
        pa.set_ready(hp, km[hp])
    emit_vtm(p, "l0_wv", "l0_bv", 0, interleave=lambda: pa.pump(6))

    @p.block.gpsimd
    def _(g):
        p.wait(g, "peS")
        src = p.d["l0_wout"].rearrange("(vk p) c -> p vk c", p=128)
        dst = c["wbA1"][:, 0:4096].rearrange("p (vk c) -> p vk c", c=1024)
        inst = g.dma_start(dst, src)
        p.inc(inst, "wS", 16)
    pa.finish()
    emit_gb_build(p)
    for qh in range(2):
        emit_attn_phaseC(p, qh, causal=False)
        emit_attn_proj(p, qh, "l0a")


def emit_l1a(p):
    c = p.c
    emit_ln(p, save_ab=False)
    for grp in range(2):
        pa = PassA(p, causal=True)
        emit_proj_fm(p, "l1_wq", [4 * grp + i for i in range(4)], c["qf"], scale=QK_SCALE)
        km = emit_proj_fm(p, "l1_wk", [4 * grp + i for i in range(4)], c["kf"], scale=1.0)
        for hp in range(4):
            pa.set_ready(hp, km[hp])
        emit_vtm(p, "l1_wv", "l1_bv", 520 * grp, interleave=lambda: pa.pump(6))

        @p.block.gpsimd
        def _(g, grp=grp):
            p.wait(g, "peS")
            src = p.d["l1_wo"][512 * grp:512 * (grp + 1), :].rearrange(
                "(vk p) c -> p vk c", p=128)
            dst = c["wbA1"][:, 0:4096].rearrange("p (vk c) -> p vk c", c=1024)
            inst = g.dma_start(dst, src)
            p.inc(inst, "wS", 16)
        pa.finish()
        for qh in range(2):
            emit_attn_phaseC(p, qh, causal=True)
            emit_attn_proj(p, qh, "l1a")


def emit_l0f(p):
    c = p.c
    block = p.block
    emit_ln(p, save_ab=False)

    for th in range(2):
        wm = {}
        pe_kt = {}
        for hc in range(6):
            @block.gpsimd
            def _(g, hc=hc, th=th):
                if hc >= 1:
                    p.wait(g, "peS", pe_kt[hc * 4 - 1])
                elif th == 0:
                    p.wait(g, "peS")
                else:
                    p.wait(g, "peS")
                inst = g.dma_start(c["wbA0"][:, 0:4096],
                                   p.d["l0_wg"][hc].rearrange("p dk c -> p (dk c)"))
                p.inc(inst, "wS", 16)
                inst = g.dma_start(c["wbA1"][:, 0:4096],
                                   p.d["l0_wu"][hc].rearrange("p dk c -> p (dk c)"))
                wm[hc] = p.inc(inst, "wS", 16)

            for ht in range(4):
                kt = hc * 4 + ht

                @block.tensor
                def _(t, hc=hc, ht=ht, th=th, kt=kt):
                    if ht == 0:
                        p.wait(t, "wS", wm[hc])
                    if (th, hc, ht) == (0, 0, 0):
                        p.wait(t, "ioS", p.lnf_ready)
                    p.wait(t, "dvS"); p.wait(t, "acS")
                    inst = None
                    for dk in range(DK):
                        for tc in range(2):
                            t.matmul(c["pA"][:, tc * 512:(tc + 1) * 512],
                                     c["wbA0"][:, dk * 512 + ht * 128: dk * 512 + (ht + 1) * 128],
                                     lnf_strip(p, dk, th * 1024 + tc * 512,
                                               th * 1024 + (tc + 1) * 512),
                                     start=(dk == 0), stop=(dk == DK - 1))
                            inst = t.matmul(c["pB"][:, tc * 512:(tc + 1) * 512],
                                            c["wbA1"][:, dk * 512 + ht * 128: dk * 512 + (ht + 1) * 128],
                                            lnf_strip(p, dk, th * 1024 + tc * 512,
                                                      th * 1024 + (tc + 1) * 512),
                                            start=(dk == 0), stop=(dk == DK - 1))
                    pe_kt[kt] = p.inc(inst, "peS")
                pe_m = p.cnt["peS"]

                @block.scalar
                def _(s, pe_m=pe_m):
                    p.wait(s, "peS", pe_m)
                    p.wait(s, "dvS")
                    inst = s.activation(c["lntm"][:], c["pA"][:], AF.Silu, scale=DEQ)
                    p.inc(inst, "acS")
                ac_m = p.cnt["acS"]

                @block.vector
                def _(v, kt=kt, ac_m=ac_m):
                    p.wait(v, "acS", ac_m)
                    inst = v.scalar_tensor_tensor(c["hfm"][:, kt * 1024:(kt + 1) * 1024],
                                                  c["pB"][:], DEQ, c["lntm"][:],
                                                  op0=AluOpType.mult, op1=AluOpType.mult)
                    p.inc(inst, "dvS")
        dv_up = p.cnt["dvS"]

        pe_dkt = {}
        for dc in range(2):
            pe_dn = None
            for ktg in range(6):
                @block.gpsimd
                def _(g, dc=dc, ktg=ktg):
                    k = dc * 6 + ktg
                    if k >= 2:
                        p.wait(g, "peS", pe_dkt[k - 2])
                    elif k == 0:
                        p.wait(g, "peS")
                    inst = g.dma_start(c["wbB0" if ktg % 2 == 0 else "wbB1"][:],
                                       p.d["l0_wd"][dc, ktg].rearrange("p j c -> p (j c)"))
                    wm[(dc, ktg)] = p.inc(inst, "wS", 16)

                @block.tensor
                def _(t, dc=dc, ktg=ktg, dv_up=dv_up):
                    p.wait(t, "wS", wm[(dc, ktg)])
                    if ktg == 0:
                        p.wait(t, "dvS")
                        p.wait(t, "acS")
                    wb = c["wbB0" if ktg % 2 == 0 else "wbB1"]
                    inst = None
                    for j in range(4):
                        kt = ktg * 4 + j
                        for tt in range(8):
                            pst = c[PSUMS[tt // 2]]
                            inst = t.matmul(pst[:, (tt % 2) * 512:(tt % 2 + 1) * 512],
                                            c["hfm"][:, kt * 1024 + tt * 128: kt * 1024 + (tt + 1) * 128],
                                            wb[:, j * 512:(j + 1) * 512],
                                            start=(kt == 0), stop=(kt == 23))
                    pe_dkt[dc * 6 + ktg] = p.inc(inst, "peS")
            pe_dn = p.cnt["peS"]

            for tt in range(8):
                @block.vector
                def _(v, tt=tt, dc=dc, th=th, pe_dn=pe_dn):
                    p.wait(v, "peS", pe_dn)
                    pst = c[PSUMS[tt // 2]]
                    col = (th * 8 + tt) * 1024 + dc * 512
                    inst = v.scalar_tensor_tensor(c["xb"][:, col:col + 512],
                                                  pst[:, (tt % 2) * 512:(tt % 2 + 1) * 512],
                                                  DEQ, c["xb"][:, col:col + 512],
                                                  op0=AluOpType.mult, op1=AluOpType.add)
                    p.inc(inst, "dvS")


def emit_l1f(p):
    c = p.c
    block = p.block
    emit_ln(p, save_ab=False)

    @block.gpsimd
    def _(g):
        p.wait(g, "acS")
        src = p.d["l1_wgr"].rearrange("(dk p) c -> p dk c", p=128)
        dst = c["wgr_sb"][:].rearrange("p (dk c) -> p dk c", c=8)
        inst = g.dma_start(dst, src)
        p.inc(inst, "wS", 16)
    wgr_m = p.cnt["wS"]
    for tt in range(NT):
        @block.tensor
        def _(t, tt=tt):
            if tt == 0:
                p.wait(t, "wS", wgr_m)
                p.wait(t, "ioS", p.lnf_ready)
                p.wait(t, "acS")
            else:
                p.wait(t, "acS")
            t.matmul(c["pC"][:, 0:8], c["ones_bf"][:, 0:128], c["bgbb"][:],
                     start=True, stop=True)
            inst = None
            for dk in range(DK):
                inst = t.matmul(c["pC"][:, 0:8], lnf_strip(p, dk, tt * 128, (tt + 1) * 128),
                                c["wgr_sb"][:, dk * 8:(dk + 1) * 8],
                                start=False, stop=(dk == DK - 1))
            p.inc(inst, "peS")
        pe_r = p.cnt["peS"]

        @block.scalar
        def _(s, pe_r=pe_r):
            p.wait(s, "peS", pe_r)
            p.wait(s, "dvS")
            inst = s.activation(c["rtmp"][:], c["pC"][:, 0:8], AF.Sigmoid)
            p.inc(inst, "acS")
        ac_r = p.cnt["acS"]

        @block.vector
        def _(v, tt=tt, ac_r=ac_r):
            p.wait(v, "acS", ac_r)
            inst = v.tensor_reduce(c["rcol"][:, 0:1], c["rtmp"][:], axis=AX.X,
                                   op=AluOpType.max)
            p.inc(inst, "dvS")
            p.wait(v, "dvS")
            inst = v.tensor_scalar(c["rtm2"][:], c["rtmp"][:], c["rcol"][:, 0:1], None,
                                   op0=AluOpType.is_ge, op1=AluOpType.add,
                                   accum_out=c["rcol"][:, 2:3])
            p.inc(inst, "dvS")
            p.wait(v, "dvS")
            inst = v.scalar_tensor_tensor(c["rtm3"][:], c["rtm2"][:], -1e30, c["rtmp"][:],
                                          op0=AluOpType.mult, op1=AluOpType.add)
            p.inc(inst, "dvS")
            p.wait(v, "dvS")
            inst = v.tensor_reduce(c["rcol"][:, 1:2], c["rtm3"][:], axis=AX.X,
                                   op=AluOpType.max)
            p.inc(inst, "dvS")
            p.wait(v, "dvS")
            inst = v.tensor_scalar(c["rcol"][:, 3:4], c["rcol"][:, 2:3], 2.0, None,
                                   op0=AluOpType.is_ge)
            p.inc(inst, "dvS")
            p.wait(v, "dvS")
            inst = v.tensor_tensor(c["rcol"][:, 2:3], c["rcol"][:, 0:1], c["rcol"][:, 1:2],
                                   op=AluOpType.subtract)
            p.inc(inst, "dvS")
            p.wait(v, "dvS")
            inst = v.scalar_tensor_tensor(c["rcol"][:, 0:1], c["rcol"][:, 2:3],
                                          c["rcol"][:, 3:4], c["rcol"][:, 1:2],
                                          op0=AluOpType.mult, op1=AluOpType.add)
            p.inc(inst, "dvS")
            p.wait(v, "dvS")
            inst = v.tensor_scalar(c["rtm3"][:], c["rtmp"][:], c["rcol"][:, 0:1], None,
                                   op0=AluOpType.is_ge)
            p.inc(inst, "dvS")
            p.wait(v, "dvS")
            inst = v.tensor_tensor(c["w_sb"][:, tt * 8:(tt + 1) * 8], c["rtm3"][:],
                                   c["rtmp"][:], op=AluOpType.mult)
            p.inc(inst, "dvS")

    for th in range(2):
        for e in range(9):
            wm1, ac_g, pe_u = {}, {}, {}

            @block.gpsimd
            def _(g, e=e):
                p.wait(g, "peS")
                inst = g.dma_start(c["rowb"][:], p.d["l1_b2"][:, e * 1024:(e + 1) * 1024])
                p.inc(inst, "wS", 16)
            io_b2 = p.cnt["wS"]
            for hc in range(8):
                @block.gpsimd
                def _(g, e=e, hc=hc):
                    if hc >= 2:
                        p.wait(g, "peS", pe_u[(hc - 2, 3)])
                    elif hc == 0:
                        p.wait(g, "peS")
                    inst = g.dma_start(c["wbA0" if hc % 2 == 0 else "wbA1"][:, 0:4096],
                                       p.d["l1_w1"][e, hc].rearrange("p dk c -> p (dk c)"))
                    wm1[hc] = p.inc(inst, "wS", 16)

                for ht in range(4):
                    kt = hc * 4 + ht

                    @block.tensor
                    def _(t, hc=hc, ht=ht, th=th, kt=kt):
                        if ht == 0:
                            p.wait(t, "wS", wm1[hc])
                        if kt >= 4:
                            p.wait(t, "acS", ac_g[kt - 4])
                        elif kt == 0:
                            p.wait(t, "dvS"); p.wait(t, "acS")
                        wb = c["wbA0" if hc % 2 == 0 else "wbA1"]
                        ps = c[PSUMS[kt % 4]]
                        inst = None
                        for dk in range(DK):
                            for tc in range(2):
                                inst = t.matmul(ps[:, tc * 512:(tc + 1) * 512],
                                                wb[:, dk * 512 + ht * 128: dk * 512 + (ht + 1) * 128],
                                                lnf_strip(p, dk, th * 1024 + tc * 512,
                                                          th * 1024 + (tc + 1) * 512),
                                                start=(dk == 0), stop=(dk == DK - 1))
                        pe_u[(hc, ht)] = p.inc(inst, "peS")

                @block.scalar
                def _(s, e=e, hc=hc):
                    for ht in range(4):
                        kt = hc * 4 + ht
                        p.wait(s, "peS", pe_u[(hc, ht)])
                        ps = c[PSUMS[kt % 4]]
                        inst = s.activation(c["hfm"][:, kt * 1024:(kt + 1) * 1024], ps[:],
                                            AF.Gelu,
                                            bias=c["b1_sb"][:, e * 32 + kt:e * 32 + kt + 1])
                        ac_g[kt] = p.inc(inst, "acS")
            ac_up = p.cnt["acS"]

            wm2, pe_d = {}, {}

            @block.gpsimd
            def _(g, e=e):
                p.wait(g, "peS")
                inst = g.dma_start(c["rowb"][:], p.d["l1_b2"][:, e * 1024:(e + 1) * 1024])
                p.inc(inst, "wS", 16)
            io_b2 = p.cnt["wS"]
            for dc in range(2):
                @block.tensor
                def _(t, e=e, dc=dc, ac_up=ac_up, io_b2=io_b2):
                    p.wait(t, "acS", ac_up)
                    p.wait(t, "dvS")
                    p.wait(t, "wS", io_b2)
                    inst = None
                    for tt in range(8):
                        pst = c[PSUMS[tt // 2]]
                        inst = t.matmul(pst[:, (tt % 2) * 512:(tt % 2 + 1) * 512],
                                        c["ones_bf"][:, 0:128],
                                        c["rowb"][0:1, dc * 512:(dc + 1) * 512],
                                        start=True, stop=True)
                    p.inc(inst, "peS")

                for ktg in range(8):
                    @block.gpsimd
                    def _(g, e=e, dc=dc, ktg=ktg):
                        k = dc * 8 + ktg
                        if k >= 2:
                            p.wait(g, "peS", pe_d[k - 2])
                        inst = g.dma_start(c["wbB0" if ktg % 2 == 0 else "wbB1"][:],
                                           p.d["l1_w2"][e, dc, ktg].rearrange("p j c -> p (j c)"))
                        wm2[(dc, ktg)] = p.inc(inst, "wS", 16)

                    @block.tensor
                    def _(t, dc=dc, ktg=ktg):
                        p.wait(t, "wS", wm2[(dc, ktg)])
                        p.wait(t, "acS", ac_g[min(ktg * 4 + 3, 31)])
                        wb = c["wbB0" if ktg % 2 == 0 else "wbB1"]
                        inst = None
                        for j in range(4):
                            kt = ktg * 4 + j
                            for tt in range(8):
                                pst = c[PSUMS[tt // 2]]
                                inst = t.matmul(pst[:, (tt % 2) * 512:(tt % 2 + 1) * 512],
                                                c["hfm"][:, kt * 1024 + tt * 128:
                                                          kt * 1024 + (tt + 1) * 128],
                                                wb[:, j * 512:(j + 1) * 512],
                                                start=False, stop=(kt == 31))
                        pe_d[dc * 8 + ktg] = p.inc(inst, "peS")
                pe_dn = p.cnt["peS"]

                for tt in range(8):
                    @block.vector
                    def _(v, e=e, tt=tt, dc=dc, th=th, pe_dn=pe_dn):
                        p.wait(v, "peS", pe_dn)
                        pst = c[PSUMS[tt // 2]]
                        tg = th * 8 + tt
                        col = tg * 1024 + dc * 512
                        if e < 8:
                            inst = v.scalar_tensor_tensor(
                                c["xb"][:, col:col + 512],
                                pst[:, (tt % 2) * 512:(tt % 2 + 1) * 512],
                                c["w_sb"][:, tg * 8 + e:tg * 8 + e + 1],
                                c["xb"][:, col:col + 512],
                                op0=AluOpType.mult, op1=AluOpType.add)
                        else:
                            inst = v.tensor_tensor(
                                c["xb"][:, col:col + 512],
                                c["xb"][:, col:col + 512],
                                pst[:, (tt % 2) * 512:(tt % 2 + 1) * 512],
                                op=AluOpType.add)
                        p.inc(inst, "dvS")


SLOT_EMITTERS = {"l0a": emit_l0a, "l0f": emit_l0f, "l1a": emit_l1a, "l1f": emit_l1f}


def build_program(slots, n_batches):
    nc = bass.Bass("TRN2", target_bir_lowering=False)
    dram = {}
    dram["x"] = nc.dram_tensor("x", [n_batches * S, D], F32, kind="ExternalInput").ap()
    dram["out"] = nc.dram_tensor("out", [n_batches * S, D], F32, kind="ExternalOutput").ap()
    dram["mbounce"] = nc.dram_tensor("mbounce", [128, 128], BF16, kind="Internal").ap()
    wnames = sorted({w for sl in slots for w in SLOT_WEIGHTS[sl]})
    for w in wnames:
        sh, dt = PREP_SPECS[w]
        dram[w] = nc.dram_tensor(w, list(sh), dt, kind="ExternalInput").ap()
    with ExitStack() as st:
        block = st.enter_context(nc.Block())
        p = P(nc, block, st)
        p.d = dram
        alloc_buffers(p, st)
        emit_consts(p, slots)
        for b in range(n_batches):
            emit_xload(p, b)
            for sl in slots:
                SLOT_EMITTERS[sl](p)
            emit_xstore(p, b)

        @block.sync
        def _(sy):
            p.wait(sy, "ioS")
    return nc, wnames


def kernel(**inputs):
    prep = host_prep(inputs)
    x = np.ascontiguousarray(inputs["L0_attn_x"].astype(np.float32, copy=False)
                             ).reshape(B, S, D)
    groups = [ALL_SLOTS] if MODE == "fused" else [(s,) for s in ALL_SLOTS]
    for slots in groups:
        nc, wnames = build_program(slots, BPC)
        in_maps = []
        for core in range(N_CORES):
            m = {w: prep[w] for w in wnames}
            m["x"] = np.ascontiguousarray(
                x[core * BPC:(core + 1) * BPC].reshape(BPC * S, D))
            in_maps.append(m)
        res = run_bass_kernel_spmd(nc, in_maps, core_ids=list(range(N_CORES)))
        x = np.concatenate(
            [r["out"].reshape(BPC, S, D) for r in res.results], axis=0)
    return np.ascontiguousarray(x.reshape(B, S, D)).astype(np.float32, copy=False)


# version 08
# speedup vs baseline: 1.0528x; 1.0528x over previous
import numpy as np
from contextlib import ExitStack

import concourse.bass as bass
import concourse.mybir as mybir
from concourse.bass_utils import run_bass_kernel_spmd
from concourse.alu_op_type import AluOpType

F32 = mybir.dt.float32
BF16 = mybir.dt.bfloat16
I32 = mybir.dt.int32
AF = mybir.ActivationFunctionType
AX = mybir.AxisListType

B, S, D = 32, 2048, 1024
N_CORES = 8
BPC = B // N_CORES
NT = S // 128
DK = D // 128
RSQRT2 = float(2 ** (-0.5))
LN_EPS = 1e-5
QK_SCALE = 0.125
DEQ = 1.0 / 127.0
MODE = "fused"

ALL_SLOTS = ("l0a", "l0f", "l1a", "l1f")


def _lhs_stream(w_t, n_out_tiles):
    dk = w_t.shape[0] // 128
    a = w_t.reshape(dk, 128, n_out_tiles, 128)
    return np.ascontiguousarray(a.transpose(2, 1, 0, 3))


def _up_stream(w_t, n_hc, w_hc):
    dk = w_t.shape[0] // 128
    a = w_t.reshape(dk, 128, n_hc, w_hc)
    return np.ascontiguousarray(a.transpose(2, 1, 0, 3))


def _down_stream(w_t, n_dc, w_dc, g):
    kt = w_t.shape[0] // 128
    a = w_t.reshape(kt // g, g, 128, n_dc, w_dc)
    return np.ascontiguousarray(a.transpose(3, 0, 2, 1, 4))


def host_prep(inp):
    p = {}
    W_qk = inp["L0_attn_W_qk"]
    wq = np.ascontiguousarray(W_qk.reshape(D, 8, 64, 2)[..., 0].reshape(D, 512))
    wk = np.ascontiguousarray(W_qk.reshape(D, 8, 64, 2)[..., 1].reshape(D, 512))
    b_qk = inp["L0_attn_b_qk"]
    p["l0_bq"] = np.ascontiguousarray(b_qk.reshape(8, 64, 2)[..., 0].reshape(4, 128).T)
    p["l0_bk"] = np.ascontiguousarray(b_qk.reshape(8, 64, 2)[..., 1].reshape(4, 128).T)
    p["l0_wq"] = _lhs_stream(wq, 4)
    p["l0_wk"] = _lhs_stream(wk, 4)
    wv_ext = np.zeros((D, 8, 65), np.float32)
    wv_ext[:, :, :64] = inp["L0_attn_W_v"].reshape(D, 8, 64)
    p["l0_wv"] = np.ascontiguousarray(wv_ext.reshape(D, 520))
    bv_ext = np.zeros((8, 65), np.float32)
    bv_ext[:, :64] = inp["L0_attn_b_v"].reshape(8, 64)
    bv_ext[:, 64] = 1.0
    p["l0_bv"] = np.ascontiguousarray(bv_ext.reshape(1, 520))
    p["l0_wout"] = np.ascontiguousarray(inp["L0_attn_W_out"])
    p["l0_bout"] = np.ascontiguousarray(inp["L0_attn_b_out"].reshape(1, D))
    p["l0_gamma"] = np.ascontiguousarray(inp["L0_attn_ln_gamma"].reshape(1, D))
    p["l0_beta"] = np.ascontiguousarray(inp["L0_attn_ln_beta"].reshape(1, D))
    wg_t = np.ascontiguousarray(inp["L0_ffn_w_gate"].T)
    wu_t = np.ascontiguousarray(inp["L0_ffn_w_up"].T)
    wd_t = np.ascontiguousarray(inp["L0_ffn_w_down"].T)
    p["l0_wg"] = _up_stream(wg_t, 6, 512)
    p["l0_wu"] = _up_stream(wu_t, 6, 512)
    p["l0_wd"] = _down_stream(wd_t, 2, 512, 4)
    wq1 = np.ascontiguousarray(inp["L1_attn_W_Q"].transpose(2, 0, 1).reshape(D, 1024))
    wk1 = np.ascontiguousarray(inp["L1_attn_W_K"].transpose(2, 0, 1).reshape(D, 1024))
    p["l1_wq"] = _lhs_stream(wq1, 8)
    p["l1_wk"] = _lhs_stream(wk1, 8)
    wv1_ext = np.zeros((D, 16, 65), np.float32)
    wv1_ext[:, :, :64] = inp["L1_attn_W_V"].transpose(2, 0, 1).reshape(D, 16, 64)
    p["l1_wv"] = np.ascontiguousarray(wv1_ext.reshape(D, 1040))
    bv1 = np.zeros((16, 65), np.float32)
    bv1[:, 64] = 1.0
    p["l1_bv"] = np.ascontiguousarray(bv1.reshape(1, 1040))
    p["l1_wo"] = np.ascontiguousarray(inp["L1_attn_W_O"].T)
    W1 = np.concatenate([inp["L1_ffn_W1"], inp["L1_ffn_Ws1"][None]], 0)
    W2 = np.concatenate([inp["L1_ffn_W2"], inp["L1_ffn_Ws2"][None]], 0)
    b1 = np.concatenate([inp["L1_ffn_b1"], inp["L1_ffn_bs1"][None]], 0)
    b2 = np.concatenate([inp["L1_ffn_b2"], inp["L1_ffn_bs2"][None]], 0)
    w1t = W1.transpose(0, 2, 1)
    p["l1_w1"] = np.ascontiguousarray(
        w1t.reshape(9, DK, 128, 8, 512).transpose(0, 3, 2, 1, 4))
    w2t = W2.transpose(0, 2, 1)
    p["l1_w2"] = np.ascontiguousarray(
        w2t.reshape(9, 8, 4, 128, 2, 512).transpose(0, 4, 1, 3, 2, 5))
    p["l1_b1"] = np.ascontiguousarray(b1.reshape(9, 32, 128).transpose(0, 2, 1))
    p["l1_b2"] = np.ascontiguousarray(b2.reshape(1, 9 * D))
    p["l1_wgr"] = np.ascontiguousarray(inp["L1_ffn_Wg"].T)
    p["l1_bgb"] = np.ascontiguousarray(
        np.stack([inp["L1_ffn_bg"], inp["L1_ffn_bias"]], 0))
    return p


PREP_SPECS = {
    "l0_bq": ((128, 4), F32), "l0_bk": ((128, 4), F32),
    "l0_wq": ((4, 128, 8, 128), F32), "l0_wk": ((4, 128, 8, 128), F32),
    "l0_wv": ((D, 520), F32), "l0_bv": ((1, 520), F32),
    "l0_wout": ((512, D), F32), "l0_bout": ((1, D), F32),
    "l0_gamma": ((1, D), F32), "l0_beta": ((1, D), F32),
    "l0_wg": ((6, 128, 8, 512), I32), "l0_wu": ((6, 128, 8, 512), I32),
    "l0_wd": ((2, 6, 128, 4, 512), I32),
    "l1_wq": ((8, 128, 8, 128), F32), "l1_wk": ((8, 128, 8, 128), F32),
    "l1_wv": ((D, 1040), F32), "l1_bv": ((1, 1040), F32),
    "l1_wo": ((D, D), F32),
    "l1_w1": ((9, 8, 128, 8, 512), F32), "l1_w2": ((9, 2, 8, 128, 4, 512), F32),
    "l1_b1": ((9, 128, 32), F32), "l1_b2": ((1, 9 * D), F32),
    "l1_wgr": ((D, 8), F32), "l1_bgb": ((2, 8), F32),
}

SLOT_WEIGHTS = {
    "l0a": ["l0_bq", "l0_bk", "l0_wq", "l0_wk", "l0_wv", "l0_bv", "l0_wout",
            "l0_bout", "l0_gamma", "l0_beta"],
    "l0f": ["l0_wg", "l0_wu", "l0_wd"],
    "l1a": ["l1_wq", "l1_wk", "l1_wv", "l1_bv", "l1_wo"],
    "l1f": ["l1_w1", "l1_w2", "l1_b1", "l1_b2", "l1_wgr", "l1_bgb"],
}


class P:
    def __init__(self, nc, block, st):
        self.nc, self.block, self.st = nc, block, st
        self.c = {}
        self.d = {}
        self.cnt = {}
        self.hs = {}
        for n in ("wS", "ioS", "peS", "dvS", "acS", "gpS"):
            self.hs[n] = st.enter_context(nc.semaphore(n))
            self.cnt[n] = 0

    def inc(self, inst, name, n=1):
        inst.then_inc(self.hs[name], n)
        self.cnt[name] += n
        return self.cnt[name]

    def wait(self, eng, name, val=None):
        v = self.cnt[name] if val is None else val
        if v > 0:
            eng.wait_ge(self.hs[name], v)


def lnf_strip(p, dk, lo, hi):
    r = p.c["lnf"][:].rearrange("p (dk q) -> p dk q", dk=DK)
    return r[:, dk, lo:hi]


def alloc_buffers(p, st):
    nc = p.nc
    def T(n, sh, dt):
        p.c[n] = st.enter_context(nc.sbuf_tensor(n, sh, dt))
    def PS(n):
        p.c[n] = st.enter_context(nc.psum_tensor(n, [128, 2048], F32))
    T("xb", [128, NT * 1024], F32)
    T("lnf", [128, DK * 2048], BF16)
    T("blk", [128, 32896], BF16)
    T("wbA0", [128, 4160], BF16)
    T("wbA1", [128, 4160], BF16)
    T("wbB0", [128, 2048], BF16)
    T("wbB1", [128, 2048], BF16)
    T("ptb", [128, 2048], BF16)
    T("junk", [128, 1024], BF16)
    T("sum_c", [128, NT], F32); T("sq_c", [128, NT], F32)
    T("mu", [128, NT], F32); T("msq", [128, NT], F32); T("mu2", [128, NT], F32)
    T("var", [128, NT], F32); T("lnv", [128, NT], F32); T("rs", [128, NT], F32)
    T("nmr", [128, NT], F32); T("A_c", [128, NT], F32); T("B_c", [128, NT], F32)
    T("ycol", [128, 8], F32)
    T("eps_col", [128, 1], F32)
    T("M_all", [128, 128], F32)
    T("mtmp", [128, 2], F32)
    T("mts", [128, 128], BF16)
    T("negmax", [1, 2048], BF16)
    T("rb", [64, 512], F32)
    T("w_sb", [128, NT * 8], F32)
    T("rtmp", [128, 8], F32)
    T("rtm2", [128, 8], F32)
    T("rtm3", [128, 8], F32)
    T("rcol", [128, 4], F32)
    T("ident", [128, 128], F32)
    T("maskT", [128, 128], BF16)
    T("ones_f", [1, 128], F32)
    T("ones_bf", [1, 128], BF16)
    T("negones_bf", [1, 128], BF16)
    T("rowb", [1, D], BF16)
    T("rrow", [1, 512], BF16)
    T("bgbb", [1, 8], BF16)
    T("bq_fm", [128, 8], F32)
    T("bk_fm", [128, 8], F32)
    T("b1_sb", [128, 9 * 32], F32)
    T("bgb", [1, 8], F32)
    T("wgr_sb", [128, 64], BF16)
    PS("pAB"); PS("pCD")
    p.c["pA"] = p.c["pAB"][:][:, 0:1024]
    p.c["pB"] = p.c["pAB"][:][:, 1024:2048]
    p.c["pC"] = p.c["pCD"][:][:, 0:1024]
    p.c["pD"] = p.c["pCD"][:][:, 1024:2048]
    p.c["pt0"] = p.c["ptb"][:, 0:1024]
    p.c["pt1"] = p.c["ptb"][:, 1024:2048]
    p.c["lntm"] = p.c["ptb"][:].bitcast(F32)
    blk = p.c["blk"]
    p.c["qf"] = blk[:, 0:8192]
    p.c["kf"] = blk[:, 8192:16384]
    p.c["of"] = blk[:, 16384:20480]
    p.c["vtm"] = blk[:, 20480:28800]
    gb = blk[:, 28800:32896]
    p.c["hfm"] = blk[:, 0:32768]
    p.c["gamma_bc"] = gb.bitcast(F32)[:, 0:1024]
    p.c["beta_bc"] = gb.bitcast(F32)[:, 1024:2048]


PSUMS = ("pA", "pB", "pC", "pD")


PRELOAD_FUNCS = {6: AF.Exp, 18: AF.Silu, 2: AF.Sigmoid, 10: AF.Gelu}


def preload_set(p, idx):
    @p.block.scalar
    def _(s):
        s.preload_act_table(PRELOAD_FUNCS[idx])



def emit_consts(p, slots):
    nc, block, c = p.nc, p.block, p.c
    with ExitStack() as st2:
        ic = st2.enter_context(nc.sbuf_tensor("iotac", [128, 128], I32))
        ip_ = st2.enter_context(nc.sbuf_tensor("iotap", [128, 1], I32))
        icf = st2.enter_context(nc.sbuf_tensor("iotacf", [128, 128], F32))
        ipf = st2.enter_context(nc.sbuf_tensor("iotapf", [128, 1], F32))

        @block.gpsimd
        def _(g):
            g.iota(ic[:], pattern=[[1, 128]], base=0, channel_multiplier=0)
            g.iota(ip_[:], pattern=[[0, 1]], base=0, channel_multiplier=1)
            inst = g.tensor_copy(icf[:], ic[:])
            p.inc(inst, "gpS")
            inst = g.tensor_copy(ipf[:], ip_[:])
            p.inc(inst, "gpS")
            p.wait(g, "gpS")
            g.tensor_scalar(c["ident"][:], icf[:], ipf[:, 0:1], None, op0=AluOpType.is_equal)
            inst = g.tensor_scalar(c["maskT"][:], icf[:], ipf[:, 0:1], None, op0=AluOpType.is_ge)
            p.inc(inst, "gpS")
    gp_c = p.cnt["gpS"]

    @block.vector
    def _(v):
        v.memset(c["ones_f"][:], 1.0)
        v.memset(c["ones_bf"][:], 1.0)
        v.memset(c["negones_bf"][:], -1.0)
        inst = v.memset(c["eps_col"][:], LN_EPS)
        p.inc(inst, "dvS")

    @block.sync
    def _(sy):
        if "l1f" in slots:
            inst = sy.dma_start(c["b1_sb"][:].rearrange("p (e c) -> p e c", c=32),
                                p.d["l1_b1"].rearrange("e p c -> p e c"))
            p.inc(inst, "ioS", 16)
            inst = sy.dma_start(c["rtmp"][0:1, :], p.d["l1_bgb"][0:1, :])
            p.inc(inst, "ioS", 16)
            inst = sy.dma_start(c["rtm2"][0:1, :], p.d["l1_bgb"][1:2, :])
            p.inc(inst, "ioS", 16)
        if "l0a" in slots:
            inst = sy.dma_start(c["bq_fm"][:, 0:4], p.d["l0_bq"])
            p.inc(inst, "ioS", 16)
            inst = sy.dma_start(c["bk_fm"][:, 0:4], p.d["l0_bk"])
            p.inc(inst, "ioS", 16)
    io0 = p.cnt["ioS"]

    @block.vector
    def _(v):
        p.wait(v, "ioS", io0)
        p.wait(v, "gpS", gp_c)
        if "l0a" in slots:
            inst = v.tensor_scalar(c["bq_fm"][:, 0:4], c["bq_fm"][:, 0:4], QK_SCALE, None,
                                   op0=AluOpType.mult)
            p.inc(inst, "dvS")
        if "l1f" in slots:
            inst = v.tensor_tensor(c["bgb"][:], c["rtmp"][0:1, :], c["rtm2"][0:1, :],
                                   op=AluOpType.add)
            p.inc(inst, "dvS")
            p.wait(v, "dvS")
            inst = v.tensor_copy(c["bgbb"][:], c["bgb"][:])
            p.inc(inst, "dvS")


def emit_xload(p, b):
    @p.block.sync
    def _(sy):
        p.wait(sy, "peS"); p.wait(sy, "dvS"); p.wait(sy, "acS"); p.wait(sy, "ioS")
        src = p.d["x"][b * S:(b + 1) * S, :].rearrange("(t p) c -> p t c", p=128)
        dst = p.c["xb"][:].rearrange("p (t c) -> p t c", c=1024)
        inst = sy.dma_start(dst, src)
        p.inc(inst, "ioS", 16)


def emit_xstore(p, b):
    @p.block.sync
    def _(sy):
        p.wait(sy, "dvS"); p.wait(sy, "acS"); p.wait(sy, "peS")
        dst = p.d["out"][b * S:(b + 1) * S, :].rearrange("(t p) c -> p t c", p=128)
        srcb = p.c["xb"][:].rearrange("p (t c) -> p t c", c=1024)
        inst = sy.dma_start(dst, srcb)
        p.inc(inst, "ioS", 16)


def emit_ln(p, save_ab=False):
    block, c = p.block, p.c
    io_x = p.cnt["ioS"]

    @block.scalar
    def _(s):
        p.wait(s, "ioS", io_x)
        p.wait(s, "dvS"); p.wait(s, "peS")
        for tt in range(NT):
            inst = s.activation(c["junk"][:], c["xb"][:, tt * 1024:(tt + 1) * 1024],
                                AF.Square, accum_out=c["sq_c"][:, tt:tt + 1])
            p.inc(inst, "acS")
    ac_sq = p.cnt["acS"]

    @block.vector
    def _(v):
        p.wait(v, "ioS", io_x)
        p.wait(v, "peS")
        for tt in range(NT):
            inst = v.tensor_scalar(c["junk"][:], c["xb"][:, tt * 1024:(tt + 1) * 1024],
                                   1.0, None, op0=AluOpType.mult, op1=AluOpType.add,
                                   accum_out=c["sum_c"][:, tt:tt + 1])
            p.inc(inst, "dvS")
        p.wait(v, "dvS")
        inst = v.tensor_scalar(c["mu"][:], c["sum_c"][:], 1.0 / D, None, op0=AluOpType.mult)
        p.inc(inst, "dvS")
        p.wait(v, "acS", ac_sq)
        inst = v.tensor_scalar(c["msq"][:], c["sq_c"][:], 1.0 / D, None, op0=AluOpType.mult)
        p.inc(inst, "dvS")
        p.wait(v, "dvS")
        inst = v.tensor_tensor(c["mu2"][:], c["mu"][:], c["mu"][:], op=AluOpType.mult)
        p.inc(inst, "dvS")
        p.wait(v, "dvS")
        inst = v.tensor_tensor(c["var"][:], c["msq"][:], c["mu2"][:], op=AluOpType.subtract)
        p.inc(inst, "dvS")
    dv_var = p.cnt["dvS"]

    @block.scalar
    def _(s):
        p.wait(s, "dvS", dv_var)
        inst = s.activation(c["lnv"][:], c["var"][:], AF.Ln, bias=c["eps_col"][:, 0:1])
        p.inc(inst, "acS")
        p.wait(s, "acS")
        inst = s.activation(c["rs"][:], c["lnv"][:], AF.Exp, scale=-0.5)
        p.inc(inst, "acS")
    ac_rs = p.cnt["acS"]

    @block.vector
    def _(v):
        p.wait(v, "acS", ac_rs)
        inst = v.scalar_tensor_tensor(c["nmr"][:], c["mu"][:], -1.0, c["rs"][:],
                                      op0=AluOpType.mult, op1=AluOpType.mult)
        p.inc(inst, "dvS")
        if save_ab:
            inst = v.tensor_scalar(c["A_c"][:], c["rs"][:], RSQRT2, 1.0,
                                   op0=AluOpType.mult, op1=AluOpType.add)
            p.inc(inst, "dvS")
            inst = v.scalar_tensor_tensor(c["B_c"][:], c["mu"][:], -RSQRT2, c["rs"][:],
                                          op0=AluOpType.mult, op1=AluOpType.mult)
            p.inc(inst, "dvS")
    dv_nmr = p.cnt["dvS"]

    ac_apply, pe_tr, ac_ev = {}, {}, {}
    for tt in range(NT):
        @block.scalar
        def _(s, tt=tt):
            if tt == 0:
                p.wait(s, "dvS", dv_nmr)
            if tt >= 1:
                p.wait(s, "peS", pe_tr[tt - 1])
            inst = s.activation(c["lntm"][:], c["xb"][:, tt * 1024:(tt + 1) * 1024],
                                AF.Identity, scale=c["rs"][:, tt:tt + 1],
                                bias=c["nmr"][:, tt:tt + 1])
            ac_apply[tt] = p.inc(inst, "acS")

        @block.tensor
        def _(t, tt=tt):
            p.wait(t, "acS", ac_apply[tt])
            if tt >= 1:
                p.wait(t, "acS", ac_ev[tt - 1])
            for dk in range(DK):
                inst = t.transpose(c["pC"][:, dk * 128:(dk + 1) * 128],
                                   c["lntm"][:, dk * 128:(dk + 1) * 128], c["ident"][:])
            pe_tr[tt] = p.inc(inst, "peS")

        @block.scalar
        def _(s, tt=tt):
            p.wait(s, "peS", pe_tr[tt])
            out = c["lnf"][:].rearrange("p (dk q) -> p dk q", dk=DK)[:, :, tt * 128:(tt + 1) * 128]
            inst = s.activation(out, c["pC"][:], AF.Copy)
            ac_ev[tt] = p.inc(inst, "acS")
    p.lnf_ready = p.cnt["ioS"]


def emit_proj_fm(p, dram_name, ot_list, out_sb, bias_fm=None, bias_cols=None, scale=1.0,
                 interleave=None):
    block, c = p.block, p.c
    wmark, pe_done, ac_done = {}, {}, {}
    for i, ot in enumerate(ot_list):
        @block.gpsimd
        def _(g, i=i, ot=ot):
            if i < 2:
                p.wait(g, "peS")
            else:
                p.wait(g, "peS", pe_done[i - 2])
            src = p.d[dram_name][ot].rearrange("p dk c -> p (dk c)")
            inst = g.dma_start(c["wbB0" if i % 2 == 0 else "wbB1"][:, 0:DK * 128], src)
            wmark[i] = p.inc(inst, "wS", 16)

        @block.tensor
        def _(t, i=i):
            p.wait(t, "wS", wmark[i])
            if i == 0:
                p.wait(t, "acS"); p.wait(t, "dvS")
                p.wait(t, "ioS", p.lnf_ready)
            if i >= 1:
                p.wait(t, "acS", ac_done[i - 1])
            wb = c["wbB0" if i % 2 == 0 else "wbB1"]
            inst = None
            for dk in range(DK):
                for tc in range(4):
                    inst = t.matmul(c["pAB"][:][:, tc * 512:(tc + 1) * 512],
                                    wb[:, dk * 128:(dk + 1) * 128],
                                    lnf_strip(p, dk, tc * 512, (tc + 1) * 512),
                                    start=(dk == 0), stop=(dk == DK - 1))
            pe_done[i] = p.inc(inst, "peS")

        @block.scalar
        def _(s, i=i):
            p.wait(s, "peS", pe_done[i])
            kw = {}
            fn = AF.Copy
            if bias_fm is not None:
                kw["bias"] = bias_fm[:, bias_cols[i]:bias_cols[i] + 1]
                fn = AF.Identity
            inst = s.activation(out_sb[:, i * 2048:(i + 1) * 2048], c["pAB"][:], fn,
                                scale=scale, **kw)
            ac_done[i] = p.inc(inst, "acS")
        if interleave is not None:
            interleave()
    return ac_done


def emit_vtm(p, wv_name, bv_name, voff, interleave=None):
    block, c = p.block, p.c
    vc = 520

    @block.gpsimd
    def _(g):
        p.wait(g, "peS")
        src = p.d[wv_name].rearrange("(dk p) c -> p dk c", p=128)[:, :, voff:voff + vc]
        dst = c["wbA0"][:, 0:DK * vc].rearrange("p (dk c) -> p dk c", dk=DK)
        inst = g.dma_start(dst, src)
        p.inc(inst, "wS", 16)
    wm = p.cnt["wS"]

    @block.gpsimd
    def _(g):
        inst = g.dma_start(c["rowb"][:, 0:vc], p.d[bv_name][:, voff:voff + vc])
        p.inc(inst, "wS", 16)
    iom = p.cnt["wS"]

    pe_done, ac_done = {}, {}
    for tt in range(NT):
        @block.tensor
        def _(t, tt=tt):
            if tt == 0:
                p.wait(t, "wS", max(wm, iom))
                p.wait(t, "acS")
                p.wait(t, "ioS", p.lnf_ready)
            if tt >= 2:
                p.wait(t, "acS", ac_done[tt - 2])
            ps = c["pA"] if tt % 2 == 0 else c["pB"]
            t.matmul(ps[:, 0:512], c["ones_bf"][:, 0:128], c["rowb"][0:1, 0:512],
                     start=True, stop=True)
            t.matmul(ps[:, 512:vc], c["ones_bf"][:, 0:128], c["rowb"][0:1, 512:vc],
                     start=True, stop=True)
            inst = None
            for dk in range(DK):
                t.matmul(ps[:, 0:512], lnf_strip(p, dk, tt * 128, (tt + 1) * 128),
                         c["wbA0"][:, dk * vc:dk * vc + 512],
                         start=False, stop=(dk == DK - 1))
                inst = t.matmul(ps[:, 512:vc], lnf_strip(p, dk, tt * 128, (tt + 1) * 128),
                                c["wbA0"][:, dk * vc + 512:(dk + 1) * vc],
                                start=False, stop=(dk == DK - 1))
            pe_done[tt] = p.inc(inst, "peS")

        @block.scalar
        def _(s, tt=tt):
            p.wait(s, "peS", pe_done[tt])
            ps = c["pA"] if tt % 2 == 0 else c["pB"]
            inst = s.activation(c["vtm"][:, tt * 520:tt * 520 + vc], ps[:, 0:vc], AF.Copy)
            ac_done[tt] = p.inc(inst, "acS")
        if interleave is not None:
            interleave()


class PassA:

    def __init__(self, p, causal):
        self.p = p
        self.causal = causal
        self.units = [(h, qt) for h in range(8) for qt in range(NT)]
        self.idx = 0
        self.kmark = {}
        self.dv_done = {}
        self.pe_done = {}

    def set_ready(self, hp, ac_mark):
        self.kmark[hp] = ac_mark

    def avail(self):
        n = 0
        for (h, qt) in self.units[self.idx:]:
            if h // 2 in self.kmark:
                n += 1
            else:
                break
        return n

    def pump(self, n):
        p, block, c = self.p, self.p.block, self.p.c
        n = min(n, self.avail())
        for _ in range(n):
            h, qt = self.units[self.idx]
            k = self.idx
            self.idx += 1
            hp, hb = h // 2, (h % 2) * 64
            lim = 2048 if not self.causal else (qt + 1) * 128
            causal = self.causal

            @block.tensor
            def _(t, h=h, qt=qt, hp=hp, hb=hb, lim=lim, k=k):
                if qt == 0:
                    p.wait(t, "acS", self.kmark[hp])
                if k >= 1:
                    p.wait(t, "dvS", self.dv_done[k - 1])
                lhs = c["qf"][hb:hb + 64, hp * 2048 + qt * 128: hp * 2048 + (qt + 1) * 128]
                inst = None
                for sc in range(0, lim, 512):
                    w = min(512, lim - sc)
                    inst = t.matmul(c["pCD"][:][:, sc:sc + w], lhs,
                                    c["kf"][hb:hb + 64, hp * 2048 + sc: hp * 2048 + sc + w],
                                    start=True, stop=True)
                self.pe_done[k] = p.inc(inst, "peS")

            @block.vector
            def _(v, lim=lim, k=k, h=h, qt=qt):
                p.wait(v, "peS", self.pe_done[k])
                inst = v.tensor_reduce(c["M_all"][:, h * NT + qt:h * NT + qt + 1],
                                       c["pCD"][:][:, 0:lim], axis=AX.X, op=AluOpType.max)
                self.dv_done[k] = p.inc(inst, "dvS")

    def finish(self):
        p, block, c = self.p, self.p.block, self.p.c
        self.pump(len(self.units))
        assert self.idx == len(self.units)
        dvm = p.cnt["dvS"]

        @block.tensor
        def _(t):
            p.wait(t, "dvS", dvm)
            inst = t.transpose(c["pC"][:, 0:128], c["M_all"][:], c["ident"][:])
            p.inc(inst, "peS")
        pet = p.cnt["peS"]

        @block.scalar
        def _(s):
            p.wait(s, "peS", pet)
            inst = s.activation(c["mts"][:], c["pC"][:, 0:128], AF.Copy)
            p.inc(inst, "acS")
        act = p.cnt["acS"]

        @block.sync
        def _(sy):
            p.wait(sy, "acS", act)
            inst = sy.dma_start(p.d["mbounce"][:], c["mts"][:])
            p.inc(inst, "ioS", 16)


def emit_attn_phaseC(p, qh, causal):
    block, c = p.block, p.c
    iom = p.cnt["ioS"]

    def make_norm(pe_last, hp, hb):
        def norm():
            for half in range(2):
                @block.vector
                def _(v, half=half):
                    if half == 0:
                        p.wait(v, "peS", pe_last)
                    p.wait(v, "acS")
                    p.wait(v, "dvS")
                    with p.nc.allow_low_precision(reason="bf16 rinv row for rank-1 bcast"):
                        inst = v.reciprocal(c["rrow"][:],
                                            c["pD"][64:65, half * 512:(half + 1) * 512])
                    p.inc(inst, "dvS")
                dv_r = p.cnt["dvS"]

                @block.tensor
                def _(t, dv_r=dv_r, half=half):
                    p.wait(t, "dvS", dv_r)
                    p.wait(t, "acS")
                    inst = t.matmul(c["pB"][0:64, half * 512:(half + 1) * 512],
                                    c["ones_bf"][:, 0:64], c["rrow"][:],
                                    start=True, stop=True)
                    p.inc(inst, "peS")
                pe_rb = p.cnt["peS"]

                @block.scalar
                def _(s, pe_rb=pe_rb, half=half):
                    p.wait(s, "peS", pe_rb)
                    inst = s.activation(c["rb"][:], c["pB"][0:64, half * 512:(half + 1) * 512],
                                        AF.Copy)
                    p.inc(inst, "acS")
                ac_rb = p.cnt["acS"]

                @block.vector
                def _(v, ac_rb=ac_rb, half=half):
                    p.wait(v, "acS", ac_rb)
                    inst = v.tensor_tensor(
                        c["of"][hb:hb + 64, hp * 1024 + half * 512: hp * 1024 + (half + 1) * 512],
                        c["pD"][0:64, half * 512:(half + 1) * 512], c["rb"][:],
                        op=AluOpType.mult)
                    p.inc(inst, "dvS")
        return norm

    pending_norm = None
    for h in range(8):
        hp, hb = h // 2, (h % 2) * 64
        pt_hi = NT if not causal else (qh + 1) * 8
        ac_head0 = p.cnt["acS"]
        ac_exp = {}
        pe_pv = {}
        pe_s2 = {}

        @block.sync
        def _(sy, h=h):
            p.wait(sy, "ioS", iom)
            if h > 0 or qh > 0:
                p.wait(sy, "peS")
            src = p.d["mbounce"][h * 16:(h + 1) * 16, :].rearrange("a b -> (a b)").unsqueeze(0)
            inst = sy.dma_start(c["negmax"][:], src)
            p.inc(inst, "ioS", 16)
        io_h = p.cnt["ioS"]

        for ptp in range(0, pt_hi, 2):
            pts = [q for q in (ptp, ptp + 1) if q < pt_hi]
            for pt in pts:
                sps = c["pC"] if pt % 2 == 0 else c["pA"]
                qc_min = 0 if not causal else max(0, (pt * 128 - qh * 1024 - 512) // 512 + 1)

                @block.tensor
                def _(t, h=h, pt=pt, hp=hp, hb=hb, qc_min=qc_min, sps=sps):
                    if pt == 0:
                        p.wait(t, "ioS", io_h)
                        p.wait(t, "acS", ac_head0)
                    if pt >= 2:
                        p.wait(t, "acS", ac_exp[pt - 2])
                    inst = None
                    for qc in range(qc_min, 2):
                        ps = sps[:, qc * 512:(qc + 1) * 512]
                        t.matmul(ps, c["negones_bf"][:, 0:128],
                                 c["negmax"][0:1, qh * 1024 + qc * 512: qh * 1024 + (qc + 1) * 512],
                                 start=True, stop=True)
                        inst = t.matmul(ps,
                                        c["kf"][hb:hb + 64,
                                                hp * 2048 + pt * 128:hp * 2048 + (pt + 1) * 128],
                                        c["qf"][hb:hb + 64,
                                                hp * 2048 + qh * 1024 + qc * 512:
                                                hp * 2048 + qh * 1024 + (qc + 1) * 512],
                                        start=False, stop=True)
                    pe_s2[pt] = p.inc(inst, "peS")

            if ptp == 0 and pending_norm is not None:
                pending_norm()
                pending_norm = None

            @block.scalar
            def _(s, pts=tuple(pts), qh=qh, causal=causal):
                for pt in pts:
                    ptile = c["pt0"] if pt % 2 == 0 else c["pt1"]
                    sps = c["pC"] if pt % 2 == 0 else c["pA"]
                    qc_min = 0 if not causal else max(0, (pt * 128 - qh * 1024 - 512) // 512 + 1)
                    p.wait(s, "peS", pe_s2[pt])
                    if pt >= 2:
                        p.wait(s, "peS", pe_pv[pt - 2])
                    inst = s.activation(ptile[:, qc_min * 512:1024],
                                        sps[:, qc_min * 512:1024], AF.Exp)
                    ac_exp[pt] = p.inc(inst, "acS")

            dv_mask = {}
            for pt in pts:
                qd = pt * 128 - qh * 1024 if causal else -1000000
                if causal and 0 <= qd < 1024:
                    ptile = c["pt0"] if pt % 2 == 0 else c["pt1"]

                    @block.vector
                    def _(v, pt=pt, qd=qd, ptile=ptile):
                        p.wait(v, "acS", ac_exp[pt])
                        lo = (qd // 512) * 512
                        if qd > lo:
                            inst = v.memset(ptile[:, lo:qd], 0.0)
                            p.inc(inst, "dvS")
                            p.wait(v, "dvS")
                        inst = v.tensor_tensor(ptile[:, qd:qd + 128], ptile[:, qd:qd + 128],
                                               c["maskT"][:], op=AluOpType.mult)
                        p.inc(inst, "dvS")
                    dv_mask[pt] = p.cnt["dvS"]

            for pt in pts:
                ptile = c["pt0"] if pt % 2 == 0 else c["pt1"]
                qc_min = 0 if not causal else max(0, (pt * 128 - qh * 1024 - 512) // 512 + 1)

                @block.tensor
                def _(t, h=h, pt=pt, qc_min=qc_min, ptile=ptile, pt_hi=pt_hi):
                    p.wait(t, "acS", ac_exp[pt])
                    if pt == 0:
                        p.wait(t, "dvS")
                    if pt in dv_mask:
                        p.wait(t, "dvS", dv_mask[pt])
                    inst = None
                    for qc in range(qc_min, 2):
                        last_pt = pt_hi - 1 if not causal else min(pt_hi - 1, qh * 8 + qc * 4 + 3)
                        inst = t.matmul(c["pD"][0:65, qc * 512:(qc + 1) * 512],
                                        c["vtm"][:, pt * 520 + h * 65: pt * 520 + (h + 1) * 65],
                                        ptile[:, qc * 512:(qc + 1) * 512],
                                        start=(pt == 0), stop=(pt == last_pt))
                    pe_pv[pt] = p.inc(inst, "peS")

        pending_norm = make_norm(p.cnt["peS"], hp, hb)
    pending_norm()


def emit_attn_proj(p, qh, slot):
    block, c = p.block, p.c
    dv_of = p.cnt["dvS"]
    pe_done = {}
    ac_free = {}
    dv_free = {}
    if slot == "l0a":
        @block.gpsimd
        def _(g):
            p.wait(g, "peS")
            inst = g.dma_start(c["rowb"][:], p.d["l0_bout"])
            p.inc(inst, "wS", 16)
        io_bout = p.cnt["wS"]
    for ti in range(8):
        tt = qh * 8 + ti
        ps = c["pA"] if ti % 2 == 0 else c["pB"]

        @block.tensor
        def _(t, ti=ti, ps=ps):
            if ti == 0:
                p.wait(t, "dvS", dv_of)
                p.wait(t, "wS")
                p.wait(t, "acS")
                if slot == "l0a":
                    p.wait(t, "wS", io_bout)
            if ti >= 2:
                if slot == "l0a":
                    p.wait(t, "dvS", dv_free[ti - 2])
                    p.wait(t, "acS", ac_free[ti - 2])
                else:
                    p.wait(t, "dvS")
            inst = None
            if slot == "l0a":
                t.matmul(ps[:, 0:512], c["ones_bf"][:, 0:128], c["rowb"][0:1, 0:512],
                         start=True, stop=True)
                t.matmul(ps[:, 512:1024], c["ones_bf"][:, 0:128],
                         c["rowb"][0:1, 512:1024], start=True, stop=True)
            for vk in range(4):
                for dc in range(2):
                    inst = t.matmul(ps[:, dc * 512:(dc + 1) * 512],
                                    c["of"][:, vk * 1024 + ti * 128: vk * 1024 + (ti + 1) * 128],
                                    c["wbA1"][:, vk * 1024 + dc * 512: vk * 1024 + (dc + 1) * 512],
                                    start=(False if slot == "l0a" else vk == 0),
                                    stop=(vk == 3))
            pe_done[ti] = p.inc(inst, "peS")

        if slot == "l1a":
            @block.vector
            def _(v, ti=ti, tt=tt, ps=ps):
                p.wait(v, "peS", pe_done[ti])
                inst = v.tensor_tensor(c["xb"][:, tt * 1024:(tt + 1) * 1024],
                                       c["xb"][:, tt * 1024:(tt + 1) * 1024],
                                       ps[:], op=AluOpType.add)
                p.inc(inst, "dvS")
        else:
            @block.scalar
            def _(s, ti=ti, ps=ps):
                p.wait(s, "peS", pe_done[ti])
                p.wait(s, "dvS")
                inst = s.activation(c["junk"][:], ps[:], AF.Square,
                                    accum_out=c["ycol"][:, 1:2])
                p.inc(inst, "acS")
            ac_sq = p.cnt["acS"]

            @block.vector
            def _(v, ti=ti, ps=ps, ac_sq=ac_sq):
                p.wait(v, "peS", pe_done[ti])
                p.wait(v, "dvS")
                p.wait(v, "acS", ac_sq - 1)
                inst = v.tensor_scalar(c["junk"][:], ps[:], 1.0, None,
                                       op0=AluOpType.mult, op1=AluOpType.add,
                                       accum_out=c["ycol"][:, 0:1])
                dv_free[ti] = p.inc(inst, "dvS")
                p.wait(v, "dvS")
                inst = v.tensor_scalar(c["ycol"][:, 2:3], c["ycol"][:, 0:1], 1.0 / D, None,
                                       op0=AluOpType.mult)
                p.inc(inst, "dvS")
                p.wait(v, "acS", ac_sq)
                p.wait(v, "dvS")
                inst = v.tensor_scalar(c["ycol"][:, 3:4], c["ycol"][:, 1:2], 1.0 / D, None,
                                       op0=AluOpType.mult)
                p.inc(inst, "dvS")
                p.wait(v, "dvS")
                inst = v.tensor_tensor(c["ycol"][:, 4:5], c["ycol"][:, 2:3], c["ycol"][:, 2:3],
                                       op=AluOpType.mult)
                p.inc(inst, "dvS")
                p.wait(v, "dvS")
                inst = v.tensor_tensor(c["ycol"][:, 5:6], c["ycol"][:, 3:4], c["ycol"][:, 4:5],
                                       op=AluOpType.subtract)
                p.inc(inst, "dvS")
            dv_vy = p.cnt["dvS"]

            @block.scalar
            def _(s, dv_vy=dv_vy):
                p.wait(s, "dvS", dv_vy)
                inst = s.activation(c["ycol"][:, 6:7], c["ycol"][:, 5:6], AF.Ln,
                                    bias=c["eps_col"][:, 0:1])
                p.inc(inst, "acS")
                p.wait(s, "acS")
                inst = s.activation(c["ycol"][:, 7:8], c["ycol"][:, 6:7], AF.Exp, scale=-0.5)
                p.inc(inst, "acS")
            ac_ry = p.cnt["acS"]

            @block.vector
            def _(v, ac_ry=ac_ry):
                p.wait(v, "acS", ac_ry)
                inst = v.scalar_tensor_tensor(c["ycol"][:, 4:5], c["ycol"][:, 2:3], -1.0,
                                              c["ycol"][:, 7:8], op0=AluOpType.mult,
                                              op1=AluOpType.mult)
                p.inc(inst, "dvS")
            dv_nmy = p.cnt["dvS"]

            @block.scalar
            def _(s, ps=ps, dv_nmy=dv_nmy, tt=tt, ti=ti):
                p.wait(s, "dvS", dv_nmy)
                inst = s.activation(c["lntm"][:], ps[:], AF.Identity,
                                    scale=c["ycol"][:, 7:8], bias=c["ycol"][:, 4:5])
                ac_free[ti] = p.inc(inst, "acS")
                p.wait(s, "acS")
                inst = s.activation(c["xb"][:, tt * 1024:(tt + 1) * 1024],
                                    c["xb"][:, tt * 1024:(tt + 1) * 1024], AF.Identity,
                                    scale=c["A_c"][:, tt:tt + 1], bias=c["B_c"][:, tt:tt + 1])
                p.inc(inst, "acS")
            ac_t = p.cnt["acS"]

            @block.vector
            def _(v, ac_t=ac_t, tt=tt):
                p.wait(v, "acS", ac_t)
                inst = v.tensor_tensor(c["lntm"][:], c["lntm"][:], c["gamma_bc"],
                                       op=AluOpType.mult)
                p.inc(inst, "dvS")
                p.wait(v, "dvS")
                inst = v.tensor_tensor(c["lntm"][:], c["lntm"][:], c["beta_bc"],
                                       op=AluOpType.add)
                p.inc(inst, "dvS")
                p.wait(v, "dvS")
                inst = v.tensor_tensor(c["xb"][:, tt * 1024:(tt + 1) * 1024],
                                       c["xb"][:, tt * 1024:(tt + 1) * 1024],
                                       c["lntm"][:], op=AluOpType.add)
                p.inc(inst, "dvS")


def emit_gb_build(p):
    block, c = p.block, p.c
    for which, dname, ps in (("g", "l0_gamma", "pA"), ("b", "l0_beta", "pB")):
        @block.gpsimd
        def _(g, dname=dname):
            p.wait(g, "peS")
            inst = g.dma_start(c["rowb"][:], p.d[dname])
            p.inc(inst, "wS", 16)
        iom = p.cnt["wS"]

        @block.tensor
        def _(t, ps=ps, iom=iom):
            p.wait(t, "wS", iom)
            p.wait(t, "dvS"); p.wait(t, "acS")
            inst = None
            for half in range(2):
                inst = t.matmul(c[ps][:, half * 512:(half + 1) * 512], c["ones_bf"][:, 0:128],
                                c["rowb"][0:1, half * 512:(half + 1) * 512],
                                start=True, stop=True)
            p.inc(inst, "peS")
    pe1 = p.cnt["peS"]

    @block.scalar
    def _(s):
        p.wait(s, "peS", pe1)
        s.activation(c["gamma_bc"], c["pA"][:], AF.Copy, scale=RSQRT2)
        inst = s.activation(c["beta_bc"], c["pB"][:], AF.Copy, scale=RSQRT2)
        p.inc(inst, "acS")


def emit_l0a(p):
    c = p.c
    emit_ln(p, save_ab=True)
    pa = PassA(p, causal=False)
    emit_proj_fm(p, "l0_wq", [0, 1, 2, 3], c["qf"], bias_fm=c["bq_fm"],
                 bias_cols=[0, 1, 2, 3], scale=QK_SCALE)

    def hook():
        pa.pump(6)
    km = emit_proj_fm(p, "l0_wk", [0, 1, 2, 3], c["kf"], bias_fm=c["bk_fm"],
                      bias_cols=[0, 1, 2, 3], scale=1.0,
                      interleave=None)
    for hp in range(4):
        pa.set_ready(hp, km[hp])
    emit_vtm(p, "l0_wv", "l0_bv", 0, interleave=lambda: pa.pump(6))

    @p.block.gpsimd
    def _(g):
        p.wait(g, "peS")
        src = p.d["l0_wout"].rearrange("(vk p) c -> p vk c", p=128)
        dst = c["wbA1"][:, 0:4096].rearrange("p (vk c) -> p vk c", c=1024)
        inst = g.dma_start(dst, src)
        p.inc(inst, "wS", 16)
    pa.finish()
    emit_gb_build(p)
    for qh in range(2):
        emit_attn_phaseC(p, qh, causal=False)
        emit_attn_proj(p, qh, "l0a")


def emit_l1a(p):
    c = p.c
    emit_ln(p, save_ab=False)
    for grp in range(2):
        pa = PassA(p, causal=True)
        emit_proj_fm(p, "l1_wq", [4 * grp + i for i in range(4)], c["qf"], scale=QK_SCALE)
        km = emit_proj_fm(p, "l1_wk", [4 * grp + i for i in range(4)], c["kf"], scale=1.0)
        for hp in range(4):
            pa.set_ready(hp, km[hp])
        emit_vtm(p, "l1_wv", "l1_bv", 520 * grp, interleave=lambda: pa.pump(6))

        @p.block.gpsimd
        def _(g, grp=grp):
            p.wait(g, "peS")
            src = p.d["l1_wo"][512 * grp:512 * (grp + 1), :].rearrange(
                "(vk p) c -> p vk c", p=128)
            dst = c["wbA1"][:, 0:4096].rearrange("p (vk c) -> p vk c", c=1024)
            inst = g.dma_start(dst, src)
            p.inc(inst, "wS", 16)
        pa.finish()
        for qh in range(2):
            emit_attn_phaseC(p, qh, causal=True)
            emit_attn_proj(p, qh, "l1a")


def emit_l0f(p):
    c = p.c
    block = p.block
    emit_ln(p, save_ab=False)

    for th in range(2):
        wm = {}
        pe_kt = {}
        for hc in range(6):
            @block.gpsimd
            def _(g, hc=hc, th=th):
                if hc >= 1:
                    p.wait(g, "peS", pe_kt[hc * 4 - 1])
                elif th == 0:
                    p.wait(g, "peS")
                else:
                    p.wait(g, "peS")
                inst = g.dma_start(c["wbA0"][:, 0:4096],
                                   p.d["l0_wg"][hc].rearrange("p dk c -> p (dk c)"))
                p.inc(inst, "wS", 16)
                inst = g.dma_start(c["wbA1"][:, 0:4096],
                                   p.d["l0_wu"][hc].rearrange("p dk c -> p (dk c)"))
                wm[hc] = p.inc(inst, "wS", 16)

            for ht in range(4):
                kt = hc * 4 + ht

                @block.tensor
                def _(t, hc=hc, ht=ht, th=th, kt=kt):
                    if ht == 0:
                        p.wait(t, "wS", wm[hc])
                    if (th, hc, ht) == (0, 0, 0):
                        p.wait(t, "ioS", p.lnf_ready)
                    p.wait(t, "dvS"); p.wait(t, "acS")
                    inst = None
                    for dk in range(DK):
                        for tc in range(2):
                            t.matmul(c["pA"][:, tc * 512:(tc + 1) * 512],
                                     c["wbA0"][:, dk * 512 + ht * 128: dk * 512 + (ht + 1) * 128],
                                     lnf_strip(p, dk, th * 1024 + tc * 512,
                                               th * 1024 + (tc + 1) * 512),
                                     start=(dk == 0), stop=(dk == DK - 1))
                            inst = t.matmul(c["pB"][:, tc * 512:(tc + 1) * 512],
                                            c["wbA1"][:, dk * 512 + ht * 128: dk * 512 + (ht + 1) * 128],
                                            lnf_strip(p, dk, th * 1024 + tc * 512,
                                                      th * 1024 + (tc + 1) * 512),
                                            start=(dk == 0), stop=(dk == DK - 1))
                    pe_kt[kt] = p.inc(inst, "peS")
                pe_m = p.cnt["peS"]

                @block.scalar
                def _(s, pe_m=pe_m):
                    p.wait(s, "peS", pe_m)
                    p.wait(s, "dvS")
                    inst = s.activation(c["lntm"][:], c["pA"][:], AF.Silu, scale=DEQ)
                    p.inc(inst, "acS")
                ac_m = p.cnt["acS"]

                @block.vector
                def _(v, kt=kt, ac_m=ac_m):
                    p.wait(v, "acS", ac_m)
                    inst = v.scalar_tensor_tensor(c["hfm"][:, kt * 1024:(kt + 1) * 1024],
                                                  c["pB"][:], DEQ, c["lntm"][:],
                                                  op0=AluOpType.mult, op1=AluOpType.mult)
                    p.inc(inst, "dvS")
        dv_up = p.cnt["dvS"]

        pe_dkt = {}
        for dc in range(2):
            pe_dn = None
            for ktg in range(6):
                @block.gpsimd
                def _(g, dc=dc, ktg=ktg):
                    k = dc * 6 + ktg
                    if k >= 2:
                        p.wait(g, "peS", pe_dkt[k - 2])
                    elif k == 0:
                        p.wait(g, "peS")
                    inst = g.dma_start(c["wbB0" if ktg % 2 == 0 else "wbB1"][:],
                                       p.d["l0_wd"][dc, ktg].rearrange("p j c -> p (j c)"))
                    wm[(dc, ktg)] = p.inc(inst, "wS", 16)

                @block.tensor
                def _(t, dc=dc, ktg=ktg, dv_up=dv_up):
                    p.wait(t, "wS", wm[(dc, ktg)])
                    if ktg == 0:
                        p.wait(t, "dvS")
                        p.wait(t, "acS")
                    wb = c["wbB0" if ktg % 2 == 0 else "wbB1"]
                    inst = None
                    for j in range(4):
                        kt = ktg * 4 + j
                        for tt in range(8):
                            pst = c[PSUMS[tt // 2]]
                            inst = t.matmul(pst[:, (tt % 2) * 512:(tt % 2 + 1) * 512],
                                            c["hfm"][:, kt * 1024 + tt * 128: kt * 1024 + (tt + 1) * 128],
                                            wb[:, j * 512:(j + 1) * 512],
                                            start=(kt == 0), stop=(kt == 23))
                    pe_dkt[dc * 6 + ktg] = p.inc(inst, "peS")
            pe_dn = p.cnt["peS"]

            for tt in range(8):
                @block.vector
                def _(v, tt=tt, dc=dc, th=th, pe_dn=pe_dn):
                    p.wait(v, "peS", pe_dn)
                    pst = c[PSUMS[tt // 2]]
                    col = (th * 8 + tt) * 1024 + dc * 512
                    inst = v.scalar_tensor_tensor(c["xb"][:, col:col + 512],
                                                  pst[:, (tt % 2) * 512:(tt % 2 + 1) * 512],
                                                  DEQ, c["xb"][:, col:col + 512],
                                                  op0=AluOpType.mult, op1=AluOpType.add)
                    p.inc(inst, "dvS")


def emit_l1f(p):
    c = p.c
    block = p.block
    emit_ln(p, save_ab=False)

    @block.gpsimd
    def _(g):
        p.wait(g, "acS")
        src = p.d["l1_wgr"].rearrange("(dk p) c -> p dk c", p=128)
        dst = c["wgr_sb"][:].rearrange("p (dk c) -> p dk c", c=8)
        inst = g.dma_start(dst, src)
        p.inc(inst, "wS", 16)
    wgr_m = p.cnt["wS"]
    for tt in range(NT):
        @block.tensor
        def _(t, tt=tt):
            if tt == 0:
                p.wait(t, "wS", wgr_m)
                p.wait(t, "ioS", p.lnf_ready)
                p.wait(t, "acS")
            else:
                p.wait(t, "acS")
            t.matmul(c["pC"][:, 0:8], c["ones_bf"][:, 0:128], c["bgbb"][:],
                     start=True, stop=True)
            inst = None
            for dk in range(DK):
                inst = t.matmul(c["pC"][:, 0:8], lnf_strip(p, dk, tt * 128, (tt + 1) * 128),
                                c["wgr_sb"][:, dk * 8:(dk + 1) * 8],
                                start=False, stop=(dk == DK - 1))
            p.inc(inst, "peS")
        pe_r = p.cnt["peS"]

        @block.scalar
        def _(s, pe_r=pe_r):
            p.wait(s, "peS", pe_r)
            p.wait(s, "dvS")
            inst = s.activation(c["rtmp"][:], c["pC"][:, 0:8], AF.Sigmoid)
            p.inc(inst, "acS")
        ac_r = p.cnt["acS"]

        @block.vector
        def _(v, tt=tt, ac_r=ac_r):
            p.wait(v, "acS", ac_r)
            inst = v.tensor_reduce(c["rcol"][:, 0:1], c["rtmp"][:], axis=AX.X,
                                   op=AluOpType.max)
            p.inc(inst, "dvS")
            p.wait(v, "dvS")
            inst = v.tensor_scalar(c["rtm2"][:], c["rtmp"][:], c["rcol"][:, 0:1], None,
                                   op0=AluOpType.is_ge, op1=AluOpType.add,
                                   accum_out=c["rcol"][:, 2:3])
            p.inc(inst, "dvS")
            p.wait(v, "dvS")
            inst = v.scalar_tensor_tensor(c["rtm3"][:], c["rtm2"][:], -1e30, c["rtmp"][:],
                                          op0=AluOpType.mult, op1=AluOpType.add)
            p.inc(inst, "dvS")
            p.wait(v, "dvS")
            inst = v.tensor_reduce(c["rcol"][:, 1:2], c["rtm3"][:], axis=AX.X,
                                   op=AluOpType.max)
            p.inc(inst, "dvS")
            p.wait(v, "dvS")
            inst = v.tensor_scalar(c["rcol"][:, 3:4], c["rcol"][:, 2:3], 2.0, None,
                                   op0=AluOpType.is_ge)
            p.inc(inst, "dvS")
            p.wait(v, "dvS")
            inst = v.tensor_tensor(c["rcol"][:, 2:3], c["rcol"][:, 0:1], c["rcol"][:, 1:2],
                                   op=AluOpType.subtract)
            p.inc(inst, "dvS")
            p.wait(v, "dvS")
            inst = v.scalar_tensor_tensor(c["rcol"][:, 0:1], c["rcol"][:, 2:3],
                                          c["rcol"][:, 3:4], c["rcol"][:, 1:2],
                                          op0=AluOpType.mult, op1=AluOpType.add)
            p.inc(inst, "dvS")
            p.wait(v, "dvS")
            inst = v.tensor_scalar(c["rtm3"][:], c["rtmp"][:], c["rcol"][:, 0:1], None,
                                   op0=AluOpType.is_ge)
            p.inc(inst, "dvS")
            p.wait(v, "dvS")
            inst = v.tensor_tensor(c["w_sb"][:, tt * 8:(tt + 1) * 8], c["rtm3"][:],
                                   c["rtmp"][:], op=AluOpType.mult)
            p.inc(inst, "dvS")

    for th in range(2):
        for e in range(9):
            wm1, ac_g, pe_u = {}, {}, {}

            @block.gpsimd
            def _(g, e=e):
                p.wait(g, "peS")
                inst = g.dma_start(c["rowb"][:], p.d["l1_b2"][:, e * 1024:(e + 1) * 1024])
                p.inc(inst, "wS", 16)
            io_b2 = p.cnt["wS"]
            for hc in range(8):
                @block.gpsimd
                def _(g, e=e, hc=hc):
                    if hc >= 2:
                        p.wait(g, "peS", pe_u[(hc - 2, 3)])
                    elif hc == 0:
                        p.wait(g, "peS")
                    inst = g.dma_start(c["wbA0" if hc % 2 == 0 else "wbA1"][:, 0:4096],
                                       p.d["l1_w1"][e, hc].rearrange("p dk c -> p (dk c)"))
                    wm1[hc] = p.inc(inst, "wS", 16)

                for ht in range(4):
                    kt = hc * 4 + ht

                    @block.tensor
                    def _(t, hc=hc, ht=ht, th=th, kt=kt):
                        if ht == 0:
                            p.wait(t, "wS", wm1[hc])
                        if kt >= 4:
                            p.wait(t, "acS", ac_g[kt - 4])
                        elif kt == 0:
                            p.wait(t, "dvS"); p.wait(t, "acS")
                        wb = c["wbA0" if hc % 2 == 0 else "wbA1"]
                        ps = c[PSUMS[kt % 4]]
                        inst = None
                        for dk in range(DK):
                            for tc in range(2):
                                inst = t.matmul(ps[:, tc * 512:(tc + 1) * 512],
                                                wb[:, dk * 512 + ht * 128: dk * 512 + (ht + 1) * 128],
                                                lnf_strip(p, dk, th * 1024 + tc * 512,
                                                          th * 1024 + (tc + 1) * 512),
                                                start=(dk == 0), stop=(dk == DK - 1))
                        pe_u[(hc, ht)] = p.inc(inst, "peS")

                @block.scalar
                def _(s, e=e, hc=hc):
                    for ht in range(4):
                        kt = hc * 4 + ht
                        p.wait(s, "peS", pe_u[(hc, ht)])
                        ps = c[PSUMS[kt % 4]]
                        inst = s.activation(c["hfm"][:, kt * 1024:(kt + 1) * 1024], ps[:],
                                            AF.Gelu,
                                            bias=c["b1_sb"][:, e * 32 + kt:e * 32 + kt + 1])
                        ac_g[kt] = p.inc(inst, "acS")
            ac_up = p.cnt["acS"]

            wm2, pe_d = {}, {}

            @block.gpsimd
            def _(g, e=e):
                p.wait(g, "peS")
                inst = g.dma_start(c["rowb"][:], p.d["l1_b2"][:, e * 1024:(e + 1) * 1024])
                p.inc(inst, "wS", 16)
            io_b2 = p.cnt["wS"]
            for dc in range(2):
                @block.tensor
                def _(t, e=e, dc=dc, ac_up=ac_up, io_b2=io_b2):
                    p.wait(t, "acS", ac_up)
                    p.wait(t, "dvS")
                    p.wait(t, "wS", io_b2)
                    inst = None
                    for tt in range(8):
                        pst = c[PSUMS[tt // 2]]
                        inst = t.matmul(pst[:, (tt % 2) * 512:(tt % 2 + 1) * 512],
                                        c["ones_bf"][:, 0:128],
                                        c["rowb"][0:1, dc * 512:(dc + 1) * 512],
                                        start=True, stop=True)
                    p.inc(inst, "peS")

                for ktg in range(8):
                    @block.gpsimd
                    def _(g, e=e, dc=dc, ktg=ktg):
                        k = dc * 8 + ktg
                        if k >= 2:
                            p.wait(g, "peS", pe_d[k - 2])
                        inst = g.dma_start(c["wbB0" if ktg % 2 == 0 else "wbB1"][:],
                                           p.d["l1_w2"][e, dc, ktg].rearrange("p j c -> p (j c)"))
                        wm2[(dc, ktg)] = p.inc(inst, "wS", 16)

                    @block.tensor
                    def _(t, dc=dc, ktg=ktg):
                        p.wait(t, "wS", wm2[(dc, ktg)])
                        p.wait(t, "acS", ac_g[min(ktg * 4 + 3, 31)])
                        wb = c["wbB0" if ktg % 2 == 0 else "wbB1"]
                        inst = None
                        for j in range(4):
                            kt = ktg * 4 + j
                            for tt in range(8):
                                pst = c[PSUMS[tt // 2]]
                                inst = t.matmul(pst[:, (tt % 2) * 512:(tt % 2 + 1) * 512],
                                                c["hfm"][:, kt * 1024 + tt * 128:
                                                          kt * 1024 + (tt + 1) * 128],
                                                wb[:, j * 512:(j + 1) * 512],
                                                start=False, stop=(kt == 31))
                        pe_d[dc * 8 + ktg] = p.inc(inst, "peS")
                pe_dn = p.cnt["peS"]

                for tt in range(8):
                    @block.vector
                    def _(v, e=e, tt=tt, dc=dc, th=th, pe_dn=pe_dn):
                        p.wait(v, "peS", pe_dn)
                        pst = c[PSUMS[tt // 2]]
                        tg = th * 8 + tt
                        col = tg * 1024 + dc * 512
                        if e < 8:
                            inst = v.scalar_tensor_tensor(
                                c["xb"][:, col:col + 512],
                                pst[:, (tt % 2) * 512:(tt % 2 + 1) * 512],
                                c["w_sb"][:, tg * 8 + e:tg * 8 + e + 1],
                                c["xb"][:, col:col + 512],
                                op0=AluOpType.mult, op1=AluOpType.add)
                        else:
                            inst = v.tensor_tensor(
                                c["xb"][:, col:col + 512],
                                c["xb"][:, col:col + 512],
                                pst[:, (tt % 2) * 512:(tt % 2 + 1) * 512],
                                op=AluOpType.add)
                        p.inc(inst, "dvS")


SLOT_EMITTERS = {"l0a": emit_l0a, "l0f": emit_l0f, "l1a": emit_l1a, "l1f": emit_l1f}


def build_program(slots, n_batches):
    nc = bass.Bass("TRN2", target_bir_lowering=False)
    dram = {}
    dram["x"] = nc.dram_tensor("x", [n_batches * S, D], F32, kind="ExternalInput").ap()
    dram["out"] = nc.dram_tensor("out", [n_batches * S, D], F32, kind="ExternalOutput").ap()
    dram["mbounce"] = nc.dram_tensor("mbounce", [128, 128], BF16, kind="Internal").ap()
    wnames = sorted({w for sl in slots for w in SLOT_WEIGHTS[sl]})
    for w in wnames:
        sh, dt = PREP_SPECS[w]
        dram[w] = nc.dram_tensor(w, list(sh), dt, kind="ExternalInput").ap()
    with ExitStack() as st:
        block = st.enter_context(nc.Block())
        p = P(nc, block, st)
        p.d = dram
        alloc_buffers(p, st)
        emit_consts(p, slots)
        for b in range(n_batches):
            emit_xload(p, b)
            for sl in slots:
                SLOT_EMITTERS[sl](p)
            emit_xstore(p, b)

        @block.sync
        def _(sy):
            p.wait(sy, "ioS")
    return nc, wnames


def kernel(**inputs):
    prep = host_prep(inputs)
    x = np.ascontiguousarray(inputs["L0_attn_x"].astype(np.float32, copy=False)
                             ).reshape(B, S, D)
    groups = [ALL_SLOTS] if MODE == "fused" else [(s,) for s in ALL_SLOTS]
    for slots in groups:
        nc, wnames = build_program(slots, BPC)
        in_maps = []
        for core in range(N_CORES):
            m = {w: prep[w] for w in wnames}
            m["x"] = np.ascontiguousarray(
                x[core * BPC:(core + 1) * BPC].reshape(BPC * S, D))
            in_maps.append(m)
        res = run_bass_kernel_spmd(nc, in_maps, core_ids=list(range(N_CORES)))
        x = np.concatenate(
            [r["out"].reshape(BPC, S, D) for r in res.results], axis=0)
    return np.ascontiguousarray(x.reshape(B, S, D)).astype(np.float32, copy=False)


# version 12
# speedup vs baseline: 1.0891x; 1.0345x over previous
import numpy as np
from contextlib import ExitStack

import concourse.bass as bass
import concourse.mybir as mybir
from concourse.bass_utils import run_bass_kernel_spmd
from concourse.alu_op_type import AluOpType

F32 = mybir.dt.float32
BF16 = mybir.dt.bfloat16
I32 = mybir.dt.int32
AF = mybir.ActivationFunctionType
AX = mybir.AxisListType

B, S, D = 32, 2048, 1024
N_CORES = 8
BPC = B // N_CORES
NT = S // 128
DK = D // 128
RSQRT2 = float(2 ** (-0.5))
LN_EPS = 1e-5
QK_SCALE = 0.125
DEQ = 1.0 / 127.0
MODE = "fused"

ALL_SLOTS = ("l0a", "l0f", "l1a", "l1f")


def _lhs_stream(w_t, n_out_tiles):
    dk = w_t.shape[0] // 128
    a = w_t.reshape(dk, 128, n_out_tiles, 128)
    return np.ascontiguousarray(a.transpose(2, 1, 0, 3))


def _up_stream(w_t, n_hc, w_hc):
    dk = w_t.shape[0] // 128
    a = w_t.reshape(dk, 128, n_hc, w_hc)
    return np.ascontiguousarray(a.transpose(2, 1, 0, 3))


def _down_stream(w_t, n_dc, w_dc, g):
    kt = w_t.shape[0] // 128
    a = w_t.reshape(kt // g, g, 128, n_dc, w_dc)
    return np.ascontiguousarray(a.transpose(3, 0, 2, 1, 4))


def host_prep(inp):
    p = {}
    W_qk = inp["L0_attn_W_qk"]
    wq = np.ascontiguousarray(W_qk.reshape(D, 8, 64, 2)[..., 0].reshape(D, 512))
    wk = np.ascontiguousarray(W_qk.reshape(D, 8, 64, 2)[..., 1].reshape(D, 512))
    b_qk = inp["L0_attn_b_qk"]
    p["l0_bq"] = np.ascontiguousarray(b_qk.reshape(8, 64, 2)[..., 0].reshape(4, 128).T)
    p["l0_bk"] = np.ascontiguousarray(b_qk.reshape(8, 64, 2)[..., 1].reshape(4, 128).T)
    p["l0_wq"] = _lhs_stream(wq, 4)
    p["l0_wk"] = _lhs_stream(wk, 4)
    wv_ext = np.zeros((D, 8, 65), np.float32)
    wv_ext[:, :, :64] = inp["L0_attn_W_v"].reshape(D, 8, 64)
    p["l0_wv"] = np.ascontiguousarray(wv_ext.reshape(D, 520))
    bv_ext = np.zeros((8, 65), np.float32)
    bv_ext[:, :64] = inp["L0_attn_b_v"].reshape(8, 64)
    bv_ext[:, 64] = 1.0
    p["l0_bv"] = np.ascontiguousarray(bv_ext.reshape(1, 520))
    p["l0_wout"] = np.ascontiguousarray(inp["L0_attn_W_out"])
    p["l0_bout"] = np.ascontiguousarray(inp["L0_attn_b_out"].reshape(1, D))
    p["l0_gamma"] = np.ascontiguousarray(inp["L0_attn_ln_gamma"].reshape(1, D))
    p["l0_beta"] = np.ascontiguousarray(inp["L0_attn_ln_beta"].reshape(1, D))
    wg_t = np.ascontiguousarray(inp["L0_ffn_w_gate"].T)
    wu_t = np.ascontiguousarray(inp["L0_ffn_w_up"].T)
    wd_t = np.ascontiguousarray(inp["L0_ffn_w_down"].T)
    p["l0_wg"] = _up_stream(wg_t, 6, 512)
    p["l0_wu"] = _up_stream(wu_t, 6, 512)
    p["l0_wd"] = _down_stream(wd_t, 2, 512, 4)
    wq1 = np.ascontiguousarray(inp["L1_attn_W_Q"].transpose(2, 0, 1).reshape(D, 1024))
    wk1 = np.ascontiguousarray(inp["L1_attn_W_K"].transpose(2, 0, 1).reshape(D, 1024))
    p["l1_wq"] = _lhs_stream(wq1, 8)
    p["l1_wk"] = _lhs_stream(wk1, 8)
    wv1_ext = np.zeros((D, 16, 65), np.float32)
    wv1_ext[:, :, :64] = inp["L1_attn_W_V"].transpose(2, 0, 1).reshape(D, 16, 64)
    p["l1_wv"] = np.ascontiguousarray(wv1_ext.reshape(D, 1040))
    bv1 = np.zeros((16, 65), np.float32)
    bv1[:, 64] = 1.0
    p["l1_bv"] = np.ascontiguousarray(bv1.reshape(1, 1040))
    p["l1_wo"] = np.ascontiguousarray(inp["L1_attn_W_O"].T)
    W1 = np.concatenate([inp["L1_ffn_W1"], inp["L1_ffn_Ws1"][None]], 0)
    W2 = np.concatenate([inp["L1_ffn_W2"], inp["L1_ffn_Ws2"][None]], 0)
    b1 = np.concatenate([inp["L1_ffn_b1"], inp["L1_ffn_bs1"][None]], 0)
    b2 = np.concatenate([inp["L1_ffn_b2"], inp["L1_ffn_bs2"][None]], 0)
    w1t = W1.transpose(0, 2, 1)
    p["l1_w1"] = np.ascontiguousarray(
        w1t.reshape(9, DK, 128, 8, 512).transpose(0, 3, 2, 1, 4))
    w2t = W2.transpose(0, 2, 1)
    p["l1_w2"] = np.ascontiguousarray(
        w2t.reshape(9, 8, 4, 128, 2, 512).transpose(0, 4, 1, 3, 2, 5))
    p["l1_b1"] = np.ascontiguousarray(b1.reshape(9, 32, 128).transpose(0, 2, 1))
    p["l1_b2"] = np.ascontiguousarray(b2.reshape(1, 9 * D))
    p["l1_wgr"] = np.ascontiguousarray(inp["L1_ffn_Wg"].T)
    p["l1_bgb"] = np.ascontiguousarray(
        np.stack([inp["L1_ffn_bg"], inp["L1_ffn_bias"]], 0))
    return p


PREP_SPECS = {
    "l0_bq": ((128, 4), F32), "l0_bk": ((128, 4), F32),
    "l0_wq": ((4, 128, 8, 128), F32), "l0_wk": ((4, 128, 8, 128), F32),
    "l0_wv": ((D, 520), F32), "l0_bv": ((1, 520), F32),
    "l0_wout": ((512, D), F32), "l0_bout": ((1, D), F32),
    "l0_gamma": ((1, D), F32), "l0_beta": ((1, D), F32),
    "l0_wg": ((6, 128, 8, 512), I32), "l0_wu": ((6, 128, 8, 512), I32),
    "l0_wd": ((2, 6, 128, 4, 512), I32),
    "l1_wq": ((8, 128, 8, 128), F32), "l1_wk": ((8, 128, 8, 128), F32),
    "l1_wv": ((D, 1040), F32), "l1_bv": ((1, 1040), F32),
    "l1_wo": ((D, D), F32),
    "l1_w1": ((9, 8, 128, 8, 512), F32), "l1_w2": ((9, 2, 8, 128, 4, 512), F32),
    "l1_b1": ((9, 128, 32), F32), "l1_b2": ((1, 9 * D), F32),
    "l1_wgr": ((D, 8), F32), "l1_bgb": ((2, 8), F32),
}

SLOT_WEIGHTS = {
    "l0a": ["l0_bq", "l0_bk", "l0_wq", "l0_wk", "l0_wv", "l0_bv", "l0_wout",
            "l0_bout", "l0_gamma", "l0_beta"],
    "l0f": ["l0_wg", "l0_wu", "l0_wd"],
    "l1a": ["l1_wq", "l1_wk", "l1_wv", "l1_bv", "l1_wo"],
    "l1f": ["l1_w1", "l1_w2", "l1_b1", "l1_b2", "l1_wgr", "l1_bgb"],
}


class P:
    def __init__(self, nc, block, st):
        self.nc, self.block, self.st = nc, block, st
        self.c = {}
        self.d = {}
        self.cnt = {}
        self.hs = {}
        for n in ("wS", "ioS", "peS", "dvS", "acS", "gpS"):
            self.hs[n] = st.enter_context(nc.semaphore(n))
            self.cnt[n] = 0

    def inc(self, inst, name, n=1):
        inst.then_inc(self.hs[name], n)
        self.cnt[name] += n
        return self.cnt[name]

    def wait(self, eng, name, val=None):
        v = self.cnt[name] if val is None else val
        if v > 0:
            eng.wait_ge(self.hs[name], v)


def lnf_strip(p, dk, lo, hi):
    r = p.c["lnf"][:].rearrange("p (dk q) -> p dk q", dk=DK)
    return r[:, dk, lo:hi]


def alloc_buffers(p, st):
    nc = p.nc
    def T(n, sh, dt):
        p.c[n] = st.enter_context(nc.sbuf_tensor(n, sh, dt))
    def PS(n):
        p.c[n] = st.enter_context(nc.psum_tensor(n, [128, 2048], F32))
    T("xb", [128, NT * 1024], F32)
    T("lnf", [128, DK * 2048], BF16)
    T("blk", [128, 32896], BF16)
    T("wbA0", [128, 4160], BF16)
    T("wbA1", [128, 4160], BF16)
    T("wbB0", [128, 2048], BF16)
    T("wbB1", [128, 2048], BF16)
    T("ptb", [128, 2048], BF16)
    T("pt2", [128, 1024], BF16)
    T("junk", [128, 1024], BF16)
    T("sum_c", [128, NT], F32); T("sq_c", [128, NT], F32)
    T("mu", [128, NT], F32); T("msq", [128, NT], F32); T("mu2", [128, NT], F32)
    T("var", [128, NT], F32); T("lnv", [128, NT], F32); T("rs", [128, NT], F32)
    T("nmr", [128, NT], F32); T("A_c", [128, NT], F32); T("B_c", [128, NT], F32)
    T("ycol", [128, 8], F32)
    T("eps_col", [128, 1], F32)
    T("M_all", [128, 128], F32)
    T("mtmp", [128, 2], F32)
    T("mts", [128, 128], BF16)
    T("negmax", [1, 2048], BF16)
    T("rb", [64, 512], F32)
    T("w_sb", [128, NT * 8], F32)
    T("rtmp", [128, 8], F32)
    T("rtm2", [128, 8], F32)
    T("rtm3", [128, 8], F32)
    T("rcol", [128, 4], F32)
    T("ident", [128, 128], F32)
    T("maskT", [128, 128], BF16)
    T("ones_f", [1, 128], F32)
    T("ones_bf", [1, 128], BF16)
    T("negones_bf", [1, 128], BF16)
    T("rowb", [1, D], BF16)
    T("rrow", [1, 512], BF16)
    T("bgbb", [1, 8], BF16)
    T("bq_fm", [128, 8], F32)
    T("bk_fm", [128, 8], F32)
    T("b1_sb", [128, 9 * 32], F32)
    T("bgb", [1, 8], F32)
    T("wgr_sb", [128, 64], BF16)
    PS("pAB"); PS("pCD")
    p.c["pA"] = p.c["pAB"][:][:, 0:1024]
    p.c["pB"] = p.c["pAB"][:][:, 1024:2048]
    p.c["pC"] = p.c["pCD"][:][:, 0:1024]
    p.c["pD"] = p.c["pCD"][:][:, 1024:2048]
    p.c["pt0"] = p.c["ptb"][:, 0:1024]
    p.c["pt1"] = p.c["ptb"][:, 1024:2048]
    p.c["lntm"] = p.c["ptb"][:].bitcast(F32)
    blk = p.c["blk"]
    p.c["qf"] = blk[:, 0:8192]
    p.c["kf"] = blk[:, 8192:16384]
    p.c["of"] = blk[:, 16384:20480]
    p.c["vtm"] = blk[:, 20480:28800]
    gb = blk[:, 28800:32896]
    p.c["hfm"] = blk[:, 0:32768]
    p.c["gamma_bc"] = gb.bitcast(F32)[:, 0:1024]
    p.c["beta_bc"] = gb.bitcast(F32)[:, 1024:2048]


PSUMS = ("pA", "pB", "pC", "pD")


PRELOAD_FUNCS = {6: AF.Exp, 18: AF.Silu, 2: AF.Sigmoid, 10: AF.Gelu}


def preload_set(p, idx):
    @p.block.scalar
    def _(s):
        s.preload_act_table(PRELOAD_FUNCS[idx])



def emit_consts(p, slots):
    nc, block, c = p.nc, p.block, p.c
    with ExitStack() as st2:
        ic = st2.enter_context(nc.sbuf_tensor("iotac", [128, 128], I32))
        ip_ = st2.enter_context(nc.sbuf_tensor("iotap", [128, 1], I32))
        icf = st2.enter_context(nc.sbuf_tensor("iotacf", [128, 128], F32))
        ipf = st2.enter_context(nc.sbuf_tensor("iotapf", [128, 1], F32))

        @block.gpsimd
        def _(g):
            g.iota(ic[:], pattern=[[1, 128]], base=0, channel_multiplier=0)
            g.iota(ip_[:], pattern=[[0, 1]], base=0, channel_multiplier=1)
            inst = g.tensor_copy(icf[:], ic[:])
            p.inc(inst, "gpS")
            inst = g.tensor_copy(ipf[:], ip_[:])
            p.inc(inst, "gpS")
            p.wait(g, "gpS")
            g.tensor_scalar(c["ident"][:], icf[:], ipf[:, 0:1], None, op0=AluOpType.is_equal)
            inst = g.tensor_scalar(c["maskT"][:], icf[:], ipf[:, 0:1], None, op0=AluOpType.is_ge)
            p.inc(inst, "gpS")
    gp_c = p.cnt["gpS"]

    @block.vector
    def _(v):
        v.memset(c["ones_f"][:], 1.0)
        v.memset(c["ones_bf"][:], 1.0)
        v.memset(c["negones_bf"][:], -1.0)
        inst = v.memset(c["eps_col"][:], LN_EPS)
        p.inc(inst, "dvS")

    @block.sync
    def _(sy):
        if "l1f" in slots:
            inst = sy.dma_start(c["b1_sb"][:].rearrange("p (e c) -> p e c", c=32),
                                p.d["l1_b1"].rearrange("e p c -> p e c"))
            p.inc(inst, "ioS", 16)
            inst = sy.dma_start(c["rtmp"][0:1, :], p.d["l1_bgb"][0:1, :])
            p.inc(inst, "ioS", 16)
            inst = sy.dma_start(c["rtm2"][0:1, :], p.d["l1_bgb"][1:2, :])
            p.inc(inst, "ioS", 16)
        if "l0a" in slots:
            inst = sy.dma_start(c["bq_fm"][:, 0:4], p.d["l0_bq"])
            p.inc(inst, "ioS", 16)
            inst = sy.dma_start(c["bk_fm"][:, 0:4], p.d["l0_bk"])
            p.inc(inst, "ioS", 16)
    io0 = p.cnt["ioS"]

    @block.vector
    def _(v):
        p.wait(v, "ioS", io0)
        p.wait(v, "gpS", gp_c)
        if "l0a" in slots:
            inst = v.tensor_scalar(c["bq_fm"][:, 0:4], c["bq_fm"][:, 0:4], QK_SCALE, None,
                                   op0=AluOpType.mult)
            p.inc(inst, "dvS")
        if "l1f" in slots:
            inst = v.tensor_tensor(c["bgb"][:], c["rtmp"][0:1, :], c["rtm2"][0:1, :],
                                   op=AluOpType.add)
            p.inc(inst, "dvS")
            p.wait(v, "dvS")
            inst = v.tensor_copy(c["bgbb"][:], c["bgb"][:])
            p.inc(inst, "dvS")


def emit_xload(p, b):
    @p.block.sync
    def _(sy):
        p.wait(sy, "peS"); p.wait(sy, "dvS"); p.wait(sy, "acS"); p.wait(sy, "ioS")
        src = p.d["x"][b * S:(b + 1) * S, :].rearrange("(t p) c -> p t c", p=128)
        dst = p.c["xb"][:].rearrange("p (t c) -> p t c", c=1024)
        inst = sy.dma_start(dst, src)
        p.inc(inst, "ioS", 16)


def emit_xstore(p, b):
    @p.block.sync
    def _(sy):
        p.wait(sy, "dvS"); p.wait(sy, "acS"); p.wait(sy, "peS")
        dst = p.d["out"][b * S:(b + 1) * S, :].rearrange("(t p) c -> p t c", p=128)
        srcb = p.c["xb"][:].rearrange("p (t c) -> p t c", c=1024)
        inst = sy.dma_start(dst, srcb)
        p.inc(inst, "ioS", 16)


def emit_ln(p, save_ab=False):
    block, c = p.block, p.c
    io_x = p.cnt["ioS"]

    @block.scalar
    def _(s):
        p.wait(s, "ioS", io_x)
        p.wait(s, "dvS"); p.wait(s, "peS")
        for tt in range(NT):
            inst = s.activation(c["junk"][:], c["xb"][:, tt * 1024:(tt + 1) * 1024],
                                AF.Square, accum_out=c["sq_c"][:, tt:tt + 1])
            p.inc(inst, "acS")
    ac_sq = p.cnt["acS"]

    @block.vector
    def _(v):
        p.wait(v, "ioS", io_x)
        p.wait(v, "peS")
        for tt in range(NT):
            inst = v.tensor_scalar(c["junk"][:], c["xb"][:, tt * 1024:(tt + 1) * 1024],
                                   1.0, None, op0=AluOpType.mult, op1=AluOpType.add,
                                   accum_out=c["sum_c"][:, tt:tt + 1])
            p.inc(inst, "dvS")
        p.wait(v, "dvS")
        inst = v.tensor_scalar(c["mu"][:], c["sum_c"][:], 1.0 / D, None, op0=AluOpType.mult)
        p.inc(inst, "dvS")
        p.wait(v, "acS", ac_sq)
        inst = v.tensor_scalar(c["msq"][:], c["sq_c"][:], 1.0 / D, None, op0=AluOpType.mult)
        p.inc(inst, "dvS")
        p.wait(v, "dvS")
        inst = v.tensor_tensor(c["mu2"][:], c["mu"][:], c["mu"][:], op=AluOpType.mult)
        p.inc(inst, "dvS")
        p.wait(v, "dvS")
        inst = v.tensor_tensor(c["var"][:], c["msq"][:], c["mu2"][:], op=AluOpType.subtract)
        p.inc(inst, "dvS")
    dv_var = p.cnt["dvS"]

    @block.scalar
    def _(s):
        p.wait(s, "dvS", dv_var)
        inst = s.activation(c["lnv"][:], c["var"][:], AF.Ln, bias=c["eps_col"][:, 0:1])
        p.inc(inst, "acS")
        p.wait(s, "acS")
        inst = s.activation(c["rs"][:], c["lnv"][:], AF.Exp, scale=-0.5)
        p.inc(inst, "acS")
    ac_rs = p.cnt["acS"]

    @block.vector
    def _(v):
        p.wait(v, "acS", ac_rs)
        inst = v.scalar_tensor_tensor(c["nmr"][:], c["mu"][:], -1.0, c["rs"][:],
                                      op0=AluOpType.mult, op1=AluOpType.mult)
        p.inc(inst, "dvS")
        if save_ab:
            inst = v.tensor_scalar(c["A_c"][:], c["rs"][:], RSQRT2, 1.0,
                                   op0=AluOpType.mult, op1=AluOpType.add)
            p.inc(inst, "dvS")
            inst = v.scalar_tensor_tensor(c["B_c"][:], c["mu"][:], -RSQRT2, c["rs"][:],
                                          op0=AluOpType.mult, op1=AluOpType.mult)
            p.inc(inst, "dvS")
    dv_nmr = p.cnt["dvS"]

    ac_apply, pe_tr, ac_ev = {}, {}, {}
    for tt in range(NT):
        @block.scalar
        def _(s, tt=tt):
            if tt == 0:
                p.wait(s, "dvS", dv_nmr)
            if tt >= 1:
                p.wait(s, "peS", pe_tr[tt - 1])
            inst = s.activation(c["lntm"][:], c["xb"][:, tt * 1024:(tt + 1) * 1024],
                                AF.Identity, scale=c["rs"][:, tt:tt + 1],
                                bias=c["nmr"][:, tt:tt + 1])
            ac_apply[tt] = p.inc(inst, "acS")

        @block.tensor
        def _(t, tt=tt):
            p.wait(t, "acS", ac_apply[tt])
            if tt >= 1:
                p.wait(t, "acS", ac_ev[tt - 1])
            for dk in range(DK):
                inst = t.transpose(c["pC"][:, dk * 128:(dk + 1) * 128],
                                   c["lntm"][:, dk * 128:(dk + 1) * 128], c["ident"][:])
            pe_tr[tt] = p.inc(inst, "peS")

        @block.scalar
        def _(s, tt=tt):
            p.wait(s, "peS", pe_tr[tt])
            out = c["lnf"][:].rearrange("p (dk q) -> p dk q", dk=DK)[:, :, tt * 128:(tt + 1) * 128]
            inst = s.activation(out, c["pC"][:], AF.Copy)
            ac_ev[tt] = p.inc(inst, "acS")


def emit_proj_fm(p, dram_name, ot_list, out_sb, bias_fm=None, bias_cols=None, scale=1.0,
                 interleave=None):
    block, c = p.block, p.c
    wmark, pe_done, ac_done = {}, {}, {}
    for i, ot in enumerate(ot_list):
        @block.gpsimd
        def _(g, i=i, ot=ot):
            if i < 2:
                p.wait(g, "peS")
            else:
                p.wait(g, "peS", pe_done[i - 2])
            src = p.d[dram_name][ot].rearrange("p dk c -> p (dk c)")
            inst = g.dma_start(c["wbB0" if i % 2 == 0 else "wbB1"][:, 0:DK * 128], src)
            wmark[i] = p.inc(inst, "wS", 16)

        @block.tensor
        def _(t, i=i):
            p.wait(t, "wS", wmark[i])
            if i == 0:
                p.wait(t, "acS"); p.wait(t, "dvS")
            if i >= 1:
                p.wait(t, "acS", ac_done[i - 1])
            wb = c["wbB0" if i % 2 == 0 else "wbB1"]
            inst = None
            for dk in range(DK):
                for tc in range(4):
                    inst = t.matmul(c["pAB"][:][:, tc * 512:(tc + 1) * 512],
                                    wb[:, dk * 128:(dk + 1) * 128],
                                    lnf_strip(p, dk, tc * 512, (tc + 1) * 512),
                                    start=(dk == 0), stop=(dk == DK - 1))
            pe_done[i] = p.inc(inst, "peS")

        @block.scalar
        def _(s, i=i):
            p.wait(s, "peS", pe_done[i])
            kw = {}
            fn = AF.Copy
            if bias_fm is not None:
                kw["bias"] = bias_fm[:, bias_cols[i]:bias_cols[i] + 1]
                fn = AF.Identity
            inst = s.activation(out_sb[:, i * 2048:(i + 1) * 2048], c["pAB"][:], fn,
                                scale=scale, **kw)
            ac_done[i] = p.inc(inst, "acS")
        if interleave is not None:
            interleave()
    return ac_done


def emit_vtm(p, wv_name, bv_name, voff, interleave=None):
    block, c = p.block, p.c
    vc = 520

    @block.gpsimd
    def _(g):
        p.wait(g, "peS")
        src = p.d[wv_name].rearrange("(dk p) c -> p dk c", p=128)[:, :, voff:voff + vc]
        dst = c["wbA0"][:, 0:DK * vc].rearrange("p (dk c) -> p dk c", dk=DK)
        inst = g.dma_start(dst, src)
        p.inc(inst, "wS", 16)
    wm = p.cnt["wS"]

    @block.gpsimd
    def _(g):
        inst = g.dma_start(c["rowb"][:, 0:vc], p.d[bv_name][:, voff:voff + vc])
        p.inc(inst, "wS", 16)
    iom = p.cnt["wS"]

    pe_done, ac_done = {}, {}
    for tt in range(NT):
        @block.tensor
        def _(t, tt=tt):
            if tt == 0:
                p.wait(t, "wS", max(wm, iom))
                p.wait(t, "acS")
            if tt >= 2:
                p.wait(t, "acS", ac_done[tt - 2])
            ps = c["pA"] if tt % 2 == 0 else c["pB"]
            t.matmul(ps[:, 0:512], c["ones_bf"][:, 0:128], c["rowb"][0:1, 0:512],
                     start=True, stop=True)
            t.matmul(ps[:, 512:vc], c["ones_bf"][:, 0:128], c["rowb"][0:1, 512:vc],
                     start=True, stop=True)
            inst = None
            for dk in range(DK):
                t.matmul(ps[:, 0:512], lnf_strip(p, dk, tt * 128, (tt + 1) * 128),
                         c["wbA0"][:, dk * vc:dk * vc + 512],
                         start=False, stop=(dk == DK - 1))
                inst = t.matmul(ps[:, 512:vc], lnf_strip(p, dk, tt * 128, (tt + 1) * 128),
                                c["wbA0"][:, dk * vc + 512:(dk + 1) * vc],
                                start=False, stop=(dk == DK - 1))
            pe_done[tt] = p.inc(inst, "peS")

        @block.scalar
        def _(s, tt=tt):
            p.wait(s, "peS", pe_done[tt])
            ps = c["pA"] if tt % 2 == 0 else c["pB"]
            inst = s.activation(c["vtm"][:, tt * 520:tt * 520 + vc], ps[:, 0:vc], AF.Copy)
            ac_done[tt] = p.inc(inst, "acS")
        if interleave is not None:
            interleave()


class PassA:

    def __init__(self, p, causal):
        self.p = p
        self.causal = causal
        self.units = [(h, qt) for h in range(8) for qt in range(NT)]
        self.idx = 0
        self.kmark = {}
        self.dv_done = {}
        self.pe_done = {}

    def set_ready(self, hp, ac_mark):
        self.kmark[hp] = ac_mark

    def avail(self):
        n = 0
        for (h, qt) in self.units[self.idx:]:
            if h // 2 in self.kmark:
                n += 1
            else:
                break
        return n

    def pump(self, n):
        p, block, c = self.p, self.p.block, self.p.c
        n = min(n, self.avail())
        for _ in range(n):
            h, qt = self.units[self.idx]
            k = self.idx
            self.idx += 1
            hp, hb = h // 2, (h % 2) * 64
            lim = 2048 if not self.causal else (qt + 1) * 128
            causal = self.causal

            @block.tensor
            def _(t, h=h, qt=qt, hp=hp, hb=hb, lim=lim, k=k):
                if qt == 0:
                    p.wait(t, "acS", self.kmark[hp])
                if k >= 1:
                    p.wait(t, "dvS", self.dv_done[k - 1])
                lhs = c["qf"][hb:hb + 64, hp * 2048 + qt * 128: hp * 2048 + (qt + 1) * 128]
                inst = None
                for sc in range(0, lim, 512):
                    w = min(512, lim - sc)
                    inst = t.matmul(c["pCD"][:][:, sc:sc + w], lhs,
                                    c["kf"][hb:hb + 64, hp * 2048 + sc: hp * 2048 + sc + w],
                                    start=True, stop=True)
                self.pe_done[k] = p.inc(inst, "peS")

            @block.vector
            def _(v, lim=lim, k=k, h=h, qt=qt):
                p.wait(v, "peS", self.pe_done[k])
                inst = v.tensor_reduce(c["M_all"][:, h * NT + qt:h * NT + qt + 1],
                                       c["pCD"][:][:, 0:lim], axis=AX.X, op=AluOpType.max)
                self.dv_done[k] = p.inc(inst, "dvS")

    def finish(self):
        p, block, c = self.p, self.p.block, self.p.c
        self.pump(len(self.units))
        assert self.idx == len(self.units)
        dvm = p.cnt["dvS"]

        @block.tensor
        def _(t):
            p.wait(t, "dvS", dvm)
            inst = t.transpose(c["pC"][:, 0:128], c["M_all"][:], c["ident"][:])
            p.inc(inst, "peS")
        pet = p.cnt["peS"]

        @block.scalar
        def _(s):
            p.wait(s, "peS", pet)
            inst = s.activation(c["mts"][:], c["pC"][:, 0:128], AF.Copy)
            p.inc(inst, "acS")
        act = p.cnt["acS"]

        @block.sync
        def _(sy):
            p.wait(sy, "acS", act)
            inst = sy.dma_start(p.d["mbounce"][:], c["mts"][:])
            p.inc(inst, "ioS", 16)


def emit_attn_phaseC(p, qh, causal):
    block, c = p.block, p.c
    iom = p.cnt["ioS"]
    for h in range(8):
        hp, hb = h // 2, (h % 2) * 64
        pt_hi = NT if not causal else (qh + 1) * 8
        dv_prev_h = p.cnt["dvS"]
        ac_head0 = p.cnt["acS"]
        ac_exp = {}
        pe_pv = {}
        pe_s2 = {}

        @block.sync
        def _(sy, h=h):
            p.wait(sy, "ioS", iom)
            if h > 0 or qh > 0:
                p.wait(sy, "peS")
            src = p.d["mbounce"][h * 16:(h + 1) * 16, :].rearrange("a b -> (a b)").unsqueeze(0)
            inst = sy.dma_start(c["negmax"][:], src)
            p.inc(inst, "ioS", 16)
        io_h = p.cnt["ioS"]

        for ptp in range(0, pt_hi, 3):
            pts = [q for q in (ptp, ptp + 1, ptp + 2) if q < pt_hi]
            for pt in pts:
                sps = (c["pC"], c["pA"], c["pB"])[pt % 3]
                qc_min = 0 if not causal else max(0, (pt * 128 - qh * 1024 - 512) // 512 + 1)

                @block.tensor
                def _(t, h=h, pt=pt, hp=hp, hb=hb, qc_min=qc_min, sps=sps):
                    if pt == 0:
                        p.wait(t, "ioS", io_h)
                        p.wait(t, "acS", ac_head0)
                        p.wait(t, "dvS", dv_prev_h)
                    if pt >= 3:
                        p.wait(t, "acS", ac_exp[pt - 3])
                    inst = None
                    for qc in range(qc_min, 2):
                        ps = sps[:, qc * 512:(qc + 1) * 512]
                        t.matmul(ps, c["negones_bf"][:, 0:128],
                                 c["negmax"][0:1, qh * 1024 + qc * 512: qh * 1024 + (qc + 1) * 512],
                                 start=True, stop=True)
                        inst = t.matmul(ps,
                                        c["kf"][hb:hb + 64,
                                                hp * 2048 + pt * 128:hp * 2048 + (pt + 1) * 128],
                                        c["qf"][hb:hb + 64,
                                                hp * 2048 + qh * 1024 + qc * 512:
                                                hp * 2048 + qh * 1024 + (qc + 1) * 512],
                                        start=False, stop=True)
                    pe_s2[pt] = p.inc(inst, "peS")

            @block.scalar
            def _(s, pts=tuple(pts), qh=qh, causal=causal):
                for pt in pts:
                    ptile = (c["pt0"], c["pt1"], c["pt2"])[pt % 3]
                    sps = (c["pC"], c["pA"], c["pB"])[pt % 3]
                    qc_min = 0 if not causal else max(0, (pt * 128 - qh * 1024 - 512) // 512 + 1)
                    p.wait(s, "peS", pe_s2[pt])
                    if pt >= 3:
                        p.wait(s, "peS", pe_pv[pt - 3])
                    inst = s.activation(ptile[:, qc_min * 512:1024],
                                        sps[:, qc_min * 512:1024], AF.Exp)
                    ac_exp[pt] = p.inc(inst, "acS")

            dv_mask = {}
            for pt in pts:
                qd = pt * 128 - qh * 1024 if causal else -1000000
                if causal and 0 <= qd < 1024:
                    ptile = (c["pt0"], c["pt1"], c["pt2"])[pt % 3]

                    @block.vector
                    def _(v, pt=pt, qd=qd, ptile=ptile):
                        p.wait(v, "acS", ac_exp[pt])
                        lo = (qd // 512) * 512
                        if qd > lo:
                            inst = v.memset(ptile[:, lo:qd], 0.0)
                            p.inc(inst, "dvS")
                            p.wait(v, "dvS")
                        inst = v.tensor_tensor(ptile[:, qd:qd + 128], ptile[:, qd:qd + 128],
                                               c["maskT"][:], op=AluOpType.mult)
                        p.inc(inst, "dvS")
                    dv_mask[pt] = p.cnt["dvS"]

            for pt in pts:
                ptile = (c["pt0"], c["pt1"], c["pt2"])[pt % 3]
                qc_min = 0 if not causal else max(0, (pt * 128 - qh * 1024 - 512) // 512 + 1)

                @block.tensor
                def _(t, h=h, pt=pt, qc_min=qc_min, ptile=ptile, pt_hi=pt_hi):
                    p.wait(t, "acS", ac_exp[pt])
                    if pt in dv_mask:
                        p.wait(t, "dvS", dv_mask[pt])
                    inst = None
                    for qc in range(qc_min, 2):
                        last_pt = pt_hi - 1 if not causal else min(pt_hi - 1, qh * 8 + qc * 4 + 3)
                        inst = t.matmul(c["pD"][0:65, qc * 512:(qc + 1) * 512],
                                        c["vtm"][:, pt * 520 + h * 65: pt * 520 + (h + 1) * 65],
                                        ptile[:, qc * 512:(qc + 1) * 512],
                                        start=(pt == 0), stop=(pt == last_pt))
                    pe_pv[pt] = p.inc(inst, "peS")

        pe_last = p.cnt["peS"]

        for half in range(2):
            @block.vector
            def _(v, half=half, pe_last=pe_last):
                if half == 0:
                    p.wait(v, "peS", pe_last)
                p.wait(v, "acS")
                p.wait(v, "dvS")
                with p.nc.allow_low_precision(reason="bf16 rinv row for rank-1 bcast"):
                    inst = v.reciprocal(c["rrow"][:],
                                        c["pD"][64:65, half * 512:(half + 1) * 512])
                p.inc(inst, "dvS")
            dv_r = p.cnt["dvS"]

            @block.tensor
            def _(t, dv_r=dv_r, half=half):
                p.wait(t, "dvS", dv_r)
                p.wait(t, "acS")
                inst = t.matmul(c["pC"][0:64, half * 512:(half + 1) * 512],
                                c["ones_bf"][:, 0:64], c["rrow"][:],
                                start=True, stop=True)
                p.inc(inst, "peS")
            pe_rb = p.cnt["peS"]

            @block.scalar
            def _(s, pe_rb=pe_rb, half=half):
                p.wait(s, "peS", pe_rb)
                inst = s.activation(c["rb"][:], c["pC"][0:64, half * 512:(half + 1) * 512],
                                    AF.Copy)
                p.inc(inst, "acS")
            ac_rb = p.cnt["acS"]

            @block.vector
            def _(v, ac_rb=ac_rb, hp=hp, hb=hb, half=half):
                p.wait(v, "acS", ac_rb)
                inst = v.tensor_tensor(
                    c["of"][hb:hb + 64, hp * 1024 + half * 512: hp * 1024 + (half + 1) * 512],
                    c["pD"][0:64, half * 512:(half + 1) * 512], c["rb"][:],
                    op=AluOpType.mult)
                p.inc(inst, "dvS")


def emit_attn_proj(p, qh, slot):
    block, c = p.block, p.c
    dv_of = p.cnt["dvS"]
    pe_done = {}
    ac_free = {}
    dv_free = {}
    if slot == "l0a":
        @block.gpsimd
        def _(g):
            p.wait(g, "peS")
            inst = g.dma_start(c["rowb"][:], p.d["l0_bout"])
            p.inc(inst, "wS", 16)
        io_bout = p.cnt["wS"]
    for ti in range(8):
        tt = qh * 8 + ti
        ps = c["pA"] if ti % 2 == 0 else c["pB"]

        @block.tensor
        def _(t, ti=ti, ps=ps):
            if ti == 0:
                p.wait(t, "dvS", dv_of)
                p.wait(t, "wS")
                p.wait(t, "acS")
                if slot == "l0a":
                    p.wait(t, "wS", io_bout)
            if ti >= 2:
                if slot == "l0a":
                    p.wait(t, "dvS", dv_free[ti - 2])
                    p.wait(t, "acS", ac_free[ti - 2])
                else:
                    p.wait(t, "dvS")
            inst = None
            if slot == "l0a":
                t.matmul(ps[:, 0:512], c["ones_bf"][:, 0:128], c["rowb"][0:1, 0:512],
                         start=True, stop=True)
                t.matmul(ps[:, 512:1024], c["ones_bf"][:, 0:128],
                         c["rowb"][0:1, 512:1024], start=True, stop=True)
            for vk in range(4):
                for dc in range(2):
                    inst = t.matmul(ps[:, dc * 512:(dc + 1) * 512],
                                    c["of"][:, vk * 1024 + ti * 128: vk * 1024 + (ti + 1) * 128],
                                    c["wbA1"][:, vk * 1024 + dc * 512: vk * 1024 + (dc + 1) * 512],
                                    start=(False if slot == "l0a" else vk == 0),
                                    stop=(vk == 3))
            pe_done[ti] = p.inc(inst, "peS")

        if slot == "l1a":
            @block.vector
            def _(v, ti=ti, tt=tt, ps=ps):
                p.wait(v, "peS", pe_done[ti])
                inst = v.tensor_tensor(c["xb"][:, tt * 1024:(tt + 1) * 1024],
                                       c["xb"][:, tt * 1024:(tt + 1) * 1024],
                                       ps[:], op=AluOpType.add)
                p.inc(inst, "dvS")
        else:
            @block.scalar
            def _(s, ti=ti, ps=ps):
                p.wait(s, "peS", pe_done[ti])
                p.wait(s, "dvS")
                inst = s.activation(c["junk"][:], ps[:], AF.Square,
                                    accum_out=c["ycol"][:, 1:2])
                p.inc(inst, "acS")
            ac_sq = p.cnt["acS"]

            @block.vector
            def _(v, ti=ti, ps=ps, ac_sq=ac_sq):
                p.wait(v, "peS", pe_done[ti])
                p.wait(v, "dvS")
                p.wait(v, "acS", ac_sq - 1)
                inst = v.tensor_scalar(c["junk"][:], ps[:], 1.0, None,
                                       op0=AluOpType.mult, op1=AluOpType.add,
                                       accum_out=c["ycol"][:, 0:1])
                dv_free[ti] = p.inc(inst, "dvS")
                p.wait(v, "dvS")
                inst = v.tensor_scalar(c["ycol"][:, 2:3], c["ycol"][:, 0:1], 1.0 / D, None,
                                       op0=AluOpType.mult)
                p.inc(inst, "dvS")
                p.wait(v, "acS", ac_sq)
                p.wait(v, "dvS")
                inst = v.tensor_scalar(c["ycol"][:, 3:4], c["ycol"][:, 1:2], 1.0 / D, None,
                                       op0=AluOpType.mult)
                p.inc(inst, "dvS")
                p.wait(v, "dvS")
                inst = v.tensor_tensor(c["ycol"][:, 4:5], c["ycol"][:, 2:3], c["ycol"][:, 2:3],
                                       op=AluOpType.mult)
                p.inc(inst, "dvS")
                p.wait(v, "dvS")
                inst = v.tensor_tensor(c["ycol"][:, 5:6], c["ycol"][:, 3:4], c["ycol"][:, 4:5],
                                       op=AluOpType.subtract)
                p.inc(inst, "dvS")
            dv_vy = p.cnt["dvS"]

            @block.scalar
            def _(s, dv_vy=dv_vy):
                p.wait(s, "dvS", dv_vy)
                inst = s.activation(c["ycol"][:, 6:7], c["ycol"][:, 5:6], AF.Ln,
                                    bias=c["eps_col"][:, 0:1])
                p.inc(inst, "acS")
                p.wait(s, "acS")
                inst = s.activation(c["ycol"][:, 7:8], c["ycol"][:, 6:7], AF.Exp, scale=-0.5)
                p.inc(inst, "acS")
            ac_ry = p.cnt["acS"]

            @block.vector
            def _(v, ac_ry=ac_ry):
                p.wait(v, "acS", ac_ry)
                inst = v.scalar_tensor_tensor(c["ycol"][:, 4:5], c["ycol"][:, 2:3], -1.0,
                                              c["ycol"][:, 7:8], op0=AluOpType.mult,
                                              op1=AluOpType.mult)
                p.inc(inst, "dvS")
            dv_nmy = p.cnt["dvS"]

            @block.scalar
            def _(s, ps=ps, dv_nmy=dv_nmy, tt=tt, ti=ti):
                p.wait(s, "dvS", dv_nmy)
                inst = s.activation(c["lntm"][:], ps[:], AF.Identity,
                                    scale=c["ycol"][:, 7:8], bias=c["ycol"][:, 4:5])
                ac_free[ti] = p.inc(inst, "acS")
                p.wait(s, "acS")
                inst = s.activation(c["xb"][:, tt * 1024:(tt + 1) * 1024],
                                    c["xb"][:, tt * 1024:(tt + 1) * 1024], AF.Identity,
                                    scale=c["A_c"][:, tt:tt + 1], bias=c["B_c"][:, tt:tt + 1])
                p.inc(inst, "acS")
            ac_t = p.cnt["acS"]

            @block.vector
            def _(v, ac_t=ac_t, tt=tt):
                p.wait(v, "acS", ac_t)
                inst = v.tensor_tensor(c["lntm"][:], c["lntm"][:], c["gamma_bc"],
                                       op=AluOpType.mult)
                p.inc(inst, "dvS")
                p.wait(v, "dvS")
                inst = v.tensor_tensor(c["lntm"][:], c["lntm"][:], c["beta_bc"],
                                       op=AluOpType.add)
                p.inc(inst, "dvS")
                p.wait(v, "dvS")
                inst = v.tensor_tensor(c["xb"][:, tt * 1024:(tt + 1) * 1024],
                                       c["xb"][:, tt * 1024:(tt + 1) * 1024],
                                       c["lntm"][:], op=AluOpType.add)
                p.inc(inst, "dvS")


def emit_gb_build(p):
    block, c = p.block, p.c
    for which, dname, ps in (("g", "l0_gamma", "pA"), ("b", "l0_beta", "pB")):
        @block.gpsimd
        def _(g, dname=dname):
            p.wait(g, "peS")
            inst = g.dma_start(c["rowb"][:], p.d[dname])
            p.inc(inst, "wS", 16)
        iom = p.cnt["wS"]

        @block.tensor
        def _(t, ps=ps, iom=iom):
            p.wait(t, "wS", iom)
            p.wait(t, "dvS"); p.wait(t, "acS")
            inst = None
            for half in range(2):
                inst = t.matmul(c[ps][:, half * 512:(half + 1) * 512], c["ones_bf"][:, 0:128],
                                c["rowb"][0:1, half * 512:(half + 1) * 512],
                                start=True, stop=True)
            p.inc(inst, "peS")
    pe1 = p.cnt["peS"]

    @block.scalar
    def _(s):
        p.wait(s, "peS", pe1)
        s.activation(c["gamma_bc"], c["pA"][:], AF.Copy, scale=RSQRT2)
        inst = s.activation(c["beta_bc"], c["pB"][:], AF.Copy, scale=RSQRT2)
        p.inc(inst, "acS")


def emit_l0a(p):
    c = p.c
    emit_ln(p, save_ab=True)
    pa = PassA(p, causal=False)
    emit_proj_fm(p, "l0_wq", [0, 1, 2, 3], c["qf"], bias_fm=c["bq_fm"],
                 bias_cols=[0, 1, 2, 3], scale=QK_SCALE)

    def hook():
        pa.pump(6)
    km = emit_proj_fm(p, "l0_wk", [0, 1, 2, 3], c["kf"], bias_fm=c["bk_fm"],
                      bias_cols=[0, 1, 2, 3], scale=1.0,
                      interleave=None)
    for hp in range(4):
        pa.set_ready(hp, km[hp])
    emit_vtm(p, "l0_wv", "l0_bv", 0, interleave=lambda: pa.pump(6))

    @p.block.gpsimd
    def _(g):
        p.wait(g, "peS")
        src = p.d["l0_wout"].rearrange("(vk p) c -> p vk c", p=128)
        dst = c["wbA1"][:, 0:4096].rearrange("p (vk c) -> p vk c", c=1024)
        inst = g.dma_start(dst, src)
        p.inc(inst, "wS", 16)
    pa.finish()
    emit_gb_build(p)
    for qh in range(2):
        emit_attn_phaseC(p, qh, causal=False)
        emit_attn_proj(p, qh, "l0a")


def emit_l1a(p):
    c = p.c
    emit_ln(p, save_ab=False)
    for grp in range(2):
        pa = PassA(p, causal=True)
        emit_proj_fm(p, "l1_wq", [4 * grp + i for i in range(4)], c["qf"], scale=QK_SCALE)
        km = emit_proj_fm(p, "l1_wk", [4 * grp + i for i in range(4)], c["kf"], scale=1.0)
        for hp in range(4):
            pa.set_ready(hp, km[hp])
        emit_vtm(p, "l1_wv", "l1_bv", 520 * grp, interleave=lambda: pa.pump(6))

        @p.block.gpsimd
        def _(g, grp=grp):
            p.wait(g, "peS")
            src = p.d["l1_wo"][512 * grp:512 * (grp + 1), :].rearrange(
                "(vk p) c -> p vk c", p=128)
            dst = c["wbA1"][:, 0:4096].rearrange("p (vk c) -> p vk c", c=1024)
            inst = g.dma_start(dst, src)
            p.inc(inst, "wS", 16)
        pa.finish()
        for qh in range(2):
            emit_attn_phaseC(p, qh, causal=True)
            emit_attn_proj(p, qh, "l1a")


def emit_l0f(p):
    c = p.c
    block = p.block
    emit_ln(p, save_ab=False)

    for th in range(2):
        wm = {}
        pe_kt = {}
        dv_stt = {}
        ac_sil = {}
        for hc in range(6):
            @block.gpsimd
            def _(g, hc=hc, th=th):
                if hc >= 1:
                    p.wait(g, "peS", pe_kt[hc * 4 - 1])
                else:
                    p.wait(g, "peS")
                inst = g.dma_start(c["wbA0"][:, 0:4096],
                                   p.d["l0_wg"][hc].rearrange("p dk c -> p (dk c)"))
                p.inc(inst, "wS", 16)
                inst = g.dma_start(c["wbA1"][:, 0:4096],
                                   p.d["l0_wu"][hc].rearrange("p dk c -> p (dk c)"))
                wm[hc] = p.inc(inst, "wS", 16)

            for pair in range(2):
                kts = [hc * 4 + pair * 2, hc * 4 + pair * 2 + 1]
                for kt in kts:
                    ht = kt - hc * 4

                    @block.tensor
                    def _(t, hc=hc, ht=ht, th=th, kt=kt):
                        if ht == 0:
                            p.wait(t, "wS", wm[hc])
                        if kt >= 2:
                            p.wait(t, "dvS", dv_stt.get(kt - 2, 0))
                            p.wait(t, "acS", ac_sil.get(kt - 2, 0))
                        elif kt < 2 and th == 0 and hc == 0:
                            p.wait(t, "dvS"); p.wait(t, "acS")
                        elif kt < 2:
                            p.wait(t, "dvS"); p.wait(t, "acS")
                        pg = c["pA"] if kt % 2 == 0 else c["pC"]
                        pu = c["pB"] if kt % 2 == 0 else c["pD"]
                        inst = None
                        for dk in range(DK):
                            for tc in range(2):
                                t.matmul(pg[:, tc * 512:(tc + 1) * 512],
                                         c["wbA0"][:, dk * 512 + ht * 128: dk * 512 + (ht + 1) * 128],
                                         lnf_strip(p, dk, th * 1024 + tc * 512,
                                                   th * 1024 + (tc + 1) * 512),
                                         start=(dk == 0), stop=(dk == DK - 1))
                                inst = t.matmul(pu[:, tc * 512:(tc + 1) * 512],
                                                c["wbA1"][:, dk * 512 + ht * 128: dk * 512 + (ht + 1) * 128],
                                                lnf_strip(p, dk, th * 1024 + tc * 512,
                                                          th * 1024 + (tc + 1) * 512),
                                                start=(dk == 0), stop=(dk == DK - 1))
                        pe_kt[kt] = p.inc(inst, "peS")

                @block.scalar
                def _(s, kts=tuple(kts)):
                    for kt in kts:
                        p.wait(s, "peS", pe_kt[kt])
                        if kt >= 2:
                            p.wait(s, "dvS", dv_stt.get(kt - 2, 0))
                        pg = c["pA"] if kt % 2 == 0 else c["pC"]
                        sb = c["pt0"] if kt % 2 == 0 else c["pt1"]
                        with p.nc.allow_low_precision(reason="bf16 silu factor (bf16 matmul path)"):
                            inst = s.activation(sb[:], pg[:], AF.Silu, scale=DEQ)
                        ac_sil[kt] = p.inc(inst, "acS")

                for kt in kts:
                    @block.vector
                    def _(v, kt=kt):
                        p.wait(v, "acS", ac_sil[kt])
                        pu = c["pB"] if kt % 2 == 0 else c["pD"]
                        sb = c["pt0"] if kt % 2 == 0 else c["pt1"]
                        inst = v.scalar_tensor_tensor(c["hfm"][:, kt * 1024:(kt + 1) * 1024],
                                                      pu[:], DEQ, sb[:],
                                                      op0=AluOpType.mult, op1=AluOpType.mult)
                        dv_stt[kt] = p.inc(inst, "dvS")
        dv_up = p.cnt["dvS"]

        pe_dkt = {}
        for dc in range(2):
            pe_dn = None
            for ktg in range(6):
                @block.gpsimd
                def _(g, dc=dc, ktg=ktg):
                    k = dc * 6 + ktg
                    if k >= 2:
                        p.wait(g, "peS", pe_dkt[k - 2])
                    elif k == 0:
                        p.wait(g, "peS")
                    inst = g.dma_start(c["wbB0" if ktg % 2 == 0 else "wbB1"][:],
                                       p.d["l0_wd"][dc, ktg].rearrange("p j c -> p (j c)"))
                    wm[(dc, ktg)] = p.inc(inst, "wS", 16)

                @block.tensor
                def _(t, dc=dc, ktg=ktg, dv_up=dv_up):
                    p.wait(t, "wS", wm[(dc, ktg)])
                    if ktg == 0:
                        p.wait(t, "dvS")
                        p.wait(t, "acS")
                    wb = c["wbB0" if ktg % 2 == 0 else "wbB1"]
                    inst = None
                    for j in range(4):
                        kt = ktg * 4 + j
                        for tt in range(8):
                            pst = c[PSUMS[tt // 2]]
                            inst = t.matmul(pst[:, (tt % 2) * 512:(tt % 2 + 1) * 512],
                                            c["hfm"][:, kt * 1024 + tt * 128: kt * 1024 + (tt + 1) * 128],
                                            wb[:, j * 512:(j + 1) * 512],
                                            start=(kt == 0), stop=(kt == 23))
                    pe_dkt[dc * 6 + ktg] = p.inc(inst, "peS")
            pe_dn = p.cnt["peS"]

            for tt in range(8):
                @block.vector
                def _(v, tt=tt, dc=dc, th=th, pe_dn=pe_dn):
                    p.wait(v, "peS", pe_dn)
                    pst = c[PSUMS[tt // 2]]
                    col = (th * 8 + tt) * 1024 + dc * 512
                    inst = v.scalar_tensor_tensor(c["xb"][:, col:col + 512],
                                                  pst[:, (tt % 2) * 512:(tt % 2 + 1) * 512],
                                                  DEQ, c["xb"][:, col:col + 512],
                                                  op0=AluOpType.mult, op1=AluOpType.add)
                    p.inc(inst, "dvS")


def emit_l1f(p):
    c = p.c
    block = p.block
    emit_ln(p, save_ab=False)

    @block.gpsimd
    def _(g):
        p.wait(g, "acS")
        src = p.d["l1_wgr"].rearrange("(dk p) c -> p dk c", p=128)
        dst = c["wgr_sb"][:].rearrange("p (dk c) -> p dk c", c=8)
        inst = g.dma_start(dst, src)
        p.inc(inst, "wS", 16)
    wgr_m = p.cnt["wS"]
    for tt in range(NT):
        @block.tensor
        def _(t, tt=tt):
            if tt == 0:
                p.wait(t, "wS", wgr_m)
                p.wait(t, "acS")
            else:
                p.wait(t, "acS")
            t.matmul(c["pC"][:, 0:8], c["ones_bf"][:, 0:128], c["bgbb"][:],
                     start=True, stop=True)
            inst = None
            for dk in range(DK):
                inst = t.matmul(c["pC"][:, 0:8], lnf_strip(p, dk, tt * 128, (tt + 1) * 128),
                                c["wgr_sb"][:, dk * 8:(dk + 1) * 8],
                                start=False, stop=(dk == DK - 1))
            p.inc(inst, "peS")
        pe_r = p.cnt["peS"]

        @block.scalar
        def _(s, pe_r=pe_r):
            p.wait(s, "peS", pe_r)
            p.wait(s, "dvS")
            inst = s.activation(c["rtmp"][:], c["pC"][:, 0:8], AF.Sigmoid)
            p.inc(inst, "acS")
        ac_r = p.cnt["acS"]

        @block.vector
        def _(v, tt=tt, ac_r=ac_r):
            p.wait(v, "acS", ac_r)
            inst = v.tensor_reduce(c["rcol"][:, 0:1], c["rtmp"][:], axis=AX.X,
                                   op=AluOpType.max)
            p.inc(inst, "dvS")
            p.wait(v, "dvS")
            inst = v.tensor_scalar(c["rtm2"][:], c["rtmp"][:], c["rcol"][:, 0:1], None,
                                   op0=AluOpType.is_ge, op1=AluOpType.add,
                                   accum_out=c["rcol"][:, 2:3])
            p.inc(inst, "dvS")
            p.wait(v, "dvS")
            inst = v.scalar_tensor_tensor(c["rtm3"][:], c["rtm2"][:], -1e30, c["rtmp"][:],
                                          op0=AluOpType.mult, op1=AluOpType.add)
            p.inc(inst, "dvS")
            p.wait(v, "dvS")
            inst = v.tensor_reduce(c["rcol"][:, 1:2], c["rtm3"][:], axis=AX.X,
                                   op=AluOpType.max)
            p.inc(inst, "dvS")
            p.wait(v, "dvS")
            inst = v.tensor_scalar(c["rcol"][:, 3:4], c["rcol"][:, 2:3], 2.0, None,
                                   op0=AluOpType.is_ge)
            p.inc(inst, "dvS")
            p.wait(v, "dvS")
            inst = v.tensor_tensor(c["rcol"][:, 2:3], c["rcol"][:, 0:1], c["rcol"][:, 1:2],
                                   op=AluOpType.subtract)
            p.inc(inst, "dvS")
            p.wait(v, "dvS")
            inst = v.scalar_tensor_tensor(c["rcol"][:, 0:1], c["rcol"][:, 2:3],
                                          c["rcol"][:, 3:4], c["rcol"][:, 1:2],
                                          op0=AluOpType.mult, op1=AluOpType.add)
            p.inc(inst, "dvS")
            p.wait(v, "dvS")
            inst = v.tensor_scalar(c["rtm3"][:], c["rtmp"][:], c["rcol"][:, 0:1], None,
                                   op0=AluOpType.is_ge)
            p.inc(inst, "dvS")
            p.wait(v, "dvS")
            inst = v.tensor_tensor(c["w_sb"][:, tt * 8:(tt + 1) * 8], c["rtm3"][:],
                                   c["rtmp"][:], op=AluOpType.mult)
            p.inc(inst, "dvS")

    for th in range(2):
        for e in range(9):
            wm1, ac_g, pe_u = {}, {}, {}

            @block.gpsimd
            def _(g, e=e):
                p.wait(g, "peS")
                inst = g.dma_start(c["rowb"][:], p.d["l1_b2"][:, e * 1024:(e + 1) * 1024])
                p.inc(inst, "wS", 16)
            io_b2 = p.cnt["wS"]
            for hc in range(8):
                @block.gpsimd
                def _(g, e=e, hc=hc):
                    if hc >= 2:
                        p.wait(g, "peS", pe_u[(hc - 2, 3)])
                    elif hc == 0:
                        p.wait(g, "peS")
                    inst = g.dma_start(c["wbA0" if hc % 2 == 0 else "wbA1"][:, 0:4096],
                                       p.d["l1_w1"][e, hc].rearrange("p dk c -> p (dk c)"))
                    wm1[hc] = p.inc(inst, "wS", 16)

                for ht in range(4):
                    kt = hc * 4 + ht

                    @block.tensor
                    def _(t, hc=hc, ht=ht, th=th, kt=kt):
                        if ht == 0:
                            p.wait(t, "wS", wm1[hc])
                        if kt >= 4:
                            p.wait(t, "acS", ac_g[kt - 4])
                        elif kt == 0:
                            p.wait(t, "dvS"); p.wait(t, "acS")
                        wb = c["wbA0" if hc % 2 == 0 else "wbA1"]
                        ps = c[PSUMS[kt % 4]]
                        inst = None
                        for dk in range(DK):
                            for tc in range(2):
                                inst = t.matmul(ps[:, tc * 512:(tc + 1) * 512],
                                                wb[:, dk * 512 + ht * 128: dk * 512 + (ht + 1) * 128],
                                                lnf_strip(p, dk, th * 1024 + tc * 512,
                                                          th * 1024 + (tc + 1) * 512),
                                                start=(dk == 0), stop=(dk == DK - 1))
                        pe_u[(hc, ht)] = p.inc(inst, "peS")

                @block.scalar
                def _(s, e=e, hc=hc):
                    for ht in range(4):
                        kt = hc * 4 + ht
                        p.wait(s, "peS", pe_u[(hc, ht)])
                        ps = c[PSUMS[kt % 4]]
                        inst = s.activation(c["hfm"][:, kt * 1024:(kt + 1) * 1024], ps[:],
                                            AF.Gelu,
                                            bias=c["b1_sb"][:, e * 32 + kt:e * 32 + kt + 1])
                        ac_g[kt] = p.inc(inst, "acS")
            ac_up = p.cnt["acS"]

            wm2, pe_d = {}, {}

            @block.gpsimd
            def _(g, e=e):
                p.wait(g, "peS")
                inst = g.dma_start(c["rowb"][:], p.d["l1_b2"][:, e * 1024:(e + 1) * 1024])
                p.inc(inst, "wS", 16)
            io_b2 = p.cnt["wS"]
            for dc in range(2):
                @block.tensor
                def _(t, e=e, dc=dc, ac_up=ac_up, io_b2=io_b2):
                    p.wait(t, "acS", ac_up)
                    p.wait(t, "dvS")
                    p.wait(t, "wS", io_b2)
                    inst = None
                    for tt in range(8):
                        pst = c[PSUMS[tt // 2]]
                        inst = t.matmul(pst[:, (tt % 2) * 512:(tt % 2 + 1) * 512],
                                        c["ones_bf"][:, 0:128],
                                        c["rowb"][0:1, dc * 512:(dc + 1) * 512],
                                        start=True, stop=True)
                    p.inc(inst, "peS")

                for ktg in range(8):
                    @block.gpsimd
                    def _(g, e=e, dc=dc, ktg=ktg):
                        k = dc * 8 + ktg
                        if k >= 2:
                            p.wait(g, "peS", pe_d[k - 2])
                        inst = g.dma_start(c["wbB0" if ktg % 2 == 0 else "wbB1"][:],
                                           p.d["l1_w2"][e, dc, ktg].rearrange("p j c -> p (j c)"))
                        wm2[(dc, ktg)] = p.inc(inst, "wS", 16)

                    @block.tensor
                    def _(t, dc=dc, ktg=ktg):
                        p.wait(t, "wS", wm2[(dc, ktg)])
                        p.wait(t, "acS", ac_g[min(ktg * 4 + 3, 31)])
                        wb = c["wbB0" if ktg % 2 == 0 else "wbB1"]
                        inst = None
                        for j in range(4):
                            kt = ktg * 4 + j
                            for tt in range(8):
                                pst = c[PSUMS[tt // 2]]
                                inst = t.matmul(pst[:, (tt % 2) * 512:(tt % 2 + 1) * 512],
                                                c["hfm"][:, kt * 1024 + tt * 128:
                                                          kt * 1024 + (tt + 1) * 128],
                                                wb[:, j * 512:(j + 1) * 512],
                                                start=False, stop=(kt == 31))
                        pe_d[dc * 8 + ktg] = p.inc(inst, "peS")
                pe_dn = p.cnt["peS"]

                for tt in range(8):
                    @block.vector
                    def _(v, e=e, tt=tt, dc=dc, th=th, pe_dn=pe_dn):
                        p.wait(v, "peS", pe_dn)
                        pst = c[PSUMS[tt // 2]]
                        tg = th * 8 + tt
                        col = tg * 1024 + dc * 512
                        if e < 8:
                            inst = v.scalar_tensor_tensor(
                                c["xb"][:, col:col + 512],
                                pst[:, (tt % 2) * 512:(tt % 2 + 1) * 512],
                                c["w_sb"][:, tg * 8 + e:tg * 8 + e + 1],
                                c["xb"][:, col:col + 512],
                                op0=AluOpType.mult, op1=AluOpType.add)
                        else:
                            inst = v.tensor_tensor(
                                c["xb"][:, col:col + 512],
                                c["xb"][:, col:col + 512],
                                pst[:, (tt % 2) * 512:(tt % 2 + 1) * 512],
                                op=AluOpType.add)
                        p.inc(inst, "dvS")


SLOT_EMITTERS = {"l0a": emit_l0a, "l0f": emit_l0f, "l1a": emit_l1a, "l1f": emit_l1f}


def build_program(slots, n_batches):
    nc = bass.Bass("TRN2", target_bir_lowering=False)
    dram = {}
    dram["x"] = nc.dram_tensor("x", [n_batches * S, D], F32, kind="ExternalInput").ap()
    dram["out"] = nc.dram_tensor("out", [n_batches * S, D], F32, kind="ExternalOutput").ap()
    dram["mbounce"] = nc.dram_tensor("mbounce", [128, 128], BF16, kind="Internal").ap()
    wnames = sorted({w for sl in slots for w in SLOT_WEIGHTS[sl]})
    for w in wnames:
        sh, dt = PREP_SPECS[w]
        dram[w] = nc.dram_tensor(w, list(sh), dt, kind="ExternalInput").ap()
    with ExitStack() as st:
        block = st.enter_context(nc.Block())
        p = P(nc, block, st)
        p.d = dram
        alloc_buffers(p, st)
        emit_consts(p, slots)
        for b in range(n_batches):
            emit_xload(p, b)
            for sl in slots:
                SLOT_EMITTERS[sl](p)
            emit_xstore(p, b)

        @block.sync
        def _(sy):
            p.wait(sy, "ioS")
    return nc, wnames


def kernel(**inputs):
    prep = host_prep(inputs)
    x = np.ascontiguousarray(inputs["L0_attn_x"].astype(np.float32, copy=False)
                             ).reshape(B, S, D)
    groups = [ALL_SLOTS] if MODE == "fused" else [(s,) for s in ALL_SLOTS]
    for slots in groups:
        nc, wnames = build_program(slots, BPC)
        in_maps = []
        for core in range(N_CORES):
            m = {w: prep[w] for w in wnames}
            m["x"] = np.ascontiguousarray(
                x[core * BPC:(core + 1) * BPC].reshape(BPC * S, D))
            in_maps.append(m)
        res = run_bass_kernel_spmd(nc, in_maps, core_ids=list(range(N_CORES)))
        x = np.concatenate(
            [r["out"].reshape(BPC, S, D) for r in res.results], axis=0)
    return np.ascontiguousarray(x.reshape(B, S, D)).astype(np.float32, copy=False)


# version 13 (final)
# speedup vs baseline: 1.1512x; 1.0570x over previous
import numpy as np
from contextlib import ExitStack

import concourse.bass as bass
import concourse.mybir as mybir
from concourse.bass_utils import run_bass_kernel_spmd
from concourse.alu_op_type import AluOpType

F32 = mybir.dt.float32
BF16 = mybir.dt.bfloat16
I32 = mybir.dt.int32
AF = mybir.ActivationFunctionType
AX = mybir.AxisListType

B, S, D = 32, 2048, 1024
N_CORES = 8
BPC = B // N_CORES
NT = S // 128
DK = D // 128
RSQRT2 = float(2 ** (-0.5))
LN_EPS = 1e-5
QK_SCALE = 0.125
DEQ = 1.0 / 127.0
MODE = "fused"

ALL_SLOTS = ("l0a", "l0f", "l1a", "l1f")


def _lhs_stream(w_t, n_out_tiles):
    dk = w_t.shape[0] // 128
    a = w_t.reshape(dk, 128, n_out_tiles, 128)
    return np.ascontiguousarray(a.transpose(2, 1, 0, 3))


def _up_stream(w_t, n_hc, w_hc):
    dk = w_t.shape[0] // 128
    a = w_t.reshape(dk, 128, n_hc, w_hc)
    return np.ascontiguousarray(a.transpose(2, 1, 0, 3))


def _down_stream(w_t, n_dc, w_dc, g):
    kt = w_t.shape[0] // 128
    a = w_t.reshape(kt // g, g, 128, n_dc, w_dc)
    return np.ascontiguousarray(a.transpose(3, 0, 2, 1, 4))


def host_prep(inp):
    p = {}
    W_qk = inp["L0_attn_W_qk"]
    wq = np.ascontiguousarray(W_qk.reshape(D, 8, 64, 2)[..., 0].reshape(D, 512))
    wk = np.ascontiguousarray(W_qk.reshape(D, 8, 64, 2)[..., 1].reshape(D, 512))
    b_qk = inp["L0_attn_b_qk"]
    p["l0_bq"] = np.ascontiguousarray(b_qk.reshape(8, 64, 2)[..., 0].reshape(4, 128).T)
    p["l0_bk"] = np.ascontiguousarray(b_qk.reshape(8, 64, 2)[..., 1].reshape(4, 128).T)
    p["l0_wq"] = _lhs_stream(wq, 4)
    p["l0_wk"] = _lhs_stream(wk, 4)
    wv_ext = np.zeros((D, 8, 65), np.float32)
    wv_ext[:, :, :64] = inp["L0_attn_W_v"].reshape(D, 8, 64)
    p["l0_wv"] = np.ascontiguousarray(wv_ext.reshape(D, 520))
    bv_ext = np.zeros((8, 65), np.float32)
    bv_ext[:, :64] = inp["L0_attn_b_v"].reshape(8, 64)
    bv_ext[:, 64] = 1.0
    p["l0_bv"] = np.ascontiguousarray(bv_ext.reshape(1, 520))
    p["l0_wout"] = np.ascontiguousarray(inp["L0_attn_W_out"])
    p["l0_bout"] = np.ascontiguousarray(inp["L0_attn_b_out"].reshape(1, D))
    p["l0_gamma"] = np.ascontiguousarray(inp["L0_attn_ln_gamma"].reshape(1, D))
    p["l0_beta"] = np.ascontiguousarray(inp["L0_attn_ln_beta"].reshape(1, D))
    wg_t = np.ascontiguousarray(inp["L0_ffn_w_gate"].T)
    wu_t = np.ascontiguousarray(inp["L0_ffn_w_up"].T)
    wd_t = np.ascontiguousarray(inp["L0_ffn_w_down"].T)
    p["l0_wg"] = _up_stream(wg_t, 6, 512)
    p["l0_wu"] = _up_stream(wu_t, 6, 512)
    p["l0_wd"] = _down_stream(wd_t, 2, 512, 4)
    wq1 = np.ascontiguousarray(inp["L1_attn_W_Q"].transpose(2, 0, 1).reshape(D, 1024))
    wk1 = np.ascontiguousarray(inp["L1_attn_W_K"].transpose(2, 0, 1).reshape(D, 1024))
    p["l1_wq"] = _lhs_stream(wq1, 8)
    p["l1_wk"] = _lhs_stream(wk1, 8)
    wv1_ext = np.zeros((D, 16, 65), np.float32)
    wv1_ext[:, :, :64] = inp["L1_attn_W_V"].transpose(2, 0, 1).reshape(D, 16, 64)
    p["l1_wv"] = np.ascontiguousarray(wv1_ext.reshape(D, 1040))
    bv1 = np.zeros((16, 65), np.float32)
    bv1[:, 64] = 1.0
    p["l1_bv"] = np.ascontiguousarray(bv1.reshape(1, 1040))
    p["l1_wo"] = np.ascontiguousarray(inp["L1_attn_W_O"].T)
    W1 = np.concatenate([inp["L1_ffn_W1"], inp["L1_ffn_Ws1"][None]], 0)
    W2 = np.concatenate([inp["L1_ffn_W2"], inp["L1_ffn_Ws2"][None]], 0)
    b1 = np.concatenate([inp["L1_ffn_b1"], inp["L1_ffn_bs1"][None]], 0)
    b2 = np.concatenate([inp["L1_ffn_b2"], inp["L1_ffn_bs2"][None]], 0)
    w1t = W1.transpose(0, 2, 1)
    p["l1_w1"] = np.ascontiguousarray(
        w1t.reshape(9, DK, 128, 8, 512).transpose(0, 3, 2, 1, 4))
    w2t = W2.transpose(0, 2, 1)
    p["l1_w2"] = np.ascontiguousarray(
        w2t.reshape(9, 8, 4, 128, 2, 512).transpose(0, 4, 1, 3, 2, 5))
    p["l1_b1"] = np.ascontiguousarray(b1.reshape(9, 32, 128).transpose(0, 2, 1))
    p["l1_b2"] = np.ascontiguousarray(b2.reshape(1, 9 * D))
    p["l1_wgr"] = np.ascontiguousarray(inp["L1_ffn_Wg"].T)
    p["l1_bgb"] = np.ascontiguousarray(
        np.stack([inp["L1_ffn_bg"], inp["L1_ffn_bias"]], 0))
    return p


PREP_SPECS = {
    "l0_bq": ((128, 4), F32), "l0_bk": ((128, 4), F32),
    "l0_wq": ((4, 128, 8, 128), F32), "l0_wk": ((4, 128, 8, 128), F32),
    "l0_wv": ((D, 520), F32), "l0_bv": ((1, 520), F32),
    "l0_wout": ((512, D), F32), "l0_bout": ((1, D), F32),
    "l0_gamma": ((1, D), F32), "l0_beta": ((1, D), F32),
    "l0_wg": ((6, 128, 8, 512), I32), "l0_wu": ((6, 128, 8, 512), I32),
    "l0_wd": ((2, 6, 128, 4, 512), I32),
    "l1_wq": ((8, 128, 8, 128), F32), "l1_wk": ((8, 128, 8, 128), F32),
    "l1_wv": ((D, 1040), F32), "l1_bv": ((1, 1040), F32),
    "l1_wo": ((D, D), F32),
    "l1_w1": ((9, 8, 128, 8, 512), F32), "l1_w2": ((9, 2, 8, 128, 4, 512), F32),
    "l1_b1": ((9, 128, 32), F32), "l1_b2": ((1, 9 * D), F32),
    "l1_wgr": ((D, 8), F32), "l1_bgb": ((2, 8), F32),
}

SLOT_WEIGHTS = {
    "l0a": ["l0_bq", "l0_bk", "l0_wq", "l0_wk", "l0_wv", "l0_bv", "l0_wout",
            "l0_bout", "l0_gamma", "l0_beta"],
    "l0f": ["l0_wg", "l0_wu", "l0_wd"],
    "l1a": ["l1_wq", "l1_wk", "l1_wv", "l1_bv", "l1_wo"],
    "l1f": ["l1_w1", "l1_w2", "l1_b1", "l1_b2", "l1_wgr", "l1_bgb"],
}


class P:
    def __init__(self, nc, block, st):
        self.nc, self.block, self.st = nc, block, st
        self.c = {}
        self.d = {}
        self.cnt = {}
        self.hs = {}
        for n in ("wS", "ioS", "peS", "dvS", "acS", "gpS"):
            self.hs[n] = st.enter_context(nc.semaphore(n))
            self.cnt[n] = 0

    def inc(self, inst, name, n=1):
        inst.then_inc(self.hs[name], n)
        self.cnt[name] += n
        return self.cnt[name]

    def wait(self, eng, name, val=None):
        v = self.cnt[name] if val is None else val
        if v > 0:
            eng.wait_ge(self.hs[name], v)


def lnf_strip(p, dk, lo, hi):
    r = p.c["lnf"][:].rearrange("p (dk q) -> p dk q", dk=DK)
    return r[:, dk, lo:hi]


def alloc_buffers(p, st):
    nc = p.nc
    def T(n, sh, dt):
        p.c[n] = st.enter_context(nc.sbuf_tensor(n, sh, dt))
    def PS(n):
        p.c[n] = st.enter_context(nc.psum_tensor(n, [128, 2048], F32))
    T("xb", [128, NT * 1024], F32)
    T("lnf", [128, DK * 2048], BF16)
    T("blk", [128, 32896], BF16)
    T("wbA0", [128, 4160], BF16)
    T("wbA1", [128, 4160], BF16)
    T("wbB0", [128, 2048], BF16)
    T("wbB1", [128, 2048], BF16)
    T("ptb", [128, 2048], BF16)
    T("pt2", [128, 1024], BF16)
    T("junk", [128, 1024], BF16)
    T("sum_c", [128, NT], F32); T("sq_c", [128, NT], F32)
    T("mu", [128, NT], F32); T("msq", [128, NT], F32); T("mu2", [128, NT], F32)
    T("var", [128, NT], F32); T("lnv", [128, NT], F32); T("rs", [128, NT], F32)
    T("nmr", [128, NT], F32); T("A_c", [128, NT], F32); T("B_c", [128, NT], F32)
    T("ycol", [128, 8], F32)
    T("eps_col", [128, 1], F32)
    T("M_all", [128, 128], F32)
    T("mtmp", [128, 2], F32)
    T("mts", [128, 128], BF16)
    T("negmax", [1, 2048], BF16)
    T("rb", [64, 512], F32)
    T("w_sb", [128, NT * 8], F32)
    T("rtmp", [128, 8], F32)
    T("rtm2", [128, 8], F32)
    T("rtm3", [128, 8], F32)
    T("rcol", [128, 4], F32)
    T("ident", [128, 128], F32)
    T("maskT", [128, 128], BF16)
    T("ones_f", [1, 128], F32)
    T("ones_bf", [1, 128], BF16)
    T("negones_bf", [1, 128], BF16)
    T("rowb", [1, D], BF16)
    T("rrow", [1, 512], BF16)
    T("bgbb", [1, 8], BF16)
    T("bq_fm", [128, 8], F32)
    T("bk_fm", [128, 8], F32)
    T("b1_sb", [128, 9 * 32], F32)
    T("bgb", [1, 8], F32)
    T("wgr_sb", [128, 64], BF16)
    PS("pAB"); PS("pCD")
    p.c["pA"] = p.c["pAB"][:][:, 0:1024]
    p.c["pB"] = p.c["pAB"][:][:, 1024:2048]
    p.c["pC"] = p.c["pCD"][:][:, 0:1024]
    p.c["pD"] = p.c["pCD"][:][:, 1024:2048]
    p.c["pt0"] = p.c["ptb"][:, 0:1024]
    p.c["pt1"] = p.c["ptb"][:, 1024:2048]
    p.c["lntm"] = p.c["ptb"][:].bitcast(F32)
    blk = p.c["blk"]
    p.c["qf"] = blk[:, 0:8192]
    p.c["kf"] = blk[:, 8192:16384]
    p.c["of"] = blk[:, 16384:20480]
    p.c["vtm"] = blk[:, 20480:28800]
    gb = blk[:, 28800:32896]
    p.c["hfm"] = blk[:, 0:32768]
    p.c["gamma_bc"] = gb.bitcast(F32)[:, 0:1024]
    p.c["beta_bc"] = gb.bitcast(F32)[:, 1024:2048]


PSUMS = ("pA", "pB", "pC", "pD")


PRELOAD_FUNCS = {6: AF.Exp, 18: AF.Silu, 2: AF.Sigmoid, 10: AF.Gelu}


def preload_set(p, idx):
    @p.block.scalar
    def _(s):
        s.preload_act_table(PRELOAD_FUNCS[idx])



def emit_consts(p, slots):
    nc, block, c = p.nc, p.block, p.c
    with ExitStack() as st2:
        ic = st2.enter_context(nc.sbuf_tensor("iotac", [128, 128], I32))
        ip_ = st2.enter_context(nc.sbuf_tensor("iotap", [128, 1], I32))
        icf = st2.enter_context(nc.sbuf_tensor("iotacf", [128, 128], F32))
        ipf = st2.enter_context(nc.sbuf_tensor("iotapf", [128, 1], F32))

        @block.gpsimd
        def _(g):
            g.iota(ic[:], pattern=[[1, 128]], base=0, channel_multiplier=0)
            g.iota(ip_[:], pattern=[[0, 1]], base=0, channel_multiplier=1)
            inst = g.tensor_copy(icf[:], ic[:])
            p.inc(inst, "gpS")
            inst = g.tensor_copy(ipf[:], ip_[:])
            p.inc(inst, "gpS")
            p.wait(g, "gpS")
            g.tensor_scalar(c["ident"][:], icf[:], ipf[:, 0:1], None, op0=AluOpType.is_equal)
            inst = g.tensor_scalar(c["maskT"][:], icf[:], ipf[:, 0:1], None, op0=AluOpType.is_ge)
            p.inc(inst, "gpS")
    gp_c = p.cnt["gpS"]

    @block.vector
    def _(v):
        v.memset(c["ones_f"][:], 1.0)
        v.memset(c["ones_bf"][:], 1.0)
        v.memset(c["negones_bf"][:], -1.0)
        inst = v.memset(c["eps_col"][:], LN_EPS)
        p.inc(inst, "dvS")

    @block.sync
    def _(sy):
        if "l1f" in slots:
            inst = sy.dma_start(c["b1_sb"][:].rearrange("p (e c) -> p e c", c=32),
                                p.d["l1_b1"].rearrange("e p c -> p e c"))
            p.inc(inst, "ioS", 16)
            inst = sy.dma_start(c["rtmp"][0:1, :], p.d["l1_bgb"][0:1, :])
            p.inc(inst, "ioS", 16)
            inst = sy.dma_start(c["rtm2"][0:1, :], p.d["l1_bgb"][1:2, :])
            p.inc(inst, "ioS", 16)
        if "l0a" in slots:
            inst = sy.dma_start(c["bq_fm"][:, 0:4], p.d["l0_bq"])
            p.inc(inst, "ioS", 16)
            inst = sy.dma_start(c["bk_fm"][:, 0:4], p.d["l0_bk"])
            p.inc(inst, "ioS", 16)
    io0 = p.cnt["ioS"]

    @block.vector
    def _(v):
        p.wait(v, "ioS", io0)
        p.wait(v, "gpS", gp_c)
        if "l0a" in slots:
            inst = v.tensor_scalar(c["bq_fm"][:, 0:4], c["bq_fm"][:, 0:4], QK_SCALE, None,
                                   op0=AluOpType.mult)
            p.inc(inst, "dvS")
        if "l1f" in slots:
            inst = v.tensor_tensor(c["bgb"][:], c["rtmp"][0:1, :], c["rtm2"][0:1, :],
                                   op=AluOpType.add)
            p.inc(inst, "dvS")
            p.wait(v, "dvS")
            inst = v.tensor_copy(c["bgbb"][:], c["bgb"][:])
            p.inc(inst, "dvS")


def emit_xload(p, b):
    @p.block.sync
    def _(sy):
        p.wait(sy, "peS"); p.wait(sy, "dvS"); p.wait(sy, "acS"); p.wait(sy, "ioS")
        src = p.d["x"][b * S:(b + 1) * S, :].rearrange("(t p) c -> p t c", p=128)
        dst = p.c["xb"][:].rearrange("p (t c) -> p t c", c=1024)
        inst = sy.dma_start(dst, src)
        p.inc(inst, "ioS", 16)


def emit_xstore(p, b):
    @p.block.sync
    def _(sy):
        p.wait(sy, "dvS"); p.wait(sy, "acS"); p.wait(sy, "peS")
        dst = p.d["out"][b * S:(b + 1) * S, :].rearrange("(t p) c -> p t c", p=128)
        srcb = p.c["xb"][:].rearrange("p (t c) -> p t c", c=1024)
        inst = sy.dma_start(dst, srcb)
        p.inc(inst, "ioS", 16)


def emit_ln(p, save_ab=False):
    block, c = p.block, p.c
    io_x = p.cnt["ioS"]

    @block.scalar
    def _(s):
        p.wait(s, "ioS", io_x)
        p.wait(s, "dvS"); p.wait(s, "peS")
        for tt in range(NT):
            inst = s.activation(c["junk"][:], c["xb"][:, tt * 1024:(tt + 1) * 1024],
                                AF.Square, accum_out=c["sq_c"][:, tt:tt + 1])
            p.inc(inst, "acS")
    ac_sq = p.cnt["acS"]

    @block.vector
    def _(v):
        p.wait(v, "ioS", io_x)
        p.wait(v, "peS")
        for tt in range(NT):
            inst = v.tensor_scalar(c["junk"][:], c["xb"][:, tt * 1024:(tt + 1) * 1024],
                                   1.0, None, op0=AluOpType.mult, op1=AluOpType.add,
                                   accum_out=c["sum_c"][:, tt:tt + 1])
            p.inc(inst, "dvS")
        p.wait(v, "dvS")
        inst = v.tensor_scalar(c["mu"][:], c["sum_c"][:], 1.0 / D, None, op0=AluOpType.mult)
        p.inc(inst, "dvS")
        p.wait(v, "acS", ac_sq)
        inst = v.tensor_scalar(c["msq"][:], c["sq_c"][:], 1.0 / D, None, op0=AluOpType.mult)
        p.inc(inst, "dvS")
        p.wait(v, "dvS")
        inst = v.tensor_tensor(c["mu2"][:], c["mu"][:], c["mu"][:], op=AluOpType.mult)
        p.inc(inst, "dvS")
        p.wait(v, "dvS")
        inst = v.tensor_tensor(c["var"][:], c["msq"][:], c["mu2"][:], op=AluOpType.subtract)
        p.inc(inst, "dvS")
    dv_var = p.cnt["dvS"]

    @block.scalar
    def _(s):
        p.wait(s, "dvS", dv_var)
        inst = s.activation(c["lnv"][:], c["var"][:], AF.Ln, bias=c["eps_col"][:, 0:1])
        p.inc(inst, "acS")
        p.wait(s, "acS")
        inst = s.activation(c["rs"][:], c["lnv"][:], AF.Exp, scale=-0.5)
        p.inc(inst, "acS")
    ac_rs = p.cnt["acS"]

    @block.vector
    def _(v):
        p.wait(v, "acS", ac_rs)
        inst = v.scalar_tensor_tensor(c["nmr"][:], c["mu"][:], -1.0, c["rs"][:],
                                      op0=AluOpType.mult, op1=AluOpType.mult)
        p.inc(inst, "dvS")
        if save_ab:
            inst = v.tensor_scalar(c["A_c"][:], c["rs"][:], RSQRT2, 1.0,
                                   op0=AluOpType.mult, op1=AluOpType.add)
            p.inc(inst, "dvS")
            inst = v.scalar_tensor_tensor(c["B_c"][:], c["mu"][:], -RSQRT2, c["rs"][:],
                                          op0=AluOpType.mult, op1=AluOpType.mult)
            p.inc(inst, "dvS")
    dv_nmr = p.cnt["dvS"]

    ac_apply, pe_tr, ac_ev = {}, {}, {}
    for tt in range(NT):
        @block.scalar
        def _(s, tt=tt):
            if tt == 0:
                p.wait(s, "dvS", dv_nmr)
            if tt >= 1:
                p.wait(s, "peS", pe_tr[tt - 1])
            inst = s.activation(c["lntm"][:], c["xb"][:, tt * 1024:(tt + 1) * 1024],
                                AF.Identity, scale=c["rs"][:, tt:tt + 1],
                                bias=c["nmr"][:, tt:tt + 1])
            ac_apply[tt] = p.inc(inst, "acS")

        @block.tensor
        def _(t, tt=tt):
            p.wait(t, "acS", ac_apply[tt])
            if tt >= 2:
                p.wait(t, "acS", ac_ev[tt - 2])
            ps = c["pC"] if tt % 2 == 0 else c["pD"]
            for dk in range(DK):
                inst = t.transpose(ps[:, dk * 128:(dk + 1) * 128],
                                   c["lntm"][:, dk * 128:(dk + 1) * 128], c["ident"][:])
            pe_tr[tt] = p.inc(inst, "peS")

        @block.scalar
        def _(s, tt=tt):
            p.wait(s, "peS", pe_tr[tt])
            ps = c["pC"] if tt % 2 == 0 else c["pD"]
            out = c["lnf"][:].rearrange("p (dk q) -> p dk q", dk=DK)[:, :, tt * 128:(tt + 1) * 128]
            inst = s.activation(out, ps[:], AF.Copy)
            ac_ev[tt] = p.inc(inst, "acS")


def emit_proj_fm(p, dram_name, ot_list, out_sb, bias_fm=None, bias_cols=None, scale=1.0,
                 interleave=None):
    block, c = p.block, p.c
    wmark, pe_done, ac_done = {}, {}, {}
    for i, ot in enumerate(ot_list):
        @block.gpsimd
        def _(g, i=i, ot=ot):
            if i < 2:
                p.wait(g, "peS")
            else:
                p.wait(g, "peS", pe_done[i - 2])
            src = p.d[dram_name][ot].rearrange("p dk c -> p (dk c)")
            inst = g.dma_start(c["wbB0" if i % 2 == 0 else "wbB1"][:, 0:DK * 128], src)
            wmark[i] = p.inc(inst, "wS", 16)

        @block.tensor
        def _(t, i=i):
            p.wait(t, "wS", wmark[i])
            if i <= 1:
                p.wait(t, "acS"); p.wait(t, "dvS")
            if i >= 2:
                p.wait(t, "acS", ac_done[i - 2])
            wb = c["wbB0" if i % 2 == 0 else "wbB1"]
            ps = c["pAB"] if i % 2 == 0 else c["pCD"]
            inst = None
            for dk in range(DK):
                for tc in range(4):
                    inst = t.matmul(ps[:][:, tc * 512:(tc + 1) * 512],
                                    wb[:, dk * 128:(dk + 1) * 128],
                                    lnf_strip(p, dk, tc * 512, (tc + 1) * 512),
                                    start=(dk == 0), stop=(dk == DK - 1))
            pe_done[i] = p.inc(inst, "peS")

        @block.scalar
        def _(s, i=i):
            p.wait(s, "peS", pe_done[i])
            ps = c["pAB"] if i % 2 == 0 else c["pCD"]
            kw = {}
            fn = AF.Copy
            if bias_fm is not None:
                kw["bias"] = bias_fm[:, bias_cols[i]:bias_cols[i] + 1]
                fn = AF.Identity
            inst = s.activation(out_sb[:, i * 2048:(i + 1) * 2048], ps[:], fn,
                                scale=scale, **kw)
            ac_done[i] = p.inc(inst, "acS")
        if interleave is not None:
            interleave()
    return ac_done


def emit_vtm(p, wv_name, bv_name, voff, interleave=None):
    block, c = p.block, p.c
    vc = 520

    @block.gpsimd
    def _(g):
        p.wait(g, "peS")
        src = p.d[wv_name].rearrange("(dk p) c -> p dk c", p=128)[:, :, voff:voff + vc]
        dst = c["wbA0"][:, 0:DK * vc].rearrange("p (dk c) -> p dk c", dk=DK)
        inst = g.dma_start(dst, src)
        p.inc(inst, "wS", 16)
    wm = p.cnt["wS"]

    @block.gpsimd
    def _(g):
        inst = g.dma_start(c["rowb"][:, 0:vc], p.d[bv_name][:, voff:voff + vc])
        p.inc(inst, "wS", 16)
    iom = p.cnt["wS"]

    pe_done, ac_done = {}, {}
    for tt in range(NT):
        @block.tensor
        def _(t, tt=tt):
            if tt == 0:
                p.wait(t, "wS", max(wm, iom))
                p.wait(t, "acS")
            if tt >= 2:
                p.wait(t, "acS", ac_done[tt - 2])
            ps = c["pA"] if tt % 2 == 0 else c["pB"]
            t.matmul(ps[:, 0:512], c["ones_bf"][:, 0:128], c["rowb"][0:1, 0:512],
                     start=True, stop=True)
            t.matmul(ps[:, 512:vc], c["ones_bf"][:, 0:128], c["rowb"][0:1, 512:vc],
                     start=True, stop=True)
            inst = None
            for dk in range(DK):
                t.matmul(ps[:, 0:512], lnf_strip(p, dk, tt * 128, (tt + 1) * 128),
                         c["wbA0"][:, dk * vc:dk * vc + 512],
                         start=False, stop=(dk == DK - 1))
                inst = t.matmul(ps[:, 512:vc], lnf_strip(p, dk, tt * 128, (tt + 1) * 128),
                                c["wbA0"][:, dk * vc + 512:(dk + 1) * vc],
                                start=False, stop=(dk == DK - 1))
            pe_done[tt] = p.inc(inst, "peS")

        @block.scalar
        def _(s, tt=tt):
            p.wait(s, "peS", pe_done[tt])
            ps = c["pA"] if tt % 2 == 0 else c["pB"]
            inst = s.activation(c["vtm"][:, tt * 520:tt * 520 + vc], ps[:, 0:vc], AF.Copy)
            ac_done[tt] = p.inc(inst, "acS")
        if interleave is not None:
            interleave()


class PassA:

    def __init__(self, p, causal):
        self.p = p
        self.causal = causal
        self.units = [(h, qt) for h in range(8) for qt in range(NT)]
        self.idx = 0
        self.kmark = {}
        self.dv_done = {}
        self.pe_done = {}

    def set_ready(self, hp, ac_mark):
        self.kmark[hp] = ac_mark

    def avail(self):
        n = 0
        for (h, qt) in self.units[self.idx:]:
            if h // 2 in self.kmark:
                n += 1
            else:
                break
        return n

    def pump(self, n):
        p, block, c = self.p, self.p.block, self.p.c
        n = min(n, self.avail())
        for _ in range(n):
            h, qt = self.units[self.idx]
            k = self.idx
            self.idx += 1
            hp, hb = h // 2, (h % 2) * 64
            lim = 2048 if not self.causal else (qt + 1) * 128
            causal = self.causal

            @block.tensor
            def _(t, h=h, qt=qt, hp=hp, hb=hb, lim=lim, k=k):
                if k == 0:
                    p.wait(t, "acS")
                if qt == 0:
                    p.wait(t, "acS", self.kmark[hp])
                if k >= 1:
                    p.wait(t, "dvS", self.dv_done[k - 1])
                lhs = c["qf"][hb:hb + 64, hp * 2048 + qt * 128: hp * 2048 + (qt + 1) * 128]
                inst = None
                for sc in range(0, lim, 512):
                    w = min(512, lim - sc)
                    inst = t.matmul(c["pCD"][:][:, sc:sc + w], lhs,
                                    c["kf"][hb:hb + 64, hp * 2048 + sc: hp * 2048 + sc + w],
                                    start=True, stop=True)
                self.pe_done[k] = p.inc(inst, "peS")

            @block.vector
            def _(v, lim=lim, k=k, h=h, qt=qt):
                p.wait(v, "peS", self.pe_done[k])
                inst = v.tensor_reduce(c["M_all"][:, h * NT + qt:h * NT + qt + 1],
                                       c["pCD"][:][:, 0:lim], axis=AX.X, op=AluOpType.max)
                self.dv_done[k] = p.inc(inst, "dvS")

    def finish(self):
        p, block, c = self.p, self.p.block, self.p.c
        self.pump(len(self.units))
        assert self.idx == len(self.units)
        dvm = p.cnt["dvS"]

        @block.tensor
        def _(t):
            p.wait(t, "dvS", dvm)
            inst = t.transpose(c["pC"][:, 0:128], c["M_all"][:], c["ident"][:])
            p.inc(inst, "peS")
        pet = p.cnt["peS"]

        @block.scalar
        def _(s):
            p.wait(s, "peS", pet)
            inst = s.activation(c["mts"][:], c["pC"][:, 0:128], AF.Copy)
            p.inc(inst, "acS")
        act = p.cnt["acS"]

        @block.sync
        def _(sy):
            p.wait(sy, "acS", act)
            inst = sy.dma_start(p.d["mbounce"][:], c["mts"][:])
            p.inc(inst, "ioS", 16)


def emit_attn_phaseC(p, qh, causal):
    block, c = p.block, p.c
    iom = p.cnt["ioS"]
    last_s2 = [0]
    for h in range(8):
        hp, hb = h // 2, (h % 2) * 64
        pt_hi = NT if not causal else (qh + 1) * 8
        dv_prev_h = p.cnt["dvS"]
        ac_head0 = p.cnt["acS"]
        ac_exp = {}
        pe_pv = {}
        pe_s2 = {}

        @block.sync
        def _(sy, h=h):
            p.wait(sy, "ioS", iom)
            if h == 0 and qh > 0:
                p.wait(sy, "peS")
            elif h > 0:
                p.wait(sy, "peS", last_s2[0])
            src = p.d["mbounce"][h * 16:(h + 1) * 16, :].rearrange("a b -> (a b)").unsqueeze(0)
            inst = sy.dma_start(c["negmax"][:], src)
            p.inc(inst, "ioS", 16)
        io_h = p.cnt["ioS"]

        for ptp in range(0, pt_hi, 3):
            pts = [q for q in (ptp, ptp + 1, ptp + 2) if q < pt_hi]
            for pt in pts:
                sps = (c["pC"], c["pA"], c["pB"])[pt % 3]
                qc_min = 0 if not causal else max(0, (pt * 128 - qh * 1024 - 512) // 512 + 1)

                @block.tensor
                def _(t, h=h, pt=pt, hp=hp, hb=hb, qc_min=qc_min, sps=sps):
                    if pt == 0:
                        p.wait(t, "ioS", io_h)
                        p.wait(t, "acS", ac_head0)
                        p.wait(t, "dvS", dv_prev_h)
                    if pt >= 3:
                        p.wait(t, "acS", ac_exp[pt - 3])
                    inst = None
                    for qc in range(qc_min, 2):
                        ps = sps[:, qc * 512:(qc + 1) * 512]
                        t.matmul(ps, c["negones_bf"][:, 0:128],
                                 c["negmax"][0:1, qh * 1024 + qc * 512: qh * 1024 + (qc + 1) * 512],
                                 start=True, stop=True)
                        inst = t.matmul(ps,
                                        c["kf"][hb:hb + 64,
                                                hp * 2048 + pt * 128:hp * 2048 + (pt + 1) * 128],
                                        c["qf"][hb:hb + 64,
                                                hp * 2048 + qh * 1024 + qc * 512:
                                                hp * 2048 + qh * 1024 + (qc + 1) * 512],
                                        start=False, stop=True)
                    pe_s2[pt] = p.inc(inst, "peS")

            @block.scalar
            def _(s, pts=tuple(pts), qh=qh, causal=causal):
                for pt in pts:
                    ptile = (c["pt0"], c["pt1"], c["pt2"])[pt % 3]
                    sps = (c["pC"], c["pA"], c["pB"])[pt % 3]
                    qc_min = 0 if not causal else max(0, (pt * 128 - qh * 1024 - 512) // 512 + 1)
                    p.wait(s, "peS", pe_s2[pt])
                    if pt >= 3:
                        p.wait(s, "peS", pe_pv[pt - 3])
                    inst = s.activation(ptile[:, qc_min * 512:1024],
                                        sps[:, qc_min * 512:1024], AF.Exp)
                    ac_exp[pt] = p.inc(inst, "acS")

            dv_mask = {}
            for pt in pts:
                qd = pt * 128 - qh * 1024 if causal else -1000000
                if causal and 0 <= qd < 1024:
                    ptile = (c["pt0"], c["pt1"], c["pt2"])[pt % 3]

                    @block.vector
                    def _(v, pt=pt, qd=qd, ptile=ptile):
                        p.wait(v, "acS", ac_exp[pt])
                        lo = (qd // 512) * 512
                        if qd > lo:
                            inst = v.memset(ptile[:, lo:qd], 0.0)
                            p.inc(inst, "dvS")
                            p.wait(v, "dvS")
                        inst = v.tensor_tensor(ptile[:, qd:qd + 128], ptile[:, qd:qd + 128],
                                               c["maskT"][:], op=AluOpType.mult)
                        p.inc(inst, "dvS")
                    dv_mask[pt] = p.cnt["dvS"]

            for pt in pts:
                ptile = (c["pt0"], c["pt1"], c["pt2"])[pt % 3]
                qc_min = 0 if not causal else max(0, (pt * 128 - qh * 1024 - 512) // 512 + 1)

                @block.tensor
                def _(t, h=h, pt=pt, qc_min=qc_min, ptile=ptile, pt_hi=pt_hi):
                    p.wait(t, "acS", ac_exp[pt])
                    if pt in dv_mask:
                        p.wait(t, "dvS", dv_mask[pt])
                    inst = None
                    for qc in range(qc_min, 2):
                        last_pt = pt_hi - 1 if not causal else min(pt_hi - 1, qh * 8 + qc * 4 + 3)
                        inst = t.matmul(c["pD"][0:65, qc * 512:(qc + 1) * 512],
                                        c["vtm"][:, pt * 520 + h * 65: pt * 520 + (h + 1) * 65],
                                        ptile[:, qc * 512:(qc + 1) * 512],
                                        start=(pt == 0), stop=(pt == last_pt))
                    pe_pv[pt] = p.inc(inst, "peS")

        pe_last = p.cnt["peS"]
        last_s2[0] = max(pe_s2.values())

        for half in range(2):
            @block.vector
            def _(v, half=half, pe_last=pe_last):
                if half == 0:
                    p.wait(v, "peS", pe_last)
                p.wait(v, "acS")
                p.wait(v, "dvS")
                with p.nc.allow_low_precision(reason="bf16 rinv row for rank-1 bcast"):
                    inst = v.reciprocal(c["rrow"][:],
                                        c["pD"][64:65, half * 512:(half + 1) * 512])
                p.inc(inst, "dvS")
            dv_r = p.cnt["dvS"]

            @block.tensor
            def _(t, dv_r=dv_r, half=half):
                p.wait(t, "dvS", dv_r)
                p.wait(t, "acS")
                inst = t.matmul(c["pC"][0:64, half * 512:(half + 1) * 512],
                                c["ones_bf"][:, 0:64], c["rrow"][:],
                                start=True, stop=True)
                p.inc(inst, "peS")
            pe_rb = p.cnt["peS"]

            @block.scalar
            def _(s, pe_rb=pe_rb, half=half):
                p.wait(s, "peS", pe_rb)
                inst = s.activation(c["rb"][:], c["pC"][0:64, half * 512:(half + 1) * 512],
                                    AF.Copy)
                p.inc(inst, "acS")
            ac_rb = p.cnt["acS"]

            @block.vector
            def _(v, ac_rb=ac_rb, hp=hp, hb=hb, half=half):
                p.wait(v, "acS", ac_rb)
                inst = v.tensor_tensor(
                    c["of"][hb:hb + 64, hp * 1024 + half * 512: hp * 1024 + (half + 1) * 512],
                    c["pD"][0:64, half * 512:(half + 1) * 512], c["rb"][:],
                    op=AluOpType.mult)
                p.inc(inst, "dvS")


def emit_attn_proj(p, qh, slot):
    block, c = p.block, p.c
    dv_of = p.cnt["dvS"]
    pe_done = {}
    ac_free = {}
    dv_free = {}
    if slot == "l0a":
        @block.gpsimd
        def _(g):
            p.wait(g, "peS")
            inst = g.dma_start(c["rowb"][:], p.d["l0_bout"])
            p.inc(inst, "wS", 16)
        io_bout = p.cnt["wS"]
    for ti in range(8):
        tt = qh * 8 + ti
        ps = c["pA"] if ti % 2 == 0 else c["pB"]

        @block.tensor
        def _(t, ti=ti, ps=ps):
            if ti == 0:
                p.wait(t, "dvS", dv_of)
                p.wait(t, "wS")
                p.wait(t, "acS")
                if slot == "l0a":
                    p.wait(t, "wS", io_bout)
            if ti >= 2:
                if slot == "l0a":
                    p.wait(t, "dvS", dv_free[ti - 2])
                    p.wait(t, "acS", ac_free[ti - 2])
                else:
                    p.wait(t, "dvS")
            inst = None
            if slot == "l0a":
                t.matmul(ps[:, 0:512], c["ones_bf"][:, 0:128], c["rowb"][0:1, 0:512],
                         start=True, stop=True)
                t.matmul(ps[:, 512:1024], c["ones_bf"][:, 0:128],
                         c["rowb"][0:1, 512:1024], start=True, stop=True)
            for vk in range(4):
                for dc in range(2):
                    inst = t.matmul(ps[:, dc * 512:(dc + 1) * 512],
                                    c["of"][:, vk * 1024 + ti * 128: vk * 1024 + (ti + 1) * 128],
                                    c["wbA1"][:, vk * 1024 + dc * 512: vk * 1024 + (dc + 1) * 512],
                                    start=(False if slot == "l0a" else vk == 0),
                                    stop=(vk == 3))
            pe_done[ti] = p.inc(inst, "peS")

        if slot == "l1a":
            @block.vector
            def _(v, ti=ti, tt=tt, ps=ps):
                p.wait(v, "peS", pe_done[ti])
                inst = v.tensor_tensor(c["xb"][:, tt * 1024:(tt + 1) * 1024],
                                       c["xb"][:, tt * 1024:(tt + 1) * 1024],
                                       ps[:], op=AluOpType.add)
                p.inc(inst, "dvS")
        else:
            @block.scalar
            def _(s, ti=ti, ps=ps):
                p.wait(s, "peS", pe_done[ti])
                p.wait(s, "dvS")
                inst = s.activation(c["junk"][:], ps[:], AF.Square,
                                    accum_out=c["ycol"][:, 1:2])
                p.inc(inst, "acS")
            ac_sq = p.cnt["acS"]

            @block.vector
            def _(v, ti=ti, ps=ps, ac_sq=ac_sq):
                p.wait(v, "peS", pe_done[ti])
                p.wait(v, "dvS")
                p.wait(v, "acS", ac_sq - 1)
                inst = v.tensor_scalar(c["junk"][:], ps[:], 1.0, None,
                                       op0=AluOpType.mult, op1=AluOpType.add,
                                       accum_out=c["ycol"][:, 0:1])
                dv_free[ti] = p.inc(inst, "dvS")
                p.wait(v, "dvS")
                inst = v.tensor_scalar(c["ycol"][:, 2:3], c["ycol"][:, 0:1], 1.0 / D, None,
                                       op0=AluOpType.mult)
                p.inc(inst, "dvS")
                p.wait(v, "acS", ac_sq)
                p.wait(v, "dvS")
                inst = v.tensor_scalar(c["ycol"][:, 3:4], c["ycol"][:, 1:2], 1.0 / D, None,
                                       op0=AluOpType.mult)
                p.inc(inst, "dvS")
                p.wait(v, "dvS")
                inst = v.tensor_tensor(c["ycol"][:, 4:5], c["ycol"][:, 2:3], c["ycol"][:, 2:3],
                                       op=AluOpType.mult)
                p.inc(inst, "dvS")
                p.wait(v, "dvS")
                inst = v.tensor_tensor(c["ycol"][:, 5:6], c["ycol"][:, 3:4], c["ycol"][:, 4:5],
                                       op=AluOpType.subtract)
                p.inc(inst, "dvS")
            dv_vy = p.cnt["dvS"]

            @block.scalar
            def _(s, dv_vy=dv_vy):
                p.wait(s, "dvS", dv_vy)
                inst = s.activation(c["ycol"][:, 6:7], c["ycol"][:, 5:6], AF.Ln,
                                    bias=c["eps_col"][:, 0:1])
                p.inc(inst, "acS")
                p.wait(s, "acS")
                inst = s.activation(c["ycol"][:, 7:8], c["ycol"][:, 6:7], AF.Exp, scale=-0.5)
                p.inc(inst, "acS")
            ac_ry = p.cnt["acS"]

            @block.vector
            def _(v, ac_ry=ac_ry):
                p.wait(v, "acS", ac_ry)
                inst = v.scalar_tensor_tensor(c["ycol"][:, 4:5], c["ycol"][:, 2:3], -1.0,
                                              c["ycol"][:, 7:8], op0=AluOpType.mult,
                                              op1=AluOpType.mult)
                p.inc(inst, "dvS")
            dv_nmy = p.cnt["dvS"]

            @block.scalar
            def _(s, ps=ps, dv_nmy=dv_nmy, tt=tt, ti=ti):
                p.wait(s, "dvS", dv_nmy)
                inst = s.activation(c["lntm"][:], ps[:], AF.Identity,
                                    scale=c["ycol"][:, 7:8], bias=c["ycol"][:, 4:5])
                ac_free[ti] = p.inc(inst, "acS")
                p.wait(s, "acS")
                inst = s.activation(c["xb"][:, tt * 1024:(tt + 1) * 1024],
                                    c["xb"][:, tt * 1024:(tt + 1) * 1024], AF.Identity,
                                    scale=c["A_c"][:, tt:tt + 1], bias=c["B_c"][:, tt:tt + 1])
                p.inc(inst, "acS")
            ac_t = p.cnt["acS"]

            @block.vector
            def _(v, ac_t=ac_t, tt=tt):
                p.wait(v, "acS", ac_t)
                inst = v.tensor_tensor(c["lntm"][:], c["lntm"][:], c["gamma_bc"],
                                       op=AluOpType.mult)
                p.inc(inst, "dvS")
                p.wait(v, "dvS")
                inst = v.tensor_tensor(c["lntm"][:], c["lntm"][:], c["beta_bc"],
                                       op=AluOpType.add)
                p.inc(inst, "dvS")
                p.wait(v, "dvS")
                inst = v.tensor_tensor(c["xb"][:, tt * 1024:(tt + 1) * 1024],
                                       c["xb"][:, tt * 1024:(tt + 1) * 1024],
                                       c["lntm"][:], op=AluOpType.add)
                p.inc(inst, "dvS")


def emit_gb_build(p):
    block, c = p.block, p.c
    for which, dname, ps in (("g", "l0_gamma", "pA"), ("b", "l0_beta", "pB")):
        @block.gpsimd
        def _(g, dname=dname):
            p.wait(g, "peS")
            inst = g.dma_start(c["rowb"][:], p.d[dname])
            p.inc(inst, "wS", 16)
        iom = p.cnt["wS"]

        @block.tensor
        def _(t, ps=ps, iom=iom):
            p.wait(t, "wS", iom)
            p.wait(t, "dvS"); p.wait(t, "acS")
            inst = None
            for half in range(2):
                inst = t.matmul(c[ps][:, half * 512:(half + 1) * 512], c["ones_bf"][:, 0:128],
                                c["rowb"][0:1, half * 512:(half + 1) * 512],
                                start=True, stop=True)
            p.inc(inst, "peS")
    pe1 = p.cnt["peS"]

    @block.scalar
    def _(s):
        p.wait(s, "peS", pe1)
        s.activation(c["gamma_bc"], c["pA"][:], AF.Copy, scale=RSQRT2)
        inst = s.activation(c["beta_bc"], c["pB"][:], AF.Copy, scale=RSQRT2)
        p.inc(inst, "acS")


def emit_l0a(p):
    c = p.c
    emit_ln(p, save_ab=True)
    pa = PassA(p, causal=False)
    emit_proj_fm(p, "l0_wq", [0, 1, 2, 3], c["qf"], bias_fm=c["bq_fm"],
                 bias_cols=[0, 1, 2, 3], scale=QK_SCALE)

    def hook():
        pa.pump(6)
    km = emit_proj_fm(p, "l0_wk", [0, 1, 2, 3], c["kf"], bias_fm=c["bk_fm"],
                      bias_cols=[0, 1, 2, 3], scale=1.0,
                      interleave=None)
    for hp in range(4):
        pa.set_ready(hp, km[hp])
    emit_vtm(p, "l0_wv", "l0_bv", 0, interleave=lambda: pa.pump(6))

    @p.block.gpsimd
    def _(g):
        p.wait(g, "peS")
        src = p.d["l0_wout"].rearrange("(vk p) c -> p vk c", p=128)
        dst = c["wbA1"][:, 0:4096].rearrange("p (vk c) -> p vk c", c=1024)
        inst = g.dma_start(dst, src)
        p.inc(inst, "wS", 16)
    pa.finish()
    emit_gb_build(p)
    for qh in range(2):
        emit_attn_phaseC(p, qh, causal=False)
        emit_attn_proj(p, qh, "l0a")


def emit_l1a(p):
    c = p.c
    emit_ln(p, save_ab=False)
    for grp in range(2):
        pa = PassA(p, causal=True)
        emit_proj_fm(p, "l1_wq", [4 * grp + i for i in range(4)], c["qf"], scale=QK_SCALE)
        km = emit_proj_fm(p, "l1_wk", [4 * grp + i for i in range(4)], c["kf"], scale=1.0)
        for hp in range(4):
            pa.set_ready(hp, km[hp])
        emit_vtm(p, "l1_wv", "l1_bv", 520 * grp, interleave=lambda: pa.pump(6))

        @p.block.gpsimd
        def _(g, grp=grp):
            p.wait(g, "peS")
            src = p.d["l1_wo"][512 * grp:512 * (grp + 1), :].rearrange(
                "(vk p) c -> p vk c", p=128)
            dst = c["wbA1"][:, 0:4096].rearrange("p (vk c) -> p vk c", c=1024)
            inst = g.dma_start(dst, src)
            p.inc(inst, "wS", 16)
        pa.finish()
        for qh in range(2):
            emit_attn_phaseC(p, qh, causal=True)
            emit_attn_proj(p, qh, "l1a")


def emit_l0f(p):
    c = p.c
    block = p.block
    emit_ln(p, save_ab=False)

    for th in range(2):
        wm = {}
        pe_kt = {}
        dv_stt = {}
        ac_sil = {}
        for hc in range(6):
            @block.gpsimd
            def _(g, hc=hc, th=th):
                if hc >= 1:
                    p.wait(g, "peS", pe_kt[hc * 4 - 1])
                else:
                    p.wait(g, "peS")
                inst = g.dma_start(c["wbA0"][:, 0:4096],
                                   p.d["l0_wg"][hc].rearrange("p dk c -> p (dk c)"))
                p.inc(inst, "wS", 16)
                inst = g.dma_start(c["wbA1"][:, 0:4096],
                                   p.d["l0_wu"][hc].rearrange("p dk c -> p (dk c)"))
                wm[hc] = p.inc(inst, "wS", 16)

            for pair in range(2):
                kts = [hc * 4 + pair * 2, hc * 4 + pair * 2 + 1]
                for kt in kts:
                    ht = kt - hc * 4

                    @block.tensor
                    def _(t, hc=hc, ht=ht, th=th, kt=kt):
                        if ht == 0:
                            p.wait(t, "wS", wm[hc])
                        if kt >= 2:
                            p.wait(t, "dvS", dv_stt.get(kt - 2, 0))
                            p.wait(t, "acS", ac_sil.get(kt - 2, 0))
                        elif kt < 2 and th == 0 and hc == 0:
                            p.wait(t, "dvS"); p.wait(t, "acS")
                        elif kt < 2:
                            p.wait(t, "dvS"); p.wait(t, "acS")
                        pg = c["pA"] if kt % 2 == 0 else c["pC"]
                        pu = c["pB"] if kt % 2 == 0 else c["pD"]
                        inst = None
                        for dk in range(DK):
                            for tc in range(2):
                                t.matmul(pg[:, tc * 512:(tc + 1) * 512],
                                         c["wbA0"][:, dk * 512 + ht * 128: dk * 512 + (ht + 1) * 128],
                                         lnf_strip(p, dk, th * 1024 + tc * 512,
                                                   th * 1024 + (tc + 1) * 512),
                                         start=(dk == 0), stop=(dk == DK - 1))
                                inst = t.matmul(pu[:, tc * 512:(tc + 1) * 512],
                                                c["wbA1"][:, dk * 512 + ht * 128: dk * 512 + (ht + 1) * 128],
                                                lnf_strip(p, dk, th * 1024 + tc * 512,
                                                          th * 1024 + (tc + 1) * 512),
                                                start=(dk == 0), stop=(dk == DK - 1))
                        pe_kt[kt] = p.inc(inst, "peS")

                @block.scalar
                def _(s, kts=tuple(kts)):
                    for kt in kts:
                        p.wait(s, "peS", pe_kt[kt])
                        if kt >= 2:
                            p.wait(s, "dvS", dv_stt.get(kt - 2, 0))
                        pg = c["pA"] if kt % 2 == 0 else c["pC"]
                        sb = c["pt0"] if kt % 2 == 0 else c["pt1"]
                        with p.nc.allow_low_precision(reason="bf16 silu factor (bf16 matmul path)"):
                            inst = s.activation(sb[:], pg[:], AF.Silu, scale=DEQ)
                        ac_sil[kt] = p.inc(inst, "acS")

                for kt in kts:
                    @block.vector
                    def _(v, kt=kt):
                        p.wait(v, "acS", ac_sil[kt])
                        pu = c["pB"] if kt % 2 == 0 else c["pD"]
                        sb = c["pt0"] if kt % 2 == 0 else c["pt1"]
                        inst = v.scalar_tensor_tensor(c["hfm"][:, kt * 1024:(kt + 1) * 1024],
                                                      pu[:], DEQ, sb[:],
                                                      op0=AluOpType.mult, op1=AluOpType.mult)
                        dv_stt[kt] = p.inc(inst, "dvS")
        dv_up = p.cnt["dvS"]

        pe_dkt = {}
        for dc in range(2):
            pe_dn = None
            for ktg in range(6):
                @block.gpsimd
                def _(g, dc=dc, ktg=ktg):
                    k = dc * 6 + ktg
                    if k >= 2:
                        p.wait(g, "peS", pe_dkt[k - 2])
                    elif k == 0:
                        p.wait(g, "peS")
                    inst = g.dma_start(c["wbB0" if ktg % 2 == 0 else "wbB1"][:],
                                       p.d["l0_wd"][dc, ktg].rearrange("p j c -> p (j c)"))
                    wm[(dc, ktg)] = p.inc(inst, "wS", 16)

                @block.tensor
                def _(t, dc=dc, ktg=ktg, dv_up=dv_up):
                    p.wait(t, "wS", wm[(dc, ktg)])
                    if ktg == 0:
                        p.wait(t, "dvS")
                        p.wait(t, "acS")
                    wb = c["wbB0" if ktg % 2 == 0 else "wbB1"]
                    inst = None
                    for j in range(4):
                        kt = ktg * 4 + j
                        for tt in range(8):
                            pst = c[PSUMS[tt // 2]]
                            inst = t.matmul(pst[:, (tt % 2) * 512:(tt % 2 + 1) * 512],
                                            c["hfm"][:, kt * 1024 + tt * 128: kt * 1024 + (tt + 1) * 128],
                                            wb[:, j * 512:(j + 1) * 512],
                                            start=(kt == 0), stop=(kt == 23))
                    pe_dkt[dc * 6 + ktg] = p.inc(inst, "peS")
            pe_dn = p.cnt["peS"]

            for tt in range(8):
                @block.vector
                def _(v, tt=tt, dc=dc, th=th, pe_dn=pe_dn):
                    p.wait(v, "peS", pe_dn)
                    pst = c[PSUMS[tt // 2]]
                    col = (th * 8 + tt) * 1024 + dc * 512
                    inst = v.scalar_tensor_tensor(c["xb"][:, col:col + 512],
                                                  pst[:, (tt % 2) * 512:(tt % 2 + 1) * 512],
                                                  DEQ, c["xb"][:, col:col + 512],
                                                  op0=AluOpType.mult, op1=AluOpType.add)
                    p.inc(inst, "dvS")


def emit_l1f(p):
    c = p.c
    block = p.block
    emit_ln(p, save_ab=False)

    @block.gpsimd
    def _(g):
        p.wait(g, "acS")
        src = p.d["l1_wgr"].rearrange("(dk p) c -> p dk c", p=128)
        dst = c["wgr_sb"][:].rearrange("p (dk c) -> p dk c", c=8)
        inst = g.dma_start(dst, src)
        p.inc(inst, "wS", 16)
    wgr_m = p.cnt["wS"]
    for tt in range(NT):
        @block.tensor
        def _(t, tt=tt):
            if tt == 0:
                p.wait(t, "wS", wgr_m)
                p.wait(t, "acS")
            else:
                p.wait(t, "acS")
            t.matmul(c["pC"][:, 0:8], c["ones_bf"][:, 0:128], c["bgbb"][:],
                     start=True, stop=True)
            inst = None
            for dk in range(DK):
                inst = t.matmul(c["pC"][:, 0:8], lnf_strip(p, dk, tt * 128, (tt + 1) * 128),
                                c["wgr_sb"][:, dk * 8:(dk + 1) * 8],
                                start=False, stop=(dk == DK - 1))
            p.inc(inst, "peS")
        pe_r = p.cnt["peS"]

        @block.scalar
        def _(s, pe_r=pe_r):
            p.wait(s, "peS", pe_r)
            p.wait(s, "dvS")
            inst = s.activation(c["rtmp"][:], c["pC"][:, 0:8], AF.Sigmoid)
            p.inc(inst, "acS")
        ac_r = p.cnt["acS"]

        @block.vector
        def _(v, tt=tt, ac_r=ac_r):
            p.wait(v, "acS", ac_r)
            inst = v.tensor_reduce(c["rcol"][:, 0:1], c["rtmp"][:], axis=AX.X,
                                   op=AluOpType.max)
            p.inc(inst, "dvS")
            p.wait(v, "dvS")
            inst = v.tensor_scalar(c["rtm2"][:], c["rtmp"][:], c["rcol"][:, 0:1], None,
                                   op0=AluOpType.is_ge, op1=AluOpType.add,
                                   accum_out=c["rcol"][:, 2:3])
            p.inc(inst, "dvS")
            p.wait(v, "dvS")
            inst = v.scalar_tensor_tensor(c["rtm3"][:], c["rtm2"][:], -1e30, c["rtmp"][:],
                                          op0=AluOpType.mult, op1=AluOpType.add)
            p.inc(inst, "dvS")
            p.wait(v, "dvS")
            inst = v.tensor_reduce(c["rcol"][:, 1:2], c["rtm3"][:], axis=AX.X,
                                   op=AluOpType.max)
            p.inc(inst, "dvS")
            p.wait(v, "dvS")
            inst = v.tensor_scalar(c["rcol"][:, 3:4], c["rcol"][:, 2:3], 2.0, None,
                                   op0=AluOpType.is_ge)
            p.inc(inst, "dvS")
            p.wait(v, "dvS")
            inst = v.tensor_tensor(c["rcol"][:, 2:3], c["rcol"][:, 0:1], c["rcol"][:, 1:2],
                                   op=AluOpType.subtract)
            p.inc(inst, "dvS")
            p.wait(v, "dvS")
            inst = v.scalar_tensor_tensor(c["rcol"][:, 0:1], c["rcol"][:, 2:3],
                                          c["rcol"][:, 3:4], c["rcol"][:, 1:2],
                                          op0=AluOpType.mult, op1=AluOpType.add)
            p.inc(inst, "dvS")
            p.wait(v, "dvS")
            inst = v.tensor_scalar(c["rtm3"][:], c["rtmp"][:], c["rcol"][:, 0:1], None,
                                   op0=AluOpType.is_ge)
            p.inc(inst, "dvS")
            p.wait(v, "dvS")
            inst = v.tensor_tensor(c["w_sb"][:, tt * 8:(tt + 1) * 8], c["rtm3"][:],
                                   c["rtmp"][:], op=AluOpType.mult)
            p.inc(inst, "dvS")

    for th in range(2):
        for e in range(9):
            wm1, ac_g, pe_u = {}, {}, {}

            @block.gpsimd
            def _(g, e=e):
                p.wait(g, "peS")
                inst = g.dma_start(c["rowb"][:], p.d["l1_b2"][:, e * 1024:(e + 1) * 1024])
                p.inc(inst, "wS", 16)
            io_b2 = p.cnt["wS"]
            for hc in range(8):
                @block.gpsimd
                def _(g, e=e, hc=hc):
                    if hc >= 2:
                        p.wait(g, "peS", pe_u[(hc - 2, 3)])
                    elif hc == 0:
                        p.wait(g, "peS")
                    inst = g.dma_start(c["wbA0" if hc % 2 == 0 else "wbA1"][:, 0:4096],
                                       p.d["l1_w1"][e, hc].rearrange("p dk c -> p (dk c)"))
                    wm1[hc] = p.inc(inst, "wS", 16)

                for ht in range(4):
                    kt = hc * 4 + ht

                    @block.tensor
                    def _(t, hc=hc, ht=ht, th=th, kt=kt):
                        if ht == 0:
                            p.wait(t, "wS", wm1[hc])
                        if kt >= 4:
                            p.wait(t, "acS", ac_g[kt - 4])
                        elif kt == 0:
                            p.wait(t, "dvS"); p.wait(t, "acS")
                        wb = c["wbA0" if hc % 2 == 0 else "wbA1"]
                        ps = c[PSUMS[kt % 4]]
                        inst = None
                        for dk in range(DK):
                            for tc in range(2):
                                inst = t.matmul(ps[:, tc * 512:(tc + 1) * 512],
                                                wb[:, dk * 512 + ht * 128: dk * 512 + (ht + 1) * 128],
                                                lnf_strip(p, dk, th * 1024 + tc * 512,
                                                          th * 1024 + (tc + 1) * 512),
                                                start=(dk == 0), stop=(dk == DK - 1))
                        pe_u[(hc, ht)] = p.inc(inst, "peS")

                @block.scalar
                def _(s, e=e, hc=hc):
                    for ht in range(4):
                        kt = hc * 4 + ht
                        p.wait(s, "peS", pe_u[(hc, ht)])
                        ps = c[PSUMS[kt % 4]]
                        inst = s.activation(c["hfm"][:, kt * 1024:(kt + 1) * 1024], ps[:],
                                            AF.Gelu,
                                            bias=c["b1_sb"][:, e * 32 + kt:e * 32 + kt + 1])
                        ac_g[kt] = p.inc(inst, "acS")
            ac_up = p.cnt["acS"]

            wm2, pe_d = {}, {}

            @block.gpsimd
            def _(g, e=e):
                p.wait(g, "peS")
                inst = g.dma_start(c["rowb"][:], p.d["l1_b2"][:, e * 1024:(e + 1) * 1024])
                p.inc(inst, "wS", 16)
            io_b2 = p.cnt["wS"]
            for dc in range(2):
                @block.tensor
                def _(t, e=e, dc=dc, ac_up=ac_up, io_b2=io_b2):
                    p.wait(t, "acS", ac_up)
                    p.wait(t, "dvS")
                    p.wait(t, "wS", io_b2)
                    inst = None
                    for tt in range(8):
                        pst = c[PSUMS[tt // 2]]
                        inst = t.matmul(pst[:, (tt % 2) * 512:(tt % 2 + 1) * 512],
                                        c["ones_bf"][:, 0:128],
                                        c["rowb"][0:1, dc * 512:(dc + 1) * 512],
                                        start=True, stop=True)
                    p.inc(inst, "peS")

                for ktg in range(8):
                    @block.gpsimd
                    def _(g, e=e, dc=dc, ktg=ktg):
                        k = dc * 8 + ktg
                        if k >= 2:
                            p.wait(g, "peS", pe_d[k - 2])
                        inst = g.dma_start(c["wbB0" if ktg % 2 == 0 else "wbB1"][:],
                                           p.d["l1_w2"][e, dc, ktg].rearrange("p j c -> p (j c)"))
                        wm2[(dc, ktg)] = p.inc(inst, "wS", 16)

                    @block.tensor
                    def _(t, dc=dc, ktg=ktg):
                        p.wait(t, "wS", wm2[(dc, ktg)])
                        p.wait(t, "acS", ac_g[min(ktg * 4 + 3, 31)])
                        wb = c["wbB0" if ktg % 2 == 0 else "wbB1"]
                        inst = None
                        for j in range(4):
                            kt = ktg * 4 + j
                            for tt in range(8):
                                pst = c[PSUMS[tt // 2]]
                                inst = t.matmul(pst[:, (tt % 2) * 512:(tt % 2 + 1) * 512],
                                                c["hfm"][:, kt * 1024 + tt * 128:
                                                          kt * 1024 + (tt + 1) * 128],
                                                wb[:, j * 512:(j + 1) * 512],
                                                start=False, stop=(kt == 31))
                        pe_d[dc * 8 + ktg] = p.inc(inst, "peS")
                pe_dn = p.cnt["peS"]

                for tt in range(8):
                    @block.vector
                    def _(v, e=e, tt=tt, dc=dc, th=th, pe_dn=pe_dn):
                        p.wait(v, "peS", pe_dn)
                        pst = c[PSUMS[tt // 2]]
                        tg = th * 8 + tt
                        col = tg * 1024 + dc * 512
                        if e < 8:
                            inst = v.scalar_tensor_tensor(
                                c["xb"][:, col:col + 512],
                                pst[:, (tt % 2) * 512:(tt % 2 + 1) * 512],
                                c["w_sb"][:, tg * 8 + e:tg * 8 + e + 1],
                                c["xb"][:, col:col + 512],
                                op0=AluOpType.mult, op1=AluOpType.add)
                        else:
                            inst = v.tensor_tensor(
                                c["xb"][:, col:col + 512],
                                c["xb"][:, col:col + 512],
                                pst[:, (tt % 2) * 512:(tt % 2 + 1) * 512],
                                op=AluOpType.add)
                        p.inc(inst, "dvS")


SLOT_EMITTERS = {"l0a": emit_l0a, "l0f": emit_l0f, "l1a": emit_l1a, "l1f": emit_l1f}


def build_program(slots, n_batches):
    nc = bass.Bass("TRN2", target_bir_lowering=False)
    dram = {}
    dram["x"] = nc.dram_tensor("x", [n_batches * S, D], F32, kind="ExternalInput").ap()
    dram["out"] = nc.dram_tensor("out", [n_batches * S, D], F32, kind="ExternalOutput").ap()
    dram["mbounce"] = nc.dram_tensor("mbounce", [128, 128], BF16, kind="Internal").ap()
    wnames = sorted({w for sl in slots for w in SLOT_WEIGHTS[sl]})
    for w in wnames:
        sh, dt = PREP_SPECS[w]
        dram[w] = nc.dram_tensor(w, list(sh), dt, kind="ExternalInput").ap()
    with ExitStack() as st:
        block = st.enter_context(nc.Block())
        p = P(nc, block, st)
        p.d = dram
        alloc_buffers(p, st)
        emit_consts(p, slots)
        for b in range(n_batches):
            emit_xload(p, b)
            for sl in slots:
                SLOT_EMITTERS[sl](p)
            emit_xstore(p, b)

        @block.sync
        def _(sy):
            p.wait(sy, "ioS")
    return nc, wnames


def kernel(**inputs):
    prep = host_prep(inputs)
    x = np.ascontiguousarray(inputs["L0_attn_x"].astype(np.float32, copy=False)
                             ).reshape(B, S, D)
    groups = [ALL_SLOTS] if MODE == "fused" else [(s,) for s in ALL_SLOTS]
    for slots in groups:
        nc, wnames = build_program(slots, BPC)
        in_maps = []
        for core in range(N_CORES):
            m = {w: prep[w] for w in wnames}
            m["x"] = np.ascontiguousarray(
                x[core * BPC:(core + 1) * BPC].reshape(BPC * S, D))
            in_maps.append(m)
        res = run_bass_kernel_spmd(nc, in_maps, core_ids=list(range(N_CORES)))
        x = np.concatenate(
            [r["out"].reshape(BPC, S, D) for r in res.results], axis=0)
    return np.ascontiguousarray(x.reshape(B, S, D)).astype(np.float32, copy=False)
